# version 02
# baseline (speedup 1.0000x reference)
import numpy as np
import concourse.bass as bass
import concourse.mybir as mybir
from concourse.bass_utils import run_bass_kernel_spmd
from contextlib import ExitStack

F32 = mybir.dt.float32
BF16 = mybir.dt.bfloat16
AOP = mybir.AluOpType
AF = mybir.ActivationFunctionType
AX = mybir.AxisListType

B, S, D = 4, 2048, 2048
H, DK = 16, 128
INTER = 8192
NEXP = 8
THETA = 10000.0
EPS = 1e-5
NCORE = 8
NSLOT = 8
TOK = NSLOT * 128
NDC = D // 128
NIC = INTER // 128
NSUP = 8
ICPS = NIC // NSUP
NEG = -1.0e30
NPASS = 4
HPP = H // NPASS


class Sem:
    def __init__(self, nc, name):
        self._cm = nc.semaphore(name)
        self.h = self._cm.__enter__()
        self.val = 0

    def inc(self, instr, n=1):
        instr.then_inc(self.h, n)
        self.val += n
        return self.val

    def wait(self, eng, val=None):
        eng.wait_ge(self.h, self.val if val is None else val)


def _mm_chunks(lo, hi):
    out = []
    c = lo
    while c < hi:
        n = min(hi, (c // 512 + 1) * 512) - c
        out.append((c, n))
        c += n
    return out


def build_program(debug=False, stop_after=None):
    nc = bass.Bass("TRN2", target_bir_lowering=False)
    top = ExitStack()

    dt = lambda n, shp, dty=F32, kind="ExternalInput": nc.dram_tensor(n, shp, dty, kind=kind).ap()
    x_in = dt("x", [S, D])
    wq_d = dt("wq", [H, NDC, 128, 128])
    wk_d = dt("wk", [H, NDC, 128, 128])
    wv_d = dt("wv", [NPASS, NDC, 128, HPP * 128])
    wo_d = dt("wo", [NDC, NDC, 128, 128])
    up_d = dt("up", [NIC, NDC, 128, 128])
    dn_d = dt("dn", [NSUP, NDC, ICPS, 128, 128])
    e_d = dt("ew", [NDC, NDC, 128, 128])
    gate_d = dt("gate", [128, 128])
    gb_d = dt("gb", [128, 32])
    ckt_d = dt("ckt", [128, S])
    skt_d = dt("skt", [128, S])
    cqt_d = dt("cqt", [128, TOK])
    sqt_d = dt("sqt", [128, TOK])
    mask_d = dt("maskp", [128, 256])
    eye_d = dt("eye", [128, 128])
    ones1_d = dt("ones1", [1, 128])
    sel_d = dt("sel", [8, 1024])
    out_d = dt("out", [TOK, D], F32, kind="ExternalOutput")
    dbg = {}
    if debug:
        dbg["xn_own"] = dt("dbg_xn_own", [128, NDC * TOK], F32, kind="ExternalOutput")
        dbg["kfm"] = dt("dbg_kfm", [128, S], F32, kind="ExternalOutput")
        dbg["vtm"] = dt("dbg_vtm", [128, 16 * HPP * 128], F32, kind="ExternalOutput")
        dbg["ctx"] = dt("dbg_ctx", [128, H * TOK], F32, kind="ExternalOutput")
        dbg["ffn"] = dt("dbg_ffn", [128, NDC * TOK], F32, kind="ExternalOutput")
        dbg["sbc"] = dt("dbg_sbc", [128, TOK], F32, kind="ExternalOutput")
        dbg["qfm"] = dt("dbg_qfm", [128, TOK], F32, kind="ExternalOutput")
        dbg["psb"] = dt("dbg_psb", [128, 9216], F32, kind="ExternalOutput")

    sb = lambda st_, n, shp, dty: st_.enter_context(nc.sbuf_tensor(n, shp, dty))

    def sbN(st_, name, n, w, dty):
        t = st_.enter_context(nc.sbuf_tensor(name, [128, n * w], dty))
        return [t[:, i * w:(i + 1) * w] for i in range(n)]
    xn_own = sb(top, "xn_own", [128, NDC * TOK], BF16)
    ctx = sb(top, "ctx", [128, H * TOK], BF16)
    eye32 = sb(top, "eye32", [128, 128], F32)
    eyebf = sb(top, "eyebf", [128, 128], BF16)
    onesbf = sb(top, "onesbf", [128, 128], BF16)
    ones1 = sb(top, "ones1_s", [1, 128], F32)
    gate_sb = sb(top, "gate_sb", [128, 128], BF16)
    gb_sb = sb(top, "gb_sb", [128, 32], F32)
    small = sb(top, "small", [128, 64], F32)
    rowb = sb(top, "rowb", [1, TOK], F32)

    psA = top.enter_context(nc.psum_tensor("psA", [128, 2048], F32))
    psB = top.enter_context(nc.psum_tensor("psB", [128, 512], F32))
    psC = top.enter_context(nc.psum_tensor("psC", [128, 1024], F32))
    psD = top.enter_context(nc.psum_tensor("psD", [128, 512], F32))
    psBb = psB.ap().bitcast(BF16)
    psAb = psA.ap().bitcast(BF16)

    T, V, SC, G, SY = nc.tensor, nc.vector, nc.scalar, nc.gpsimd, nc.sync
    ALL_ENG = (T, V, SC, G, SY)

    sems = {}

    def sem(name):
        if name not in sems:
            sems[name] = Sem(nc, name)
        return sems[name]

    def barrier(name):
        s = sem(name)
        base = s.val
        for eng in ALL_ENG:
            eng.sem_inc(s.h, 1)
        s.val = base + 5
        for eng in ALL_ENG:
            s.wait(eng)

    s_const = sem("const")
    for dst, src in ((eye32, eye_d), (gb_sb, gb_d)):
        s_const.inc(SY.dma_start(dst[:, :], src[:, :]), 16)
    s_const.inc(SY.dma_start(ones1[:, :], ones1_d[:, :]), 16)
    s_cbf = sem("cbf")
    s_cbf.inc(G.dma_start(eyebf[:, :], eye_d[:, :]), 16)
    s_cbf.inc(G.dma_start(gate_sb[:, :], gate_d[:, :]), 16)
    s_ones = sem("onesw")
    s_ones.inc(V.memset(onesbf[:, :], 1.0))

    p01 = ExitStack()
    xn_oth = sb(p01, "xn_oth", [128, NDC * TOK], BF16)
    ckt = sb(p01, "ckt_s", [128, S], F32)
    skt = sb(p01, "skt_s", [128, S], F32)
    cqt = sb(p01, "cqt_s", [128, TOK], F32)
    sqt = sb(p01, "sqt_s", [128, TOK], F32)
    maskp = sb(p01, "maskp_s", [128, 256], F32)
    for dst, src in ((ckt, ckt_d), (skt, skt_d), (cqt, cqt_d), (sqt, sqt_d), (maskp, mask_d)):
        s_const.inc(SY.dma_start(dst[:, :], src[:, :]), 16)
    s_const.wait(V)
    s_const.wait(T)
    s_const.wait(SC)

    p0s = ExitStack()
    x_tm = sbN(p0s, "x_tm", 2, D, F32)
    xn_tm = sbN(p0s, "xn_tm", 2, D, BF16)
    if True:
        s_x = sem("p0_x")
        s_ap = sem("p0_ap")
        s_iv = sem("p0_iv")
        s_nm = sem("p0_nm")
        s_tp = sem("p0_tp")
        s_ev = sem("p0_ev")
        v_ap = {}
        v_tp = {}
        for gc in range(NDC):
            bi = gc % 2
            if gc >= 2:
                s_ap.wait(SY, v_ap[gc - 2])
            s_x.inc(SY.dma_start(x_tm[bi][:, :], x_in[gc * 128:(gc + 1) * 128, :]), 16)
            s_x.wait(V)
            for q in range(4):
                st4 = V.bn_stats(small[:, 6 * q:6 * q + 6], x_tm[bi][:, 512 * q:512 * (q + 1)])
            s_nm.inc(st4)
            s_nm.wait(V)
            ag4 = V.bn_aggr(small[:, 24:26], small[:, 0:24])
            s_nm.inc(ag4)
            s_nm.wait(V)
            V.tensor_scalar(small[:, 26:27], small[:, 24:25], -1.0, None, op0=AOP.mult)
            nm = V.tensor_scalar(small[:, 27:28], small[:, 25:26], EPS, None, op0=AOP.add)
            s_nm.inc(nm)
            s_nm.wait(SC)
            SC.activation(small[:, 28:29], small[:, 27:28], AF.Ln)
            iv = SC.activation(small[:, 29:30], small[:, 28:29], AF.Exp, scale=-0.5)
            s_iv.inc(iv)
            s_iv.wait(V)
            s_nm.wait(V)
            if gc >= 2:
                s_tp.wait(V, v_tp[gc - 2])
            ap = V.tensor_scalar(xn_tm[bi][:, :], x_tm[bi][:, :], small[:, 26:27],
                                 small[:, 29:30], op0=AOP.add, op1=AOP.mult)
            v_ap[gc] = s_ap.inc(ap)
            s_ap.wait(T)
            if gc >= 1:
                s_ev.wait(T, 2 * gc)
            tgt = xn_own if bi == 0 else xn_oth
            j = gc // 2
            for half in range(2):
                pp = psBb if half == 0 else psAb[:, 0:1024]
                for k in range(8):
                    dc = half * 8 + k
                    tp = T.transpose(pp[:, k * 128:(k + 1) * 128],
                                     xn_tm[bi][:, dc * 128:(dc + 1) * 128], eyebf[:, :])
                s_tp.inc(tp)
                v_tp[gc] = s_tp.val
                s_tp.wait(V)
                dview = tgt.ap().rearrange("p (c n) -> p c n", n=TOK)[:, half * 8:half * 8 + 8,
                                                                     j * 128:(j + 1) * 128]
                sview = pp.rearrange("p (c n) -> p c n", n=128)
                ev = V.tensor_copy(dview, sview)
                s_ev.inc(ev)
        s_ev.wait(V)
    p0s.close()
    barrier("b_p0")
    if debug:
        sd = sem("dbg0")
        sd.inc(G.dma_start(dbg["xn_own"][:, :], xn_own[:, :]), 16)
        sd.wait(G)
        sd.wait(SY)

    if stop_after == "p0":
        p01.close()
        top.close()
        return nc
    poff = [0]
    for j in range(NSLOT):
        poff.append(poff[-1] + (2 * j + 2) * 128)
    PSB = poff[-1]

    p1s = ExitStack()
    kfm = sb(p1s, "kfm", [128, HPP * S], BF16)
    vtm = sb(p1s, "vtm", [128, 16 * HPP * 128], BF16)
    ovl = sb(p1s, "ovl", [128, PSB], BF16)
    wkb = sbN(p1s, "wkb", 2, NDC * 128, BF16)
    wqb = sbN(p1s, "wqb", 2, NDC * 128, BF16)
    qfm = sb(p1s, "qfm", [128, TOK], BF16)
    ptsb = sbN(p1s, "ptsb", 2, 2048, BF16)
    rsw = sb(p1s, "rsw", [128, 1024], F32)
    if True:
        psb_sb = ovl
        wvb = ovl.ap()[:, 0:NDC * HPP * 128]

        s_wv = sem("p1_wv")
        s_wk = sem("p1_wk")
        s_wq = sem("p1_wq")
        s_vp = sem("p1_vp")
        s_vev = sem("p1_vev")
        s_kp = sem("p1_kp")
        s_kro = sem("p1_kro")
        s_kswc = sem("p1_kswc")
        s_qp = sem("p1_qp")
        s_qro = sem("p1_qro")
        s_sm = sem("p1_sm")
        s_mx = sem("p1_mx")
        s_ex = sem("p1_ex")
        s_msk = sem("p1_msk")
        v_msk_head = {}
        s_pt = sem("p1_pt")
        s_ptev = sem("p1_ptev")
        s_av = sem("p1_av")
        s_rb = sem("p1_rb")
        s_cx = sem("p1_cx")

        v_vev = {}
        v_vp = {}
        v_kp = {}
        v_kro = {}
        v_pt_head = {}
        v_cx_head = {}
        v_av_head = {}

        for p in range(NPASS):
            if p > 0:
                s_pt.wait(G, v_pt_head[p * HPP - 1])
            s_wv.inc(G.dma_start(wvb.rearrange("p (c v) -> p c v", c=NDC),
                                 wv_d[p, :, :, :].rearrange("c p v -> p c v")), 16)
            s_wv.wait(T)
            if p > 0:
                s_cx.wait(T, v_cx_head[p * HPP - 1])
            for gb in range(16):
                src = xn_own if gb < 8 else xn_oth
                j = gb % 8
                pv = psB if gb % 2 == 0 else psD
                if gb >= 2:
                    s_vev.wait(T, v_vev[(p, gb - 2)])
                for dc in range(NDC):
                    mm = T.matmul(pv[:, 0:HPP * 128],
                                  src[:, dc * TOK + j * 128: dc * TOK + (j + 1) * 128],
                                  wvb[:, dc * HPP * 128:(dc + 1) * HPP * 128],
                                  start=(dc == 0), stop=(dc == NDC - 1))
                v_vp[(p, gb)] = s_vp.inc(mm)
                s_vp.wait(SC)
                if p > 0 and gb == 0:
                    s_av.wait(SC, v_av_head[p * HPP - 1])
                ev = SC.copy(vtm[:, gb * HPP * 128:(gb + 1) * HPP * 128], pv[:, 0:HPP * 128])
                v_vev[(p, gb)] = s_vev.inc(ev)
            for ih in range(HPP):
                h = p * HPP + ih
                bi = ih % 2
                if ih >= 2:
                    s_kp.wait(G, v_kp[h - 2])
                s_wk.inc(G.dma_start(wkb[bi].rearrange("p (c k) -> p c k", c=NDC),
                                     wk_d[h, :, :, :].rearrange("c p k -> p c k")), 16)
                s_wk.wait(T)
                if ih == 0:
                    s_ex.wait(T)
                else:
                    s_kro.wait(T, v_kro[h - 1])
                for dc in range(NDC):
                    for si in range(2):
                        src = xn_own if si == 0 else xn_oth
                        for grp in range(2):
                            mm = T.matmul(psA[:, si * 1024 + grp * 512: si * 1024 + (grp + 1) * 512],
                                          wkb[bi][:, dc * 128:(dc + 1) * 128],
                                          src[:, dc * TOK + grp * 512: dc * TOK + (grp + 1) * 512],
                                          start=(dc == 0), stop=(dc == NDC - 1))
                v_kp[h] = s_kp.inc(mm)
                if p > 0 and ih == 0:
                    s_cx.wait(SC, v_cx_head[p * HPP - 1])
                for hf in range(2):
                    s_kp.wait(SC)
                    if hf == 1:
                        s_kro.wait(SC)
                    SC.copy(rsw[0:64, :], psA[64:128, hf * 1024:(hf + 1) * 1024])
                    hc = SC.copy(rsw[64:128, :], psA[0:64, hf * 1024:(hf + 1) * 1024])
                    s_kswc.inc(hc)
                    s_kp.wait(V)
                    s_kswc.wait(V)
                    V.tensor_tensor(psC[:, :], psA[:, hf * 1024:(hf + 1) * 1024],
                                    ckt[:, hf * 1024:(hf + 1) * 1024], op=AOP.mult)
                    V.tensor_tensor(rsw[:, :], rsw[:, :],
                                    skt[:, hf * 1024:(hf + 1) * 1024], op=AOP.mult)
                    rp = V.tensor_tensor(kfm[:, ih * S + hf * 1024: ih * S + (hf + 1) * 1024],
                                         psC[:, :], rsw[:, :], op=AOP.add)
                    s_kro.inc(rp)
                v_kro[h] = s_kro.val
            if debug and p == 0:
                sdk = sem("dbgk")
                s_kro.wait(G, v_kro[HPP - 1])
                s_vev.wait(G)
                sdk.inc(G.dma_start(dbg["kfm"][:, :], kfm[:, 0:S]), 16)
                sdk.inc(G.dma_start(dbg["vtm"][:, 0:2048], vtm[:, 0:2048]), 16)
                sdk.wait(G)

            bst = {}

            def beta_begin(hb):
                bst[hb] = dict(av_base=s_av.val, prev_ptev=s_ptev.val,
                               evac={}, pt={}, avp={})
                if hb > 0 and (hb - 1) in v_cx_head:
                    s_cx.wait(T, v_cx_head[hb - 1])
                s_msk.wait(T, v_msk_head[hb])

            def _av_pair(hb, ihb, jp, is_first, is_last):
                st = bst[hb]
                q0 = jp * 128
                for ki, k in enumerate((jp, 8 + jp)):
                    pt_src = ptsb[jp % 2]
                    off = ki * 1024
                    for (c0, n) in _mm_chunks(q0, 1024):
                        T.matmul(psC[:, c0:c0 + n],
                                 vtm[:, k * HPP * 128 + ihb * 128: k * HPP * 128 + (ihb + 1) * 128],
                                 pt_src[:, off + c0 - q0: off + c0 - q0 + n],
                                 start=(is_first and ki == 0), stop=(is_last and ki == 1))
                        rd = psD[0:1, c0:c0 + n] if c0 < 512 else psD[32:33, c0 - 512: c0 - 512 + n]
                        mm = T.matmul(rd, onesbf[:, 0:1],
                                      pt_src[:, off + c0 - q0: off + c0 - q0 + n],
                                      start=(is_first and ki == 0), stop=(is_last and ki == 1))
                st["avp"][jp] = s_av.inc(mm)

            def _tp_block(hb, jp, ki):
                st = bst[hb]
                k = jp if ki == 0 else 8 + jp
                for sj in range(jp, NSLOT):
                    own_w = (sj + 1) * 128
                    col = (k * 128) if k < 8 else (own_w + (k - 8) * 128)
                    tp = T.transpose(psBb[:, (sj - jp) * 128:(sj - jp + 1) * 128],
                                     psb_sb[:, poff[sj] + col: poff[sj] + col + 128],
                                     eyebf[:, :])
                st["pt"][(jp, ki)] = s_pt.inc(tp)

            def _evac_block(hb, jp, ki):
                st = bst[hb]
                nsl = (NSLOT - jp) * 128
                s_pt.wait(V, st["pt"][(jp, ki)])
                if ki == 0:
                    s_av.wait(V, st["avp"][jp - 2] if jp >= 2 else st["av_base"])
                pe = V.tensor_copy(ptsb[jp % 2][:, ki * 1024: ki * 1024 + nsl],
                                   psBb[:, 0:nsl])
                st["evac"][(jp, ki)] = s_ptev.inc(pe)

            def beta_pair_transposes(hb, ihb, jp):
                st = bst[hb]
                if jp > 0:
                    s_ptev.wait(T, st["evac"][jp - 1])
                    _av_pair(hb, ihb, jp - 1, jp - 1 == 0, False)
                else:
                    s_ptev.wait(T, st["prev_ptev"])
                _tp_block(hb, jp, 0)

            def beta_pair_evacs(hb, jp):
                st = bst[hb]
                _evac_block(hb, jp, 0)
                s_ptev.wait(T, st["evac"][(jp, 0)])
                _tp_block(hb, jp, 1)
                st["pt"][jp] = st["pt"][(jp, 1)]
                _evac_block(hb, jp, 1)
                st["evac"][jp] = st["evac"][(jp, 1)]

            def beta_tail(hb, ihb):
                st = bst[hb]
                s_ptev.wait(T, st["evac"][7])
                _av_pair(hb, ihb, 7, False, True)
                s_av.wait(V)
                V.reciprocal(rowb[0:1, 0:512], psD[0:1, 0:512])
                rc = V.reciprocal(rowb[0:1, 512:1024], psD[32:33, 0:512])
                s_rb.inc(rc)
                s_rb.wait(T)
                s_ptev.wait(T, st["evac"][7])
                T.matmul(psB[:, :], ones1[:, :], rowb[0:1, 0:512], start=True, stop=True)
                bc = T.matmul(psD[:, :], ones1[:, :], rowb[0:1, 512:1024], start=True, stop=True)
                s_rb.inc(bc)
                s_rb.wait(SC)
                SC.copy(rsw[:, 0:512], psB[:, :])
                rc2 = SC.copy(rsw[:, 512:1024], psD[:, :])
                s_rb.inc(rc2)
                s_rb.wait(V)
                c1 = V.tensor_tensor(ctx[:, hb * TOK: hb * TOK + 512], psC[:, 0:512],
                                     rsw[:, 0:512], op=AOP.mult)
                c2 = V.tensor_tensor(ctx[:, hb * TOK + 512: hb * TOK + 1024], psC[:, 512:1024],
                                     rsw[:, 512:1024], op=AOP.mult)
                s_cx.inc(c1)
                v_cx_head[hb] = s_cx.inc(c2)

            pending = None
            for ih in range(HPP):
                h = p * HPP + ih
                if h > 1:
                    s_qp.wait(G, h - 1)
                s_wq.inc(G.dma_start(wqb[h % 2].rearrange("p (c k) -> p c k", c=NDC),
                                     wq_d[h, :, :, :].rearrange("c p k -> p c k")), 16)
                s_wq.wait(T)
                if ih == 0:
                    s_kro.wait(T, v_kro[p * HPP + HPP - 1])
                else:
                    s_ex.wait(T)
                for dc in range(NDC):
                    for grp in range(2):
                        mm = T.matmul(psA[:, grp * 512:(grp + 1) * 512],
                                      wqb[h % 2][:, dc * 128:(dc + 1) * 128],
                                      xn_own[:, dc * TOK + grp * 512: dc * TOK + (grp + 1) * 512],
                                      start=(dc == 0), stop=(dc == NDC - 1))
                s_qp.inc(mm)
                s_qp.wait(SC)
                gk = h - 1 if ih == 0 else h - 2
                if gk >= 0 and gk in v_cx_head:
                    s_cx.wait(SC, v_cx_head[gk])
                SC.copy(rsw[0:64, :], psA[64:128, 0:1024])
                hc = SC.copy(rsw[64:128, :], psA[0:64, 0:1024])
                s_kswc.inc(hc)
                s_qp.wait(V)
                s_kswc.wait(V)
                V.tensor_tensor(psA[:, 1024:2048], psA[:, 0:1024], cqt[:, :], op=AOP.mult)
                V.tensor_tensor(rsw[:, :], rsw[:, :], sqt[:, :], op=AOP.mult)
                rp = V.tensor_tensor(qfm[:, :], psA[:, 1024:2048], rsw[:, :], op=AOP.add)
                s_qro.inc(rp)

                if pending is not None:
                    beta_begin(pending)
                s_qro.wait(T)
                for j in range(NSLOT):
                    own_w = (j + 1) * 128
                    ext = 2 * own_w
                    if j > 0:
                        s_ex.wait(T)
                    for w in range(2):
                        base_kv = w * 1024
                        col0 = w * own_w
                        for (c0, n) in _mm_chunks(col0, col0 + own_w):
                            off = c0 - col0
                            mm = T.matmul(psA[:, c0:c0 + n],
                                          qfm[:, j * 128:(j + 1) * 128],
                                          kfm[:, ih * S + base_kv + off: ih * S + base_kv + off + n],
                                          start=True, stop=True)
                    s_sm.inc(mm)
                    if pending is not None:
                        beta_pair_transposes(pending, pending % HPP, j)
                    s_sm.wait(V)
                    mx = V.reduce_max(small[:, 32 + j:33 + j], psA[:, 0:ext], axis=AX.X, negate=True)
                    s_mx.inc(mx)
                    if pending is not None:
                        beta_pair_evacs(pending, j)
                    s_mx.wait(SC)
                    if pending is not None:
                        s_pt.wait(SC, bst[pending]["pt"][(j, 1)])
                    elif h > 0 and j == 0:
                        s_pt.wait(SC, v_pt_head[h - 1])
                    ex = SC.activation(psb_sb[:, poff[j]:poff[j] + ext], psA[:, 0:ext], AF.Exp,
                                       bias=small[:, 32 + j:33 + j])
                    s_ex.inc(ex)
                    s_ex.wait(V)
                    V.tensor_tensor(psb_sb[:, poff[j] + j * 128: poff[j] + (j + 1) * 128],
                                    psb_sb[:, poff[j] + j * 128: poff[j] + (j + 1) * 128],
                                    maskp[:, 0:128], op=AOP.mult)
                    mk = V.tensor_tensor(psb_sb[:, poff[j] + own_w + j * 128: poff[j] + own_w + (j + 1) * 128],
                                         psb_sb[:, poff[j] + own_w + j * 128: poff[j] + own_w + (j + 1) * 128],
                                         maskp[:, 128:256], op=AOP.mult)
                    s_msk.inc(mk)
                v_msk_head[h] = s_msk.val
                if pending is not None:
                    beta_tail(pending, pending % HPP)
                    v_pt_head[pending] = s_pt.val
                    v_av_head[pending] = s_av.val

                if debug and h == 0:
                    sdq = sem("dbgq")
                    s_qro.wait(G)
                    sdq.inc(G.dma_start(dbg["qfm"][:, :], qfm[:, :]), 16)
                    s_msk.wait(G, v_msk_head[h])
                    sdq.inc(G.dma_start(dbg["psb"][:, :], psb_sb[:, :]), 16)
                    sdq.wait(G)
                    sdq.wait(SC)
                    sdq.wait(T)
                pending = h
            beta_begin(pending)
            for j in range(NSLOT):
                beta_pair_transposes(pending, pending % HPP, j)
                beta_pair_evacs(pending, j)
            beta_tail(pending, pending % HPP)
            v_pt_head[pending] = s_pt.val
            v_av_head[pending] = s_av.val
            pending = None
        s_cx.wait(V)
        s_cx.wait(T)
        s_cx.wait(SC)
    p1s.close()
    p01.close()
    barrier("b_p1")
    if debug:
        sd = sem("dbgc")
        sd.inc(G.dma_start(dbg["ctx"][:, :], ctx[:, :]), 16)
        sd.wait(G)
        sd.wait(SY)

    if stop_after == "p1":
        top.close()
        return nc
    p23 = ExitStack()
    acc = sb(p23, "acc", [128, NDC * TOK], F32)

    p2s = ExitStack()
    wup = sbN(p2s, "wup", 2, NDC * 128, BF16)
    wdn = sbN(p2s, "wdn", 2, ICPS * 128, BF16)
    wee = sbN(p2s, "wee", 2, NDC * 128, BF16)
    hsup = sb(p2s, "hsup", [128, ICPS * TOK], BF16)
    ln2t = sbN(p2s, "ln2t", 4, TOK, F32)
    ybf = sbN(p2s, "ybf", 2, TOK, BF16)
    y2bf = sbN(p2s, "y2bf", 2, TOK, BF16)
    rtmp = sbN(p2s, "rtmp", 2, TOK, F32)
    sel_sb = sb(p2s, "sel_sb", [8, 1024], F32)
    rec8 = sb(p2s, "rec8", [8, 128], F32)
    s_sel = sem("p2_sel")
    s_sel.inc(SY.dma_start(sel_sb[:, :], sel_d[:, :]), 16)
    if True:
        s_wup = sem("p2_wup")
        s_wdn = sem("p2_wdn")
        s_we = sem("p2_we")
        s_up = sem("p2_up")
        s_h = sem("p2_h")
        s_dn = sem("p2_dn")
        s_acc = sem("p2_acc")
        v_up = {}
        v_h = {}
        v_dn = {}
        v_acc = {}
        for sup in range(NSUP):
            for icw in range(ICPS):
                ic = sup * ICPS + icw
                bi = ic % 2
                if ic >= 2:
                    s_up.wait(G, v_up[ic - 2])
                s_wup.inc(G.dma_start(wup[bi].rearrange("p (c k) -> p c k", c=NDC),
                                      up_d[ic, :, :, :].rearrange("c p k -> p c k")), 16)
                s_wup.wait(T)
                if ic >= 2:
                    s_h.wait(T, v_h[ic - 2])
                if sup > 0 and icw < 2:
                    s_acc.wait(T, v_acc[(sup - 1, NDC - 1)])
                ph = psA[:, (icw % 2) * 1024:(icw % 2) * 1024 + 1024]
                for dc in range(NDC):
                    for grp in range(2):
                        mm = T.matmul(ph[:, grp * 512:(grp + 1) * 512],
                                      wup[bi][:, dc * 128:(dc + 1) * 128],
                                      xn_own[:, dc * TOK + grp * 512: dc * TOK + (grp + 1) * 512],
                                      start=(dc == 0), stop=(dc == NDC - 1))
                v_up[ic] = s_up.inc(mm)
                s_up.wait(SC)
                rl = SC.activation(rtmp[ic % 2][:, :], ph[:, :], AF.Relu)
                s_up.inc(rl)
                s_up.wait(V)
                if sup > 0 and icw == 0:
                    s_dn.wait(V, v_dn[(sup - 1, NDC - 1)])
                hv = V.tensor_tensor(hsup[:, icw * TOK:(icw + 1) * TOK], rtmp[ic % 2][:, :],
                                     rtmp[ic % 2][:, :], op=AOP.mult)
                v_h[ic] = s_h.inc(hv)
            s_h.wait(T)
            for dq in range(NDC):
                bi = dq % 2
                gidx = sup * NDC + dq
                if gidx >= 2:
                    s_dn.wait(G, v_dn[(sup, dq - 2)] if dq >= 2 else v_dn[(sup - 1, NDC + dq - 2)])
                s_wdn.inc(G.dma_start(wdn[bi].rearrange("p (c k) -> p c k", c=ICPS),
                                      dn_d[sup, dq, :, :, :].rearrange("c p k -> p c k")), 16)
                s_wdn.wait(T)
                pd = psC if dq % 2 == 0 else psA[:, 0:1024]
                if gidx >= 2:
                    prev = (sup, dq - 2) if dq >= 2 else (sup - 1, NDC + dq - 2)
                    s_acc.wait(T, v_acc[prev])
                for icw in range(ICPS):
                    for grp in range(2):
                        mm = T.matmul(pd[:, grp * 512:(grp + 1) * 512],
                                      wdn[bi][:, icw * 128:(icw + 1) * 128],
                                      hsup[:, icw * TOK + grp * 512: icw * TOK + (grp + 1) * 512],
                                      start=(icw == 0), stop=(icw == ICPS - 1))
                v_dn[(sup, dq)] = s_dn.inc(mm)
                s_dn.wait(V)
                if sup == 0:
                    av = V.tensor_copy(acc[:, dq * TOK:(dq + 1) * TOK], pd[:, :])
                else:
                    av = V.tensor_tensor(acc[:, dq * TOK:(dq + 1) * TOK],
                                         acc[:, dq * TOK:(dq + 1) * TOK], pd[:, :], op=AOP.add)
                v_acc[(sup, dq)] = s_acc.inc(av)
        s_g = sem("p2_gate")
        s_cbf.wait(T)
        s_acc.wait(T)
        v_g_m8 = {}
        for tc in range(NSLOT):
            if tc > 0:
                s_g.wait(T, v_g_m8[tc - 1])
            for dc in range(NDC):
                mm = T.matmul(psD[:, 0:8], xn_own[:, dc * TOK + tc * 128: dc * TOK + (tc + 1) * 128],
                              gate_sb[:, dc * 8: dc * 8 + 8],
                              start=(dc == 0), stop=(dc == NDC - 1))
            s_g.inc(mm)
            s_g.wait(V)
            m8 = V.max(small[:, 48:56], psD[:, 0:8])
            v_g_m8[tc] = s_g.inc(m8)
            s_g.wait(V)
            tt = V.tensor_tensor(small[:, 56 + tc:57 + tc], small[:, 48:49], small[:, 49:50],
                                 op=AOP.add)
            s_g.inc(tt)
        s_g.wait(T)
        tpm = T.matmul(psD[0:8, 0:128], small[:, 56:64], eye32[:, :], start=True, stop=True)
        s_g.inc(tpm)
        s_g.wait(SC)
        cp = SC.copy(rec8[:, :], psD[0:8, 0:128])
        s_g.inc(cp)
        s_g.wait(T)
        s_sel.wait(T)
        for tc in range(NSLOT):
            mm = T.matmul(psC[:, tc * 128:(tc + 1) * 128],
                          sel_sb[:, tc * 128:(tc + 1) * 128], rec8[:, :],
                          start=True, stop=True)
        s_g.inc(mm)
        if debug:
            s_g.wait(V)
            dv = V.tensor_copy(ln2t[0][:, :], psC[:, :])
            sdv = sem("dbgs")
            sdv.inc(dv)
            sdv.wait(SY)
            sdv.inc(SY.dma_start(dbg["sbc"][:, :], ln2t[0][:, :]), 16)
            sdv.wait(SY)
        s_mo = sem("p2_moemm")
        s_ms = sem("p2_moes")
        v_mo = {}
        v_ms = {}
        s_g.wait(V)
        sbc_cp = V.tensor_copy(ln2t[2][:, :], psC[:, :])
        s_g.inc(sbc_cp)
        s_g.wait(V)
        for dq in range(NDC):
            bi = dq % 2
            if dq >= 2:
                s_mo.wait(G, v_mo[dq - 2])
            s_we.inc(G.dma_start(wee[bi].rearrange("p (c k) -> p c k", c=NDC),
                                 e_d[dq, :, :, :].rearrange("c p k -> p c k")), 16)
            s_we.wait(T)
            pd = psA[:, (dq % 2) * 1024:(dq % 2) * 1024 + 1024]
            if dq >= 2:
                s_ms.wait(T, v_ms[dq - 2][0])
            for dc in range(NDC):
                for grp in range(2):
                    mm = T.matmul(pd[:, grp * 512:(grp + 1) * 512],
                                  wee[bi][:, dc * 128:(dc + 1) * 128],
                                  xn_own[:, dc * TOK + grp * 512: dc * TOK + (grp + 1) * 512],
                                  start=(dc == 0), stop=(dc == NDC - 1))
            v_mo[dq] = s_mo.inc(mm)
            s_mo.wait(V)
            tt = V.tensor_tensor(ln2t[3][:, :], pd[:, :], ln2t[2][:, :], op=AOP.mult)
            a1 = s_ms.inc(tt)
            ad = V.tensor_tensor(acc[:, dq * TOK:(dq + 1) * TOK],
                                 acc[:, dq * TOK:(dq + 1) * TOK], ln2t[3][:, :], op=AOP.add)
            v_ms[dq] = (a1, s_ms.inc(ad))
        s_st = sem("p2_cast")
        s_smm = sem("p2_statmm")
        s_ones.wait(T)
        s_ms.wait(T)
        v_cast = {}
        v_smm = {}
        for dq in range(NDC):
            bi = dq % 2
            s_ms.wait(V, v_ms[dq][1])
            if dq >= 2:
                s_smm.wait(V, v_smm[dq - 2])
            cv = V.tensor_copy(ybf[bi][:, :], acc[:, dq * TOK:(dq + 1) * TOK])
            s_st.inc(cv)
            s_st.wait(SC)
            if dq >= 2:
                s_smm.wait(SC, v_smm[dq - 2])
            sq = SC.activation(y2bf[bi][:, :], ybf[bi][:, :], AF.Square)
            v_cast[dq] = s_st.inc(sq)
            s_st.wait(T)
            for grp in range(2):
                T.matmul(psA[:, grp * 512:512 + grp * 512], onesbf[:, :],
                         ybf[bi][:, grp * 512:(grp + 1) * 512],
                         start=(dq == 0), stop=(dq == NDC - 1))
                mm = T.matmul(psA[:, 1024 + grp * 512:1536 + grp * 512], onesbf[:, :],
                              y2bf[bi][:, grp * 512:(grp + 1) * 512],
                              start=(dq == 0), stop=(dq == NDC - 1))
            v_smm[dq] = s_smm.inc(mm)
        s_smm.wait(SC)
        SC.mul(ln2t[0][:, :], psA[:, 0:1024], 1.0 / D)
        m2 = SC.mul(ln2t[1][:, :], psA[:, 1024:2048], 1.0 / D)
        s_stm = sem("p2_stm")
        s_stm.inc(m2)
        s_stm.wait(V)
        mu2 = V.tensor_tensor(ln2t[2][:, :], ln2t[0][:, :], ln2t[0][:, :], op=AOP.mult)
        s_stm.inc(mu2)
        s_stm.wait(V)
        vv = V.scalar_tensor_tensor(ln2t[1][:, :], ln2t[1][:, :], EPS, ln2t[2][:, :],
                                    op0=AOP.add, op1=AOP.subtract)
        s_stm.inc(vv)
        s_stm.wait(SC)
        SC.activation(ln2t[2][:, :], ln2t[1][:, :], AF.Ln)
        iv = SC.activation(ln2t[1][:, :], ln2t[2][:, :], AF.Exp, scale=-0.5)
        s_stm.inc(iv)
        s_stm.wait(V)
        ng = V.tensor_scalar(ln2t[0][:, :], ln2t[0][:, :], -1.0, None, op0=AOP.mult)
        s_stm.inc(ng)
        s_ap2 = sem("p2_apply")
        for dq in range(NDC):
            V.tensor_tensor(ln2t[3][:, :], acc[:, dq * TOK:(dq + 1) * TOK], ln2t[0][:, :], op=AOP.add)
            V.tensor_tensor(ln2t[3][:, :], ln2t[3][:, :], ln2t[1][:, :], op=AOP.mult)
            ap = V.tensor_scalar(acc[:, dq * TOK:(dq + 1) * TOK], ln2t[3][:, :],
                                 gb_sb[:, dq:dq + 1], gb_sb[:, 16 + dq:17 + dq],
                                 op0=AOP.mult, op1=AOP.add)
            s_ap2.inc(ap)
        s_ap2.wait(T)
        s_ap2.wait(SC)
        s_ap2.wait(G)
        s_ap2.wait(SY)
    p2s.close()
    if debug:
        sd = sem("dbgf")
        sd.inc(SY.dma_start(dbg["ffn"][:, :], acc[:, :]), 16)
        sd.wait(SY)

    if stop_after == "p2":
        p23.close()
        top.close()
        return nc
    p3s = ExitStack()
    wos = sbN(p3s, "wos", 2, NDC * 128, BF16)
    x2 = sbN(p3s, "x2", 2, D, F32)
    otm = sbN(p3s, "otm", 2, D, F32)
    if True:
        s_wo = sem("p3_wo")
        s_womm = sem("p3_womm")
        s_wadd = sem("p3_wadd")
        s_xl = sem("p3_xl")
        s_xt = sem("p3_xt")
        s_xadd = sem("p3_xadd")
        s_ot = sem("p3_ot")
        s_oev = sem("p3_oev")
        s_ost = sem("p3_ost")
        v_womm = {}
        v_wadd = {}
        v_xl = {}
        v_xadd = {}
        v_oev = {}
        v_ost = {}
        for dq in range(NDC):
            bi = dq % 2
            if dq >= 2:
                s_womm.wait(G, v_womm[dq - 2])
            s_wo.inc(G.dma_start(wos[bi].rearrange("p (c k) -> p c k", c=NDC),
                                 wo_d[dq, :, :, :].rearrange("c p k -> p c k")), 16)
            s_wo.wait(T)
            pd = psC if dq % 2 == 0 else psA[:, 0:1024]
            if dq >= 2:
                s_wadd.wait(T, v_wadd[dq - 2])
            for fc in range(NDC):
                for grp in range(2):
                    mm = T.matmul(pd[:, grp * 512:(grp + 1) * 512],
                                  wos[bi][:, fc * 128:(fc + 1) * 128],
                                  ctx[:, fc * TOK + grp * 512: fc * TOK + (grp + 1) * 512],
                                  start=(fc == 0), stop=(fc == NDC - 1))
            v_womm[dq] = s_womm.inc(mm)
            s_womm.wait(V)
            ad = V.tensor_tensor(acc[:, dq * TOK:(dq + 1) * TOK],
                                 acc[:, dq * TOK:(dq + 1) * TOK], pd[:, :], op=AOP.add)
            v_wadd[dq] = s_wadd.inc(ad)
        accv = acc.ap().rearrange("p (c n) -> p c n", n=TOK)
        for tc in range(NSLOT):
            bi = tc % 2
            if tc >= 2:
                s_xadd.wait(SY, v_xadd[tc - 2])
            v_xl[tc] = s_xl.inc(SY.dma_start(x2[bi][:, :], x_in[2 * tc * 128:(2 * tc + 1) * 128, :]), 16)
            s_xl.wait(T)
            if tc == 0:
                s_wadd.wait(T)
            else:
                s_xadd.wait(T, v_xadd[tc - 1])
            for dc in range(NDC):
                tp = T.transpose(psA[:, dc * 128:(dc + 1) * 128],
                                 x2[bi][:, dc * 128:(dc + 1) * 128], eye32[:, :])
            s_xt.inc(tp)
            s_xt.wait(V)
            ad = V.tensor_tensor(accv[:, :, tc * 128:(tc + 1) * 128],
                                 accv[:, :, tc * 128:(tc + 1) * 128],
                                 psA.ap().rearrange("p (c n) -> p c n", n=128), op=AOP.add)
            v_xadd[tc] = s_xadd.inc(ad)
        for tc in range(NSLOT):
            bi = tc % 2
            if tc == 0:
                s_xadd.wait(T)
            else:
                s_oev.wait(T, v_oev[tc - 1])
            s_xadd.wait(T, v_xadd[tc])
            for dc in range(NDC):
                tp = T.transpose(psA[:, dc * 128:(dc + 1) * 128],
                                 acc[:, dc * TOK + tc * 128: dc * TOK + (tc + 1) * 128], eye32[:, :])
            s_ot.inc(tp)
            s_ot.wait(V)
            if tc >= 2:
                s_ost.wait(V, v_ost[tc - 2])
            ev = V.tensor_copy(otm[bi][:, :], psA[:, :])
            v_oev[tc] = s_oev.inc(ev)
            s_oev.wait(SY)
            v_ost[tc] = s_ost.inc(SY.dma_start(out_d[tc * 128:(tc + 1) * 128, :], otm[bi][:, :]), 16)
        s_ost.wait(SY)
        s_ost.wait(T)
        s_ost.wait(V)
        s_ost.wait(SC)
        s_ost.wait(G)
    p3s.close()
    p23.close()
    top.close()
    return nc



def _prep_shared(wq, wk, wv, wo, up_w, down_w, ew, gate_w, ln_g, ln_b):
    f32 = np.float32
    perm = np.concatenate([np.arange(0, 128, 2), np.arange(1, 128, 2)])
    wq_p = wq.reshape(H, DK, D)[:, perm, :]
    wk_p = wk.reshape(H, DK, D)[:, perm, :]
    wq_pre = np.ascontiguousarray(wq_p.transpose(0, 2, 1).reshape(H, NDC, 128, 128), dtype=f32)
    wk_pre = np.ascontiguousarray(wk_p.transpose(0, 2, 1).reshape(H, NDC, 128, 128), dtype=f32)
    wv_pre = np.ascontiguousarray(
        wv.T.reshape(NDC, 128, NPASS, HPP * 128).transpose(2, 0, 1, 3), dtype=f32)
    wo_pre = np.ascontiguousarray(
        wo.T.reshape(NDC, 128, NDC, 128).transpose(2, 0, 1, 3), dtype=f32)
    up_pre = np.ascontiguousarray(
        up_w.reshape(NDC, 128, NIC, 128).transpose(2, 0, 1, 3), dtype=f32)
    dn_pre = np.ascontiguousarray(
        down_w.reshape(NSUP, ICPS, 128, NDC, 128).transpose(0, 3, 1, 2, 4), dtype=f32)
    e_pre = np.ascontiguousarray(
        ew.reshape(NDC, 128, NDC, 128).transpose(2, 0, 1, 3), dtype=f32)
    gate_pre = np.ascontiguousarray(
        gate_w.reshape(NDC, 128, NEXP).transpose(1, 0, 2).reshape(128, 128), dtype=f32)
    gb_pre = np.ascontiguousarray(
        np.concatenate([ln_g.reshape(NDC, 128).T, ln_b.reshape(NDC, 128).T], axis=1), dtype=f32)
    return dict(wq=wq_pre, wk=wk_pre, wv=wv_pre, wo=wo_pre, up=up_pre, dn=dn_pre,
                ew=e_pre, gate=gate_pre, gb=gb_pre,
                eye=np.eye(128, dtype=f32), ones1=np.ones((1, 128), dtype=f32),
                sel=np.ascontiguousarray(np.repeat(np.eye(8, dtype=f32), 128, axis=1)))


def _prep_core(x_b, par):
    f32 = np.float32
    xb = np.asarray(x_b).reshape(16, 128, D)
    order = []
    for j in range(8):
        order += [2 * j + par, 2 * j + (1 - par)]
    x_sw = np.ascontiguousarray(xb[order].reshape(S, D), dtype=f32)

    inv = THETA ** (-(np.arange(0, DK, 2, dtype=np.float64) / DK))
    inv2 = np.concatenate([inv, inv])[:, None]
    own_pos = np.concatenate([np.arange(128) + (2 * j + par) * 128 for j in range(8)])
    oth_pos = np.concatenate([np.arange(128) + (2 * j + (1 - par)) * 128 for j in range(8)])
    sig_pos = np.concatenate([own_pos, oth_pos]).astype(np.float64)
    ang_k = sig_pos[None, :] * inv2
    ckt = np.cos(ang_k).astype(f32)
    skt = np.sin(ang_k).astype(f32)
    skt[0:64] *= -1.0
    ang_q = own_pos.astype(np.float64)[None, :] * inv2
    scale = 1.0 / np.sqrt(DK)
    cqt = (np.cos(ang_q) * scale).astype(f32)
    sqt = (np.sin(ang_q) * scale).astype(f32)
    sqt[0:64] *= -1.0
    tri = np.where(np.arange(128)[None, :] > np.arange(128)[:, None], 0.0, 1.0).astype(f32)
    ptile = np.full((128, 128), 0.0 if par == 0 else 1.0, dtype=f32)
    maskp = np.ascontiguousarray(np.concatenate([tri, ptile], axis=1))
    return dict(x=x_sw, ckt=ckt, skt=skt, cqt=cqt, sqt=sqt, maskp=maskp)


_NC_CACHE = {}


def kernel(L0_attn_x, L0_attn_wq, L0_attn_wk, L0_attn_wv, L0_attn_wo,
           L0_ffn_gate_w, L0_ffn_up_w, L0_ffn_down_w, L0_ffn_expert_weight,
           L0_ffn_ln_gamma, L0_ffn_ln_beta):
    f32 = np.float32
    x = np.asarray(L0_attn_x, dtype=f32)
    shared = _prep_shared(np.asarray(L0_attn_wq, f32), np.asarray(L0_attn_wk, f32),
                          np.asarray(L0_attn_wv, f32), np.asarray(L0_attn_wo, f32),
                          np.asarray(L0_ffn_up_w, f32), np.asarray(L0_ffn_down_w, f32),
                          np.asarray(L0_ffn_expert_weight, f32),
                          np.asarray(L0_ffn_gate_w, f32),
                          np.asarray(L0_ffn_ln_gamma, f32), np.asarray(L0_ffn_ln_beta, f32))
    if "nc" not in _NC_CACHE:
        _NC_CACHE["nc"] = build_program(debug=False)
    nc = _NC_CACHE["nc"]
    in_maps = []
    for core in range(NCORE):
        b, par = core // 2, core % 2
        m = dict(shared)
        m.update(_prep_core(x[b], par))
        in_maps.append(m)
    res = run_bass_kernel_spmd(nc, in_maps, core_ids=list(range(NCORE)))
    out = np.empty((B, S, D), dtype=f32)
    for core in range(NCORE):
        b, par = core // 2, core % 2
        o = np.asarray(res.results[core]["out"]).reshape(NSLOT, 128, D)
        for j in range(NSLOT):
            out[b, (2 * j + par) * 128:(2 * j + par + 1) * 128, :] = o[j]
    return out


# version 05
# speedup vs baseline: 1.0998x; 1.0998x over previous
import numpy as np
import concourse.bass as bass
import concourse.mybir as mybir
from concourse.bass_utils import run_bass_kernel_spmd
from contextlib import ExitStack

F32 = mybir.dt.float32
BF16 = mybir.dt.bfloat16
AOP = mybir.AluOpType
AF = mybir.ActivationFunctionType
AX = mybir.AxisListType

B, S, D = 4, 2048, 2048
H, DK = 16, 128
INTER = 8192
NEXP = 8
THETA = 10000.0
EPS = 1e-5
NCORE = 8
NSLOT = 8
TOK = NSLOT * 128
NDC = D // 128
NIC = INTER // 128
NSUP = 8
ICPS = NIC // NSUP
NEG = -1.0e30
NPASS = 4
HPP = H // NPASS


class Sem:
    def __init__(self, nc, name):
        self._cm = nc.semaphore(name)
        self.h = self._cm.__enter__()
        self.val = 0

    def inc(self, instr, n=1):
        instr.then_inc(self.h, n)
        self.val += n
        return self.val

    def wait(self, eng, val=None):
        eng.wait_ge(self.h, self.val if val is None else val)


def _mm_chunks(lo, hi):
    out = []
    c = lo
    while c < hi:
        n = min(hi, (c // 512 + 1) * 512) - c
        out.append((c, n))
        c += n
    return out


def build_program(debug=False, stop_after=None):
    nc = bass.Bass("TRN2", target_bir_lowering=False)
    top = ExitStack()

    dt = lambda n, shp, dty=F32, kind="ExternalInput": nc.dram_tensor(n, shp, dty, kind=kind).ap()
    x_in = dt("x", [S, D])
    wq_d = dt("wq", [H, NDC, 128, 128])
    wk_d = dt("wk", [H, NDC, 128, 128])
    wv_d = dt("wv", [NPASS, NDC, 128, HPP * 128])
    wo_d = dt("wo", [NDC, NDC, 128, 128])
    up_d = dt("up", [NIC, NDC, 128, 128])
    dn_d = dt("dn", [NSUP, NDC, ICPS, 128, 128])
    e_d = dt("ew", [NDC, NDC, 128, 128])
    gate_d = dt("gate", [128, 128])
    gb_d = dt("gb", [128, 32])
    ckt_d = dt("ckt", [128, S])
    skt_d = dt("skt", [128, S])
    cqt_d = dt("cqt", [128, TOK])
    sqt_d = dt("sqt", [128, TOK])
    mask_d = dt("maskp", [128, 256])
    eye_d = dt("eye", [128, 128])
    ones1_d = dt("ones1", [1, 128])
    sel_d = dt("sel", [8, 1024])
    out_d = dt("out", [TOK, D], F32, kind="ExternalOutput")
    dbg = {}
    if debug:
        dbg["xn_own"] = dt("dbg_xn_own", [128, NDC * TOK], F32, kind="ExternalOutput")
        dbg["kfm"] = dt("dbg_kfm", [128, S], F32, kind="ExternalOutput")
        dbg["vtm"] = dt("dbg_vtm", [128, 16 * HPP * 128], F32, kind="ExternalOutput")
        dbg["ctx"] = dt("dbg_ctx", [128, H * TOK], F32, kind="ExternalOutput")
        dbg["ffn"] = dt("dbg_ffn", [128, NDC * TOK], F32, kind="ExternalOutput")
        dbg["sbc"] = dt("dbg_sbc", [128, TOK], F32, kind="ExternalOutput")
        dbg["qfm"] = dt("dbg_qfm", [128, TOK], F32, kind="ExternalOutput")
        dbg["psb"] = dt("dbg_psb", [128, 9216], F32, kind="ExternalOutput")
        dbg["tail"] = dt("dbg_tail", [128, 1152], F32, kind="ExternalOutput")

    sb = lambda st_, n, shp, dty: st_.enter_context(nc.sbuf_tensor(n, shp, dty))

    def sbN(st_, name, n, w, dty):
        t = st_.enter_context(nc.sbuf_tensor(name, [128, n * w], dty))
        return [t[:, i * w:(i + 1) * w] for i in range(n)]
    xn_own = sb(top, "xn_own", [128, NDC * TOK], BF16)
    ctx = sb(top, "ctx", [128, H * TOK], BF16)
    eye32 = sb(top, "eye32", [128, 128], F32)
    eyebf = sb(top, "eyebf", [128, 128], BF16)
    onesbf = sb(top, "onesbf", [128, 128], BF16)
    ones1 = sb(top, "ones1_s", [1, 128], F32)
    gate_sb = sb(top, "gate_sb", [128, 128], BF16)
    gb_sb = sb(top, "gb_sb", [128, 32], F32)
    small = sb(top, "small", [128, 64], F32)
    rowb = sb(top, "rowb", [1, TOK], F32)
    sel_sb = sb(top, "sel_sb", [8, 1024], F32)
    rec8 = sb(top, "rec8", [8, 128], F32)

    psA = top.enter_context(nc.psum_tensor("psA", [128, 2048], F32))
    psB = top.enter_context(nc.psum_tensor("psB", [128, 512], F32))
    psC = top.enter_context(nc.psum_tensor("psC", [128, 1024], F32))
    psD = top.enter_context(nc.psum_tensor("psD", [128, 512], F32))
    psBb = psB.ap().bitcast(BF16)
    psAb = psA.ap().bitcast(BF16)

    T, V, SC, G, SY = nc.tensor, nc.vector, nc.scalar, nc.gpsimd, nc.sync
    ALL_ENG = (T, V, SC, G, SY)

    sems = {}

    def sem(name):
        if name not in sems:
            sems[name] = Sem(nc, name)
        return sems[name]

    def barrier(name):
        s = sem(name)
        base = s.val
        for eng in ALL_ENG:
            eng.sem_inc(s.h, 1)
        s.val = base + 5
        for eng in ALL_ENG:
            s.wait(eng)

    s_const = sem("const")
    for dst, src in ((eye32, eye_d), (gb_sb, gb_d)):
        s_const.inc(SY.dma_start(dst[:, :], src[:, :]), 16)
    s_const.inc(SY.dma_start(ones1[:, :], ones1_d[:, :]), 16)
    s_const.inc(SY.dma_start(sel_sb[:, :], sel_d[:, :]), 16)
    s_cbf = sem("cbf")
    s_cbf.inc(G.dma_start(eyebf[:, :], eye_d[:, :]), 16)
    s_cbf.inc(G.dma_start(gate_sb[:, :], gate_d[:, :]), 16)
    s_ones = sem("onesw")
    s_ones.inc(V.memset(onesbf[:, :], 1.0))

    p01 = ExitStack()
    xn_oth = sb(p01, "xn_oth", [128, NDC * TOK], BF16)
    ckt = sb(p01, "ckt_s", [128, S], F32)
    skt = sb(p01, "skt_s", [128, S], F32)
    cqt = sb(p01, "cqt_s", [128, TOK], F32)
    sqt = sb(p01, "sqt_s", [128, TOK], F32)
    maskp = sb(p01, "maskp_s", [128, 256], F32)
    for dst, src in ((ckt, ckt_d), (skt, skt_d), (cqt, cqt_d), (sqt, sqt_d), (maskp, mask_d)):
        s_const.inc(SY.dma_start(dst[:, :], src[:, :]), 16)
    s_const.wait(V)
    s_const.wait(T)
    s_const.wait(SC)

    p0s = ExitStack()
    x_tm = sbN(p0s, "x_tm", 2, D, F32)
    xn_tm = sbN(p0s, "xn_tm", 2, D, BF16)
    if True:
        s_x = sem("p0_x")
        s_ap = sem("p0_ap")
        s_iv = sem("p0_iv")
        s_nm = sem("p0_nm")
        s_tp = sem("p0_tp")
        s_ev = sem("p0_ev")
        v_ap = {}
        v_tp = {}
        for gc in range(NDC):
            bi = gc % 2
            if gc >= 2:
                s_ap.wait(SY, v_ap[gc - 2])
            s_x.inc(SY.dma_start(x_tm[bi][:, :], x_in[gc * 128:(gc + 1) * 128, :]), 16)
            s_x.wait(V)
            for q in range(4):
                st4 = V.bn_stats(small[:, 6 * q:6 * q + 6], x_tm[bi][:, 512 * q:512 * (q + 1)])
            s_nm.inc(st4)
            s_nm.wait(V)
            ag4 = V.bn_aggr(small[:, 24:26], small[:, 0:24])
            s_nm.inc(ag4)
            s_nm.wait(V)
            V.tensor_scalar(small[:, 26:27], small[:, 24:25], -1.0, None, op0=AOP.mult)
            nm = V.tensor_scalar(small[:, 27:28], small[:, 25:26], EPS, None, op0=AOP.add)
            s_nm.inc(nm)
            s_nm.wait(SC)
            SC.activation(small[:, 28:29], small[:, 27:28], AF.Ln)
            iv = SC.activation(small[:, 29:30], small[:, 28:29], AF.Exp, scale=-0.5)
            s_iv.inc(iv)
            s_iv.wait(V)
            s_nm.wait(V)
            if gc >= 2:
                s_tp.wait(V, v_tp[gc - 2])
            ap = V.tensor_scalar(xn_tm[bi][:, :], x_tm[bi][:, :], small[:, 26:27],
                                 small[:, 29:30], op0=AOP.add, op1=AOP.mult)
            v_ap[gc] = s_ap.inc(ap)
            s_ap.wait(T)
            if gc >= 1:
                s_ev.wait(T, 2 * gc)
            tgt = xn_own if bi == 0 else xn_oth
            j = gc // 2
            for half in range(2):
                pp = psBb if half == 0 else psAb[:, 0:1024]
                for k in range(8):
                    dc = half * 8 + k
                    tp = T.transpose(pp[:, k * 128:(k + 1) * 128],
                                     xn_tm[bi][:, dc * 128:(dc + 1) * 128], eyebf[:, :])
                s_tp.inc(tp)
                v_tp[gc] = s_tp.val
                s_tp.wait(V)
                dview = tgt.ap().rearrange("p (c n) -> p c n", n=TOK)[:, half * 8:half * 8 + 8,
                                                                     j * 128:(j + 1) * 128]
                sview = pp.rearrange("p (c n) -> p c n", n=128)
                ev = V.tensor_copy(dview, sview)
                s_ev.inc(ev)
        s_ev.wait(V)
    p0s.close()
    barrier("b_p0")
    if debug:
        sd = sem("dbg0")
        sd.inc(G.dma_start(dbg["xn_own"][:, :], xn_own[:, :]), 16)
        sd.wait(G)
        sd.wait(SY)

    if stop_after == "p0":
        p01.close()
        top.close()
        return nc
    poff = [0]
    for j in range(NSLOT):
        poff.append(poff[-1] + (2 * j + 2) * 128)
    PSB = poff[-1]

    p1s = ExitStack()
    kfm = sb(p1s, "kfm", [128, HPP * S], BF16)
    vtm = sb(p1s, "vtm", [128, 16 * HPP * 128], BF16)
    ovl = sb(p1s, "ovl", [128, PSB], BF16)
    wkb = sbN(p1s, "wkb", 2, NDC * 128, BF16)
    wqb = sbN(p1s, "wqb", 2, NDC * 128, BF16)
    qfm = sb(p1s, "qfm", [128, TOK], BF16)
    ptsb_t = sb(p1s, "ptsb", [128, 2 * 2048], BF16)
    ptsb = [ptsb_t[:, 0:2048], ptsb_t[:, 2048:4096]]
    rsw = ptsb_t.ap().bitcast(F32)[:, 0:1024]
    if True:
        psb_sb = ovl
        wvb = ovl.ap()[:, 0:NDC * HPP * 128]

        s_wv = sem("p1_wv")
        s_wk = sem("p1_wk")
        s_wq = sem("p1_wq")
        s_vp = sem("p1_vp")
        s_vev = sem("p1_vev")
        s_kp = sem("p1_kp")
        s_kro = sem("p1_kro")
        s_kswc = sem("p1_kswc")
        s_qp = sem("p1_qp")
        s_qro = sem("p1_qro")
        s_sm = sem("p1_sm")
        s_mx = sem("p1_mx")
        s_ex = sem("p1_ex")
        s_msk = sem("p1_msk")
        v_msk_head = {}
        s_pt = sem("p1_pt")
        s_ptev = sem("p1_ptev")
        s_av = sem("p1_av")
        s_rb = sem("p1_rb")
        s_cx = sem("p1_cx")

        v_vev = {}
        v_vp = {}
        v_kp = {}
        v_kro = {}
        v_pt_head = {}
        v_ex_head = {}
        v_cx_head = {}
        v_av_head = {}

        for p in range(NPASS):
            if p > 0:
                s_pt.wait(G, v_pt_head[p * HPP - 1])
            s_wv.inc(G.dma_start(wvb.rearrange("p (c v) -> p c v", c=NDC),
                                 wv_d[p, :, :, :].rearrange("c p v -> p c v")), 16)
            s_wv.wait(T)
            if p > 0:
                s_cx.wait(T, v_cx_head[p * HPP - 1])
            for gb in range(16):
                src = xn_own if gb < 8 else xn_oth
                j = gb % 8
                pv = psB if gb % 2 == 0 else psD
                if gb >= 2:
                    s_vev.wait(T, v_vev[(p, gb - 2)])
                for dc in range(NDC):
                    mm = T.matmul(pv[:, 0:HPP * 128],
                                  src[:, dc * TOK + j * 128: dc * TOK + (j + 1) * 128],
                                  wvb[:, dc * HPP * 128:(dc + 1) * HPP * 128],
                                  start=(dc == 0), stop=(dc == NDC - 1))
                v_vp[(p, gb)] = s_vp.inc(mm)
                s_vp.wait(SC)
                if p > 0 and gb == 0:
                    s_av.wait(SC, v_av_head[p * HPP - 1])
                ev = SC.copy(vtm[:, gb * HPP * 128:(gb + 1) * HPP * 128], pv[:, 0:HPP * 128])
                v_vev[(p, gb)] = s_vev.inc(ev)
            for ih in range(HPP):
                h = p * HPP + ih
                bi = ih % 2
                if ih >= 2:
                    s_kp.wait(G, v_kp[h - 2])
                s_wk.inc(G.dma_start(wkb[bi].rearrange("p (c k) -> p c k", c=NDC),
                                     wk_d[h, :, :, :].rearrange("c p k -> p c k")), 16)
                s_wk.wait(T)
                if ih == 0:
                    s_ex.wait(T)
                else:
                    s_kro.wait(T, v_kro[h - 1])
                for dc in range(NDC):
                    for si in range(2):
                        src = xn_own if si == 0 else xn_oth
                        for grp in range(2):
                            mm = T.matmul(psA[:, si * 1024 + grp * 512: si * 1024 + (grp + 1) * 512],
                                          wkb[bi][:, dc * 128:(dc + 1) * 128],
                                          src[:, dc * TOK + grp * 512: dc * TOK + (grp + 1) * 512],
                                          start=(dc == 0), stop=(dc == NDC - 1))
                v_kp[h] = s_kp.inc(mm)
                if p > 0 and ih == 0:
                    s_cx.wait(SC, v_cx_head[p * HPP - 1])
                for hf in range(2):
                    s_kp.wait(SC)
                    if hf == 1:
                        s_kro.wait(SC)
                    SC.copy(rsw[0:64, :], psA[64:128, hf * 1024:(hf + 1) * 1024])
                    hc = SC.copy(rsw[64:128, :], psA[0:64, hf * 1024:(hf + 1) * 1024])
                    s_kswc.inc(hc)
                    s_kp.wait(V)
                    s_kswc.wait(V)
                    V.tensor_tensor(psC[:, :], psA[:, hf * 1024:(hf + 1) * 1024],
                                    ckt[:, hf * 1024:(hf + 1) * 1024], op=AOP.mult)
                    V.tensor_tensor(rsw[:, :], rsw[:, :],
                                    skt[:, hf * 1024:(hf + 1) * 1024], op=AOP.mult)
                    rp = V.tensor_tensor(kfm[:, ih * S + hf * 1024: ih * S + (hf + 1) * 1024],
                                         psC[:, :], rsw[:, :], op=AOP.add)
                    s_kro.inc(rp)
                v_kro[h] = s_kro.val
            if debug and p == 0:
                sdk = sem("dbgk")
                s_kro.wait(G, v_kro[HPP - 1])
                s_vev.wait(G)
                sdk.inc(G.dma_start(dbg["kfm"][:, :], kfm[:, 0:S]), 16)
                sdk.inc(G.dma_start(dbg["vtm"][:, 0:2048], vtm[:, 0:2048]), 16)
                sdk.wait(G)

            bst = {}

            def beta_begin(hb):
                bst[hb] = dict(av_base=s_av.val, prev_ptev=s_ptev.val,
                               evac={}, pt={}, avp={})
                if hb > 0 and (hb - 1) in v_cx_head:
                    s_cx.wait(T, v_cx_head[hb - 1])
                s_msk.wait(T, v_msk_head[hb])

            def _av_pair(hb, ihb, jp, is_first, is_last):
                st = bst[hb]
                q0 = jp * 128
                for ki, k in enumerate((jp, 8 + jp)):
                    pt_src = ptsb[jp % 2]
                    off = ki * 1024
                    for (c0, n) in _mm_chunks(q0, 1024):
                        mm = T.matmul(psC[:, c0:c0 + n],
                                      vtm[:, k * HPP * 128 + ihb * 128: k * HPP * 128 + (ihb + 1) * 128],
                                      pt_src[:, off + c0 - q0: off + c0 - q0 + n],
                                      start=(is_first and ki == 0), stop=(is_last and ki == 1))
                st["avp"][jp] = s_av.inc(mm)

            def _tp_block(hb, jp, ki):
                st = bst[hb]
                k = jp if ki == 0 else 8 + jp
                for sj in range(jp, NSLOT):
                    own_w = (sj + 1) * 128
                    col = (k * 128) if k < 8 else (own_w + (k - 8) * 128)
                    tp = T.transpose(psBb[:, (sj - jp) * 128:(sj - jp + 1) * 128],
                                     psb_sb[:, poff[sj] + col: poff[sj] + col + 128],
                                     eyebf[:, :])
                st["pt"][(jp, ki)] = s_pt.inc(tp)

            def _evac_block(hb, jp, ki):
                st = bst[hb]
                nsl = (NSLOT - jp) * 128
                s_pt.wait(V, st["pt"][(jp, ki)])
                if ki == 0:
                    s_av.wait(V, st["avp"][jp - 2] if jp >= 2 else st["av_base"])
                pe = V.tensor_copy(ptsb[jp % 2][:, ki * 1024: ki * 1024 + nsl],
                                   psBb[:, 0:nsl])
                st["evac"][(jp, ki)] = s_ptev.inc(pe)

            def beta_pair_transposes(hb, ihb, jp):
                st = bst[hb]
                if jp > 0:
                    s_ptev.wait(T, st["evac"][jp - 1])
                    _av_pair(hb, ihb, jp - 1, jp - 1 == 0, False)
                else:
                    s_ptev.wait(T, st["prev_ptev"])
                _tp_block(hb, jp, 0)

            def beta_pair_evacs(hb, jp):
                st = bst[hb]
                _evac_block(hb, jp, 0)
                s_ptev.wait(T, st["evac"][(jp, 0)])
                _tp_block(hb, jp, 1)
                st["pt"][jp] = st["pt"][(jp, 1)]
                _evac_block(hb, jp, 1)
                st["evac"][jp] = st["evac"][(jp, 1)]

            def beta_tail(hb, ihb):
                st = bst[hb]
                s_ptev.wait(T, st["evac"][7])
                _av_pair(hb, ihb, 7, False, True)
                s_ex.wait(V, v_ex_head[hb])
                rc = V.reciprocal(small[:, 48:56],
                                  small[:, 32 + 8 * (hb % 2):40 + 8 * (hb % 2)])
                s_rb.inc(rc)
                s_rb.wait(T)
                tpm = T.matmul(psD[0:8, 0:128], small[:, 48:56], eye32[:, :],
                               start=True, stop=True)
                s_rb.inc(tpm)
                s_rb.wait(SC)
                cp = SC.copy(rec8[:, :], psD[0:8, 0:128])
                s_rb.inc(cp)
                s_rb.wait(T)
                s_ptev.wait(T, st["evac"][7])
                for sj in range(NSLOT):
                    pd = psB if sj < 4 else psD
                    bc = T.matmul(pd[:, (sj % 4) * 128:(sj % 4 + 1) * 128],
                                  sel_sb[:, sj * 128:(sj + 1) * 128], rec8[:, :],
                                  start=True, stop=True)
                s_rb.inc(bc)
                s_rb.wait(SC)
                SC.copy(rsw[:, 0:512], psB[:, :])
                rc2 = SC.copy(rsw[:, 512:1024], psD[:, :])
                s_rb.inc(rc2)
                if debug and hb == 15:
                    sdt = sem("dbgt")
                    s_rb.wait(G)
                    sdt.inc(G.dma_start(dbg["tail"][:, 0:64], small[:, :]), 16)
                    sdt.inc(G.dma_start(dbg["tail"][0:8, 64:192], rec8[:, :]), 16)
                    sdt.inc(G.dma_start(dbg["tail"][:, 192:1216] if False else dbg["tail"][:, 128:1152], rsw[:, :]), 16)
                    sdt.wait(G)
                    sdt.wait(V)
                    sdt.wait(T)
                s_rb.wait(V)
                c1 = V.tensor_tensor(ctx[:, hb * TOK: hb * TOK + 512], psC[:, 0:512],
                                     rsw[:, 0:512], op=AOP.mult)
                c2 = V.tensor_tensor(ctx[:, hb * TOK + 512: hb * TOK + 1024], psC[:, 512:1024],
                                     rsw[:, 512:1024], op=AOP.mult)
                s_cx.inc(c1)
                v_cx_head[hb] = s_cx.inc(c2)

            pending = None
            for ih in range(HPP):
                h = p * HPP + ih
                if h > 1:
                    s_qp.wait(G, h - 1)
                s_wq.inc(G.dma_start(wqb[h % 2].rearrange("p (c k) -> p c k", c=NDC),
                                     wq_d[h, :, :, :].rearrange("c p k -> p c k")), 16)
                s_wq.wait(T)
                if ih == 0:
                    s_kro.wait(T, v_kro[p * HPP + HPP - 1])
                else:
                    s_ex.wait(T)
                for dc in range(NDC):
                    for grp in range(2):
                        mm = T.matmul(psA[:, grp * 512:(grp + 1) * 512],
                                      wqb[h % 2][:, dc * 128:(dc + 1) * 128],
                                      xn_own[:, dc * TOK + grp * 512: dc * TOK + (grp + 1) * 512],
                                      start=(dc == 0), stop=(dc == NDC - 1))
                s_qp.inc(mm)
                s_qp.wait(SC)
                gk = h - 1 if ih == 0 else h - 2
                if gk >= 0 and gk in v_cx_head:
                    s_cx.wait(SC, v_cx_head[gk])
                SC.copy(rsw[0:64, :], psA[64:128, 0:1024])
                hc = SC.copy(rsw[64:128, :], psA[0:64, 0:1024])
                s_kswc.inc(hc)
                s_qp.wait(V)
                s_kswc.wait(V)
                V.tensor_tensor(psA[:, 1024:2048], psA[:, 0:1024], cqt[:, :], op=AOP.mult)
                V.tensor_tensor(rsw[:, :], rsw[:, :], sqt[:, :], op=AOP.mult)
                rp = V.tensor_tensor(qfm[:, :], psA[:, 1024:2048], rsw[:, :], op=AOP.add)
                s_qro.inc(rp)

                if pending is not None:
                    beta_begin(pending)
                    beta_pair_transposes(pending, pending % HPP, 0)
                    beta_pair_evacs(pending, 0)
                s_qro.wait(T)
                for j in range(NSLOT):
                    own_w = (j + 1) * 128
                    ext = 2 * own_w
                    if j > 0:
                        s_ex.wait(T)
                    for w in range(2):
                        base_kv = w * 1024
                        col0 = w * own_w
                        for (c0, n) in _mm_chunks(col0, col0 + own_w):
                            off = c0 - col0
                            mm = T.matmul(psA[:, c0:c0 + n],
                                          qfm[:, j * 128:(j + 1) * 128],
                                          kfm[:, ih * S + base_kv + off: ih * S + base_kv + off + n],
                                          start=True, stop=True)
                    s_sm.inc(mm)
                    if pending is not None and j < NSLOT - 1:
                        beta_pair_transposes(pending, pending % HPP, j + 1)
                    s_sm.wait(V)
                    V.tensor_tensor(psA[:, j * 128:(j + 1) * 128], psA[:, j * 128:(j + 1) * 128],
                                    maskp[:, 0:128], op=AOP.add)
                    mk = V.tensor_tensor(psA[:, own_w + j * 128: own_w + (j + 1) * 128],
                                         psA[:, own_w + j * 128: own_w + (j + 1) * 128],
                                         maskp[:, 128:256], op=AOP.add)
                    s_mx.inc(mk)
                    if pending is not None and j < NSLOT - 1:
                        beta_pair_evacs(pending, j + 1)
                    s_mx.wait(SC)
                    if pending is not None:
                        s_pt.wait(SC, bst[pending]["pt"][(j, 1)])
                    elif h > 0 and j == 0:
                        s_pt.wait(SC, v_pt_head[h - 1])
                    ex = SC.activation(psb_sb[:, poff[j]:poff[j] + ext], psA[:, 0:ext], AF.Exp,
                                       accum_out=small[:, 32 + 8 * (h % 2) + j:33 + 8 * (h % 2) + j])
                    s_ex.inc(ex)
                v_msk_head[h] = s_msk.val
                v_ex_head[h] = s_ex.val
                if pending is not None:
                    beta_tail(pending, pending % HPP)
                    v_pt_head[pending] = s_pt.val
                    v_av_head[pending] = s_av.val

                if debug and h == 0:
                    sdq = sem("dbgq")
                    s_qro.wait(G)
                    sdq.inc(G.dma_start(dbg["qfm"][:, :], qfm[:, :]), 16)
                    s_msk.wait(G, v_msk_head[h])
                    sdq.inc(G.dma_start(dbg["psb"][:, :], psb_sb[:, :]), 16)
                    sdq.wait(G)
                    sdq.wait(SC)
                    sdq.wait(T)
                pending = h
            beta_begin(pending)
            for j in range(NSLOT):
                beta_pair_transposes(pending, pending % HPP, j)
                beta_pair_evacs(pending, j)
            beta_tail(pending, pending % HPP)
            v_pt_head[pending] = s_pt.val
            v_av_head[pending] = s_av.val
            pending = None
        s_cx.wait(V)
        s_cx.wait(T)
        s_cx.wait(SC)
    p1s.close()
    p01.close()
    barrier("b_p1")
    if debug:
        sd = sem("dbgc")
        sd.inc(G.dma_start(dbg["ctx"][:, :], ctx[:, :]), 16)
        sd.wait(G)
        sd.wait(SY)

    if stop_after == "p1":
        top.close()
        return nc
    p23 = ExitStack()
    acc = sb(p23, "acc", [128, NDC * TOK], F32)

    p2s = ExitStack()
    wup = sbN(p2s, "wup", 2, NDC * 128, BF16)
    wdn = sbN(p2s, "wdn", 2, ICPS * 128, BF16)
    wee = sbN(p2s, "wee", 2, NDC * 128, BF16)
    hsup = sb(p2s, "hsup", [128, ICPS * TOK], BF16)
    ln2t = sbN(p2s, "ln2t", 4, TOK, F32)
    ybf = sbN(p2s, "ybf", 2, TOK, BF16)
    y2bf = sbN(p2s, "y2bf", 2, TOK, BF16)
    rtmp = sbN(p2s, "rtmp", 2, TOK, F32)
    if True:
        s_wup = sem("p2_wup")
        s_wdn = sem("p2_wdn")
        s_we = sem("p2_we")
        s_up = sem("p2_up")
        s_h = sem("p2_h")
        s_dn = sem("p2_dn")
        s_acc = sem("p2_acc")
        v_up = {}
        v_h = {}
        v_dn = {}
        v_acc = {}
        for sup in range(NSUP):
            for icw in range(ICPS):
                ic = sup * ICPS + icw
                bi = ic % 2
                if ic >= 2:
                    s_up.wait(G, v_up[ic - 2])
                s_wup.inc(G.dma_start(wup[bi].rearrange("p (c k) -> p c k", c=NDC),
                                      up_d[ic, :, :, :].rearrange("c p k -> p c k")), 16)
                s_wup.wait(T)
                if ic >= 2:
                    s_h.wait(T, v_h[ic - 2])
                if sup > 0 and icw < 2:
                    s_acc.wait(T, v_acc[(sup - 1, NDC - 1)])
                ph = psA[:, (icw % 2) * 1024:(icw % 2) * 1024 + 1024]
                for dc in range(NDC):
                    for grp in range(2):
                        mm = T.matmul(ph[:, grp * 512:(grp + 1) * 512],
                                      wup[bi][:, dc * 128:(dc + 1) * 128],
                                      xn_own[:, dc * TOK + grp * 512: dc * TOK + (grp + 1) * 512],
                                      start=(dc == 0), stop=(dc == NDC - 1))
                v_up[ic] = s_up.inc(mm)
                s_up.wait(SC)
                rl = SC.activation(rtmp[ic % 2][:, :], ph[:, :], AF.Relu)
                s_up.inc(rl)
                s_up.wait(V)
                if sup > 0 and icw == 0:
                    s_dn.wait(V, v_dn[(sup - 1, NDC - 1)])
                hv = V.tensor_tensor(hsup[:, icw * TOK:(icw + 1) * TOK], rtmp[ic % 2][:, :],
                                     rtmp[ic % 2][:, :], op=AOP.mult)
                v_h[ic] = s_h.inc(hv)
            s_h.wait(T)
            for dq in range(NDC):
                bi = dq % 2
                gidx = sup * NDC + dq
                if gidx >= 2:
                    s_dn.wait(G, v_dn[(sup, dq - 2)] if dq >= 2 else v_dn[(sup - 1, NDC + dq - 2)])
                s_wdn.inc(G.dma_start(wdn[bi].rearrange("p (c k) -> p c k", c=ICPS),
                                      dn_d[sup, dq, :, :, :].rearrange("c p k -> p c k")), 16)
                s_wdn.wait(T)
                pd = psC if dq % 2 == 0 else psA[:, 0:1024]
                if gidx >= 2:
                    prev = (sup, dq - 2) if dq >= 2 else (sup - 1, NDC + dq - 2)
                    s_acc.wait(T, v_acc[prev])
                for icw in range(ICPS):
                    for grp in range(2):
                        mm = T.matmul(pd[:, grp * 512:(grp + 1) * 512],
                                      wdn[bi][:, icw * 128:(icw + 1) * 128],
                                      hsup[:, icw * TOK + grp * 512: icw * TOK + (grp + 1) * 512],
                                      start=(icw == 0), stop=(icw == ICPS - 1))
                v_dn[(sup, dq)] = s_dn.inc(mm)
                s_dn.wait(V)
                if sup == 0:
                    av = V.tensor_copy(acc[:, dq * TOK:(dq + 1) * TOK], pd[:, :])
                else:
                    av = V.tensor_tensor(acc[:, dq * TOK:(dq + 1) * TOK],
                                         acc[:, dq * TOK:(dq + 1) * TOK], pd[:, :], op=AOP.add)
                v_acc[(sup, dq)] = s_acc.inc(av)
        s_g = sem("p2_gate")
        s_cbf.wait(T)
        s_acc.wait(T)
        v_g_m8 = {}
        for tc in range(NSLOT):
            if tc > 0:
                s_g.wait(T, v_g_m8[tc - 1])
            for dc in range(NDC):
                mm = T.matmul(psD[:, 0:8], xn_own[:, dc * TOK + tc * 128: dc * TOK + (tc + 1) * 128],
                              gate_sb[:, dc * 8: dc * 8 + 8],
                              start=(dc == 0), stop=(dc == NDC - 1))
            s_g.inc(mm)
            s_g.wait(V)
            m8 = V.max(small[:, 48:56], psD[:, 0:8])
            v_g_m8[tc] = s_g.inc(m8)
            s_g.wait(V)
            tt = V.tensor_tensor(small[:, 56 + tc:57 + tc], small[:, 48:49], small[:, 49:50],
                                 op=AOP.add)
            s_g.inc(tt)
        s_g.wait(T)
        tpm = T.matmul(psD[0:8, 0:128], small[:, 56:64], eye32[:, :], start=True, stop=True)
        s_g.inc(tpm)
        s_g.wait(SC)
        cp = SC.copy(rec8[:, :], psD[0:8, 0:128])
        s_g.inc(cp)
        s_g.wait(T)
        for tc in range(NSLOT):
            mm = T.matmul(psC[:, tc * 128:(tc + 1) * 128],
                          sel_sb[:, tc * 128:(tc + 1) * 128], rec8[:, :],
                          start=True, stop=True)
        s_g.inc(mm)
        if debug:
            s_g.wait(V)
            dv = V.tensor_copy(ln2t[0][:, :], psC[:, :])
            sdv = sem("dbgs")
            sdv.inc(dv)
            sdv.wait(SY)
            sdv.inc(SY.dma_start(dbg["sbc"][:, :], ln2t[0][:, :]), 16)
            sdv.wait(SY)
        s_mo = sem("p2_moemm")
        s_ms = sem("p2_moes")
        v_mo = {}
        v_ms = {}
        s_g.wait(V)
        sbc_cp = V.tensor_copy(ln2t[2][:, :], psC[:, :])
        s_g.inc(sbc_cp)
        s_g.wait(V)
        for dq in range(NDC):
            bi = dq % 2
            if dq >= 2:
                s_mo.wait(G, v_mo[dq - 2])
            s_we.inc(G.dma_start(wee[bi].rearrange("p (c k) -> p c k", c=NDC),
                                 e_d[dq, :, :, :].rearrange("c p k -> p c k")), 16)
            s_we.wait(T)
            pd = psA[:, (dq % 2) * 1024:(dq % 2) * 1024 + 1024]
            if dq >= 2:
                s_ms.wait(T, v_ms[dq - 2][0])
            for dc in range(NDC):
                for grp in range(2):
                    mm = T.matmul(pd[:, grp * 512:(grp + 1) * 512],
                                  wee[bi][:, dc * 128:(dc + 1) * 128],
                                  xn_own[:, dc * TOK + grp * 512: dc * TOK + (grp + 1) * 512],
                                  start=(dc == 0), stop=(dc == NDC - 1))
            v_mo[dq] = s_mo.inc(mm)
            s_mo.wait(V)
            tt = V.tensor_tensor(ln2t[3][:, :], pd[:, :], ln2t[2][:, :], op=AOP.mult)
            a1 = s_ms.inc(tt)
            ad = V.tensor_tensor(acc[:, dq * TOK:(dq + 1) * TOK],
                                 acc[:, dq * TOK:(dq + 1) * TOK], ln2t[3][:, :], op=AOP.add)
            v_ms[dq] = (a1, s_ms.inc(ad))
        s_st = sem("p2_cast")
        s_smm = sem("p2_statmm")
        s_ones.wait(T)
        s_ms.wait(T)
        v_cast = {}
        v_smm = {}
        for dq in range(NDC):
            bi = dq % 2
            s_ms.wait(V, v_ms[dq][1])
            if dq >= 2:
                s_smm.wait(V, v_smm[dq - 2])
            cv = V.tensor_copy(ybf[bi][:, :], acc[:, dq * TOK:(dq + 1) * TOK])
            s_st.inc(cv)
            s_st.wait(SC)
            if dq >= 2:
                s_smm.wait(SC, v_smm[dq - 2])
            sq = SC.activation(y2bf[bi][:, :], ybf[bi][:, :], AF.Square)
            v_cast[dq] = s_st.inc(sq)
            s_st.wait(T)
            for grp in range(2):
                T.matmul(psA[:, grp * 512:512 + grp * 512], onesbf[:, :],
                         ybf[bi][:, grp * 512:(grp + 1) * 512],
                         start=(dq == 0), stop=(dq == NDC - 1))
                mm = T.matmul(psA[:, 1024 + grp * 512:1536 + grp * 512], onesbf[:, :],
                              y2bf[bi][:, grp * 512:(grp + 1) * 512],
                              start=(dq == 0), stop=(dq == NDC - 1))
            v_smm[dq] = s_smm.inc(mm)
        s_smm.wait(SC)
        SC.mul(ln2t[0][:, :], psA[:, 0:1024], 1.0 / D)
        m2 = SC.mul(ln2t[1][:, :], psA[:, 1024:2048], 1.0 / D)
        s_stm = sem("p2_stm")
        s_stm.inc(m2)
        s_stm.wait(V)
        mu2 = V.tensor_tensor(ln2t[2][:, :], ln2t[0][:, :], ln2t[0][:, :], op=AOP.mult)
        s_stm.inc(mu2)
        s_stm.wait(V)
        vv = V.scalar_tensor_tensor(ln2t[1][:, :], ln2t[1][:, :], EPS, ln2t[2][:, :],
                                    op0=AOP.add, op1=AOP.subtract)
        s_stm.inc(vv)
        s_stm.wait(SC)
        SC.activation(ln2t[2][:, :], ln2t[1][:, :], AF.Ln)
        iv = SC.activation(ln2t[1][:, :], ln2t[2][:, :], AF.Exp, scale=-0.5)
        s_stm.inc(iv)
        s_stm.wait(V)
        ng = V.tensor_scalar(ln2t[0][:, :], ln2t[0][:, :], -1.0, None, op0=AOP.mult)
        s_stm.inc(ng)
        s_ap2 = sem("p2_apply")
        for dq in range(NDC):
            V.tensor_tensor(ln2t[3][:, :], acc[:, dq * TOK:(dq + 1) * TOK], ln2t[0][:, :], op=AOP.add)
            V.tensor_tensor(ln2t[3][:, :], ln2t[3][:, :], ln2t[1][:, :], op=AOP.mult)
            ap = V.tensor_scalar(acc[:, dq * TOK:(dq + 1) * TOK], ln2t[3][:, :],
                                 gb_sb[:, dq:dq + 1], gb_sb[:, 16 + dq:17 + dq],
                                 op0=AOP.mult, op1=AOP.add)
            s_ap2.inc(ap)
        s_ap2.wait(T)
        s_ap2.wait(SC)
        s_ap2.wait(G)
        s_ap2.wait(SY)
    p2s.close()
    if debug:
        sd = sem("dbgf")
        sd.inc(SY.dma_start(dbg["ffn"][:, :], acc[:, :]), 16)
        sd.wait(SY)

    if stop_after == "p2":
        p23.close()
        top.close()
        return nc
    p3s = ExitStack()
    wos = sbN(p3s, "wos", 2, NDC * 128, BF16)
    x2 = sbN(p3s, "x2", 2, D, F32)
    otm = sbN(p3s, "otm", 2, D, F32)
    if True:
        s_wo = sem("p3_wo")
        s_womm = sem("p3_womm")
        s_wadd = sem("p3_wadd")
        s_xl = sem("p3_xl")
        s_xt = sem("p3_xt")
        s_xadd = sem("p3_xadd")
        s_ot = sem("p3_ot")
        s_oev = sem("p3_oev")
        s_ost = sem("p3_ost")
        v_womm = {}
        v_wadd = {}
        v_xl = {}
        v_xadd = {}
        v_oev = {}
        v_ost = {}
        for dq in range(NDC):
            bi = dq % 2
            if dq >= 2:
                s_womm.wait(G, v_womm[dq - 2])
            s_wo.inc(G.dma_start(wos[bi].rearrange("p (c k) -> p c k", c=NDC),
                                 wo_d[dq, :, :, :].rearrange("c p k -> p c k")), 16)
            s_wo.wait(T)
            pd = psC if dq % 2 == 0 else psA[:, 0:1024]
            if dq >= 2:
                s_wadd.wait(T, v_wadd[dq - 2])
            for fc in range(NDC):
                for grp in range(2):
                    mm = T.matmul(pd[:, grp * 512:(grp + 1) * 512],
                                  wos[bi][:, fc * 128:(fc + 1) * 128],
                                  ctx[:, fc * TOK + grp * 512: fc * TOK + (grp + 1) * 512],
                                  start=(fc == 0), stop=(fc == NDC - 1))
            v_womm[dq] = s_womm.inc(mm)
            s_womm.wait(V)
            ad = V.tensor_tensor(acc[:, dq * TOK:(dq + 1) * TOK],
                                 acc[:, dq * TOK:(dq + 1) * TOK], pd[:, :], op=AOP.add)
            v_wadd[dq] = s_wadd.inc(ad)
        accv = acc.ap().rearrange("p (c n) -> p c n", n=TOK)
        for tc in range(NSLOT):
            bi = tc % 2
            if tc >= 2:
                s_xadd.wait(SY, v_xadd[tc - 2])
            v_xl[tc] = s_xl.inc(SY.dma_start(x2[bi][:, :], x_in[2 * tc * 128:(2 * tc + 1) * 128, :]), 16)
            s_xl.wait(T)
            if tc == 0:
                s_wadd.wait(T)
            else:
                s_xadd.wait(T, v_xadd[tc - 1])
            for dc in range(NDC):
                tp = T.transpose(psA[:, dc * 128:(dc + 1) * 128],
                                 x2[bi][:, dc * 128:(dc + 1) * 128], eye32[:, :])
            s_xt.inc(tp)
            s_xt.wait(V)
            ad = V.tensor_tensor(accv[:, :, tc * 128:(tc + 1) * 128],
                                 accv[:, :, tc * 128:(tc + 1) * 128],
                                 psA.ap().rearrange("p (c n) -> p c n", n=128), op=AOP.add)
            v_xadd[tc] = s_xadd.inc(ad)
        for tc in range(NSLOT):
            bi = tc % 2
            if tc == 0:
                s_xadd.wait(T)
            else:
                s_oev.wait(T, v_oev[tc - 1])
            s_xadd.wait(T, v_xadd[tc])
            for dc in range(NDC):
                tp = T.transpose(psA[:, dc * 128:(dc + 1) * 128],
                                 acc[:, dc * TOK + tc * 128: dc * TOK + (tc + 1) * 128], eye32[:, :])
            s_ot.inc(tp)
            s_ot.wait(V)
            if tc >= 2:
                s_ost.wait(V, v_ost[tc - 2])
            ev = V.tensor_copy(otm[bi][:, :], psA[:, :])
            v_oev[tc] = s_oev.inc(ev)
            s_oev.wait(SY)
            v_ost[tc] = s_ost.inc(SY.dma_start(out_d[tc * 128:(tc + 1) * 128, :], otm[bi][:, :]), 16)
        s_ost.wait(SY)
        s_ost.wait(T)
        s_ost.wait(V)
        s_ost.wait(SC)
        s_ost.wait(G)
    p3s.close()
    p23.close()
    top.close()
    return nc



def _prep_shared(wq, wk, wv, wo, up_w, down_w, ew, gate_w, ln_g, ln_b):
    f32 = np.float32
    perm = np.concatenate([np.arange(0, 128, 2), np.arange(1, 128, 2)])
    wq_p = wq.reshape(H, DK, D)[:, perm, :]
    wk_p = wk.reshape(H, DK, D)[:, perm, :]
    wq_pre = np.ascontiguousarray(wq_p.transpose(0, 2, 1).reshape(H, NDC, 128, 128), dtype=f32)
    wk_pre = np.ascontiguousarray(wk_p.transpose(0, 2, 1).reshape(H, NDC, 128, 128), dtype=f32)
    wv_pre = np.ascontiguousarray(
        wv.T.reshape(NDC, 128, NPASS, HPP * 128).transpose(2, 0, 1, 3), dtype=f32)
    wo_pre = np.ascontiguousarray(
        wo.T.reshape(NDC, 128, NDC, 128).transpose(2, 0, 1, 3), dtype=f32)
    up_pre = np.ascontiguousarray(
        up_w.reshape(NDC, 128, NIC, 128).transpose(2, 0, 1, 3), dtype=f32)
    dn_pre = np.ascontiguousarray(
        down_w.reshape(NSUP, ICPS, 128, NDC, 128).transpose(0, 3, 1, 2, 4), dtype=f32)
    e_pre = np.ascontiguousarray(
        ew.reshape(NDC, 128, NDC, 128).transpose(2, 0, 1, 3), dtype=f32)
    gate_pre = np.ascontiguousarray(
        gate_w.reshape(NDC, 128, NEXP).transpose(1, 0, 2).reshape(128, 128), dtype=f32)
    gb_pre = np.ascontiguousarray(
        np.concatenate([ln_g.reshape(NDC, 128).T, ln_b.reshape(NDC, 128).T], axis=1), dtype=f32)
    return dict(wq=wq_pre, wk=wk_pre, wv=wv_pre, wo=wo_pre, up=up_pre, dn=dn_pre,
                ew=e_pre, gate=gate_pre, gb=gb_pre,
                eye=np.eye(128, dtype=f32), ones1=np.ones((1, 128), dtype=f32),
                sel=np.ascontiguousarray(np.repeat(np.eye(8, dtype=f32), 128, axis=1)))


def _prep_core(x_b, par):
    f32 = np.float32
    xb = np.asarray(x_b).reshape(16, 128, D)
    order = []
    for j in range(8):
        order += [2 * j + par, 2 * j + (1 - par)]
    x_sw = np.ascontiguousarray(xb[order].reshape(S, D), dtype=f32)

    inv = THETA ** (-(np.arange(0, DK, 2, dtype=np.float64) / DK))
    inv2 = np.concatenate([inv, inv])[:, None]
    own_pos = np.concatenate([np.arange(128) + (2 * j + par) * 128 for j in range(8)])
    oth_pos = np.concatenate([np.arange(128) + (2 * j + (1 - par)) * 128 for j in range(8)])
    sig_pos = np.concatenate([own_pos, oth_pos]).astype(np.float64)
    ang_k = sig_pos[None, :] * inv2
    ckt = np.cos(ang_k).astype(f32)
    skt = np.sin(ang_k).astype(f32)
    skt[0:64] *= -1.0
    ang_q = own_pos.astype(np.float64)[None, :] * inv2
    scale = 1.0 / np.sqrt(DK)
    cqt = (np.cos(ang_q) * scale).astype(f32)
    sqt = (np.sin(ang_q) * scale).astype(f32)
    sqt[0:64] *= -1.0
    tri = np.where(np.arange(128)[None, :] > np.arange(128)[:, None], NEG, 0.0).astype(f32)
    ptile = np.full((128, 128), NEG if par == 0 else 0.0, dtype=f32)
    maskp = np.ascontiguousarray(np.concatenate([tri, ptile], axis=1))
    return dict(x=x_sw, ckt=ckt, skt=skt, cqt=cqt, sqt=sqt, maskp=maskp)


_NC_CACHE = {}


def kernel(L0_attn_x, L0_attn_wq, L0_attn_wk, L0_attn_wv, L0_attn_wo,
           L0_ffn_gate_w, L0_ffn_up_w, L0_ffn_down_w, L0_ffn_expert_weight,
           L0_ffn_ln_gamma, L0_ffn_ln_beta):
    f32 = np.float32
    x = np.asarray(L0_attn_x, dtype=f32)
    shared = _prep_shared(np.asarray(L0_attn_wq, f32), np.asarray(L0_attn_wk, f32),
                          np.asarray(L0_attn_wv, f32), np.asarray(L0_attn_wo, f32),
                          np.asarray(L0_ffn_up_w, f32), np.asarray(L0_ffn_down_w, f32),
                          np.asarray(L0_ffn_expert_weight, f32),
                          np.asarray(L0_ffn_gate_w, f32),
                          np.asarray(L0_ffn_ln_gamma, f32), np.asarray(L0_ffn_ln_beta, f32))
    if "nc" not in _NC_CACHE:
        _NC_CACHE["nc"] = build_program(debug=False)
    nc = _NC_CACHE["nc"]
    in_maps = []
    for core in range(NCORE):
        b, par = core // 2, core % 2
        m = dict(shared)
        m.update(_prep_core(x[b], par))
        in_maps.append(m)
    res = run_bass_kernel_spmd(nc, in_maps, core_ids=list(range(NCORE)))
    out = np.empty((B, S, D), dtype=f32)
    for core in range(NCORE):
        b, par = core // 2, core % 2
        o = np.asarray(res.results[core]["out"]).reshape(NSLOT, 128, D)
        for j in range(NSLOT):
            out[b, (2 * j + par) * 128:(2 * j + par + 1) * 128, :] = o[j]
    return out


# version 11
# speedup vs baseline: 1.1250x; 1.0228x over previous
import numpy as np
import concourse.bass as bass
import concourse.mybir as mybir
from concourse.bass_utils import run_bass_kernel_spmd
from contextlib import ExitStack

F32 = mybir.dt.float32
BF16 = mybir.dt.bfloat16
AOP = mybir.AluOpType
AF = mybir.ActivationFunctionType
AX = mybir.AxisListType

B, S, D = 4, 2048, 2048
H, DK = 16, 128
INTER = 8192
NEXP = 8
THETA = 10000.0
EPS = 1e-5
NCORE = 8
NSLOT = 8
TOK = NSLOT * 128
NDC = D // 128
NIC = INTER // 128
NSUP = 8
ICPS = NIC // NSUP
NEG = -1.0e30
NPASS = 4
HPP = H // NPASS


class Sem:
    def __init__(self, nc, name):
        self._cm = nc.semaphore(name)
        self.h = self._cm.__enter__()
        self.val = 0

    def inc(self, instr, n=1):
        instr.then_inc(self.h, n)
        self.val += n
        return self.val

    def wait(self, eng, val=None):
        eng.wait_ge(self.h, self.val if val is None else val)


def _mm_chunks(lo, hi):
    out = []
    c = lo
    while c < hi:
        n = min(hi, (c // 512 + 1) * 512) - c
        out.append((c, n))
        c += n
    return out


def build_program(debug=False, stop_after=None):
    nc = bass.Bass("TRN2", target_bir_lowering=False)
    top = ExitStack()

    dt = lambda n, shp, dty=F32, kind="ExternalInput": nc.dram_tensor(n, shp, dty, kind=kind).ap()
    x_in = dt("x", [S, D])
    xt_in = dt("xT", [D, TOK])
    wq_d = dt("wq", [H, NDC, 128, 128])
    wk_d = dt("wk", [H, NDC, 128, 128])
    wv_d = dt("wv", [NPASS, NDC, 128, HPP * 128])
    wo_d = dt("wo", [NDC, NDC, 128, 128])
    up_d = dt("up", [NIC, NDC, 128, 128])
    dn_d = dt("dn", [NSUP, NDC, ICPS, 128, 128])
    e_d = dt("ew", [NDC, NDC, 128, 128])
    gate_d = dt("gate", [128, 128])
    gb_d = dt("gb", [128, 32])
    ckt_d = dt("ckt", [128, S])
    skt_d = dt("skt", [128, S])
    cqt_d = dt("cqt", [128, TOK])
    sqt_d = dt("sqt", [128, TOK])
    mask_d = dt("maskp", [128, 256])
    eye_d = dt("eye", [128, 128])
    ones1_d = dt("ones1", [1, 128])
    sel_d = dt("sel", [8, 1024])
    out_d = dt("out", [128, NDC * TOK], F32, kind="ExternalOutput")
    dbg = {}
    if debug:
        dbg["xn_own"] = dt("dbg_xn_own", [128, NDC * TOK], F32, kind="ExternalOutput")
        dbg["kfm"] = dt("dbg_kfm", [128, S], F32, kind="ExternalOutput")
        dbg["vtm"] = dt("dbg_vtm", [128, 16 * HPP * 128], F32, kind="ExternalOutput")
        dbg["ctx"] = dt("dbg_ctx", [128, H * TOK], F32, kind="ExternalOutput")
        dbg["ffn"] = dt("dbg_ffn", [128, NDC * TOK], F32, kind="ExternalOutput")
        dbg["sbc"] = dt("dbg_sbc", [128, TOK], F32, kind="ExternalOutput")
        dbg["qfm"] = dt("dbg_qfm", [128, TOK], F32, kind="ExternalOutput")
        dbg["psb"] = dt("dbg_psb", [128, 9216], F32, kind="ExternalOutput")
        dbg["tail"] = dt("dbg_tail", [128, 1152], F32, kind="ExternalOutput")

    sb = lambda st_, n, shp, dty: st_.enter_context(nc.sbuf_tensor(n, shp, dty))

    def sbN(st_, name, n, w, dty):
        t = st_.enter_context(nc.sbuf_tensor(name, [128, n * w], dty))
        return [t[:, i * w:(i + 1) * w] for i in range(n)]
    xn_own = sb(top, "xn_own", [128, NDC * TOK], BF16)
    ctx = sb(top, "ctx", [128, H * TOK], BF16)
    eye32 = sb(top, "eye32", [128, 128], F32)
    eyebf = sb(top, "eyebf", [128, 128], BF16)
    onesbf = sb(top, "onesbf", [128, 128], BF16)
    ones1 = sb(top, "ones1_s", [1, 128], F32)
    gate_sb = sb(top, "gate_sb", [128, 128], BF16)
    gb_sb = sb(top, "gb_sb", [128, 32], F32)
    small = sb(top, "small", [128, 64], F32)
    ovl = sb(top, "ovl", [128, 9216], BF16)
    sel_sb = sb(top, "sel_sb", [8, 1024], F32)
    rec8 = sb(top, "rec8", [8, 128], F32)

    psA = top.enter_context(nc.psum_tensor("psA", [128, 2048], F32))
    psB = top.enter_context(nc.psum_tensor("psB", [128, 512], F32))
    psC = top.enter_context(nc.psum_tensor("psC", [128, 1024], F32))
    psD = top.enter_context(nc.psum_tensor("psD", [128, 512], F32))
    psBb = psB.ap().bitcast(BF16)
    psAb = psA.ap().bitcast(BF16)

    T, V, SC, G, SY = nc.tensor, nc.vector, nc.scalar, nc.gpsimd, nc.sync
    ALL_ENG = (T, V, SC, G, SY)

    sems = {}

    def sem(name):
        if name not in sems:
            sems[name] = Sem(nc, name)
        return sems[name]

    def barrier(name):
        s = sem(name)
        base = s.val
        for eng in ALL_ENG:
            eng.sem_inc(s.h, 1)
        s.val = base + 5
        for eng in ALL_ENG:
            s.wait(eng)

    s_const = sem("const")
    for dst, src in ((eye32, eye_d), (gb_sb, gb_d)):
        s_const.inc(SY.dma_start(dst[:, :], src[:, :]), 16)
    s_const.inc(SY.dma_start(ones1[:, :], ones1_d[:, :]), 16)
    s_const.inc(SY.dma_start(sel_sb[:, :], sel_d[:, :]), 16)
    s_cbf = sem("cbf")
    s_cbf.inc(G.dma_start(eyebf[:, :], eye_d[:, :]), 16)
    s_cbf.inc(G.dma_start(gate_sb[:, :], gate_d[:, :]), 16)
    s_ones = sem("onesw")
    s_ones.inc(V.memset(onesbf[:, :], 1.0))

    p01 = ExitStack()
    xn_oth = sb(p01, "xn_oth", [128, NDC * TOK], BF16)
    ckt = sb(p01, "ckt_s", [128, S], F32)
    skt = sb(p01, "skt_s", [128, S], F32)
    cqt = sb(p01, "cqt_s", [128, TOK], F32)
    sqt = sb(p01, "sqt_s", [128, TOK], F32)
    maskp = sb(p01, "maskp_s", [128, 256], F32)
    for dst, src in ((ckt, ckt_d), (skt, skt_d), (cqt, cqt_d), (sqt, sqt_d), (maskp, mask_d)):
        s_const.inc(SY.dma_start(dst[:, :], src[:, :]), 16)
    s_const.wait(V)
    s_const.wait(T)
    s_const.wait(SC)
    s_wv0 = sem("p1_wv")
    wvb_pre = ovl.ap()[:, 0:NDC * HPP * 128]
    s_wv0.inc(G.dma_start(wvb_pre.rearrange("p (c v) -> p c v", c=NDC),
                          wv_d[0, :, :, :].rearrange("c p v -> p c v")), 16)

    poff = [0]
    for j in range(NSLOT):
        poff.append(poff[-1] + (2 * j + 2) * 128)
    PSB = poff[-1]

    p1s = ExitStack()
    kfm = sb(p1s, "kfm", [128, HPP * S], BF16)
    vtm = sb(p1s, "vtm", [128, 16 * HPP * 128], BF16)
    wkb = sbN(p1s, "wkb", 2, NDC * 128, BF16)
    wqb = sbN(p1s, "wqb", 2, NDC * 128, BF16)
    qfm = sb(p1s, "qfm", [128, TOK], BF16)
    ptsb_t = sb(p1s, "ptsb", [128, 2 * 2048], BF16)
    ptsb = [ptsb_t[:, 0:2048], ptsb_t[:, 2048:4096]]
    rsw = ptsb_t.ap().bitcast(F32)[:, 0:1024]
    psb_sb = ovl
    wvb = ovl.ap()[:, 0:NDC * HPP * 128]

    _kfmf = kfm.ap().bitcast(F32)
    x_tm = [_kfmf[:, 0:D], _kfmf[:, D:2 * D]]
    xn_tm = [ptsb_t[:, 0:D], ptsb_t[:, D:2 * D]]
    if True:
        s_x = sem("p0_x")
        s_ap = sem("p0_ap")
        s_iv = sem("p0_iv")
        s_nm = sem("p0_nm")
        s_tp = sem("p0_tp")
        s_ev = sem("p0_ev")
        v_ap = {}
        v_tp = {}
        v_ev = {}
        v_vp0 = {}
        v_vev0 = {}
        for gc in range(NDC):
            bi = gc % 2
            if gc >= 2:
                s_ap.wait(SY, v_ap[gc - 2])
            s_x.inc(SY.dma_start(x_tm[bi][:, :], x_in[gc * 128:(gc + 1) * 128, :]), 16)
            s_x.wait(V)
            for q in range(4):
                st4 = V.bn_stats(small[:, 6 * q:6 * q + 6], x_tm[bi][:, 512 * q:512 * (q + 1)])
            s_nm.inc(st4)
            s_nm.wait(V)
            ag4 = V.bn_aggr(small[:, 24:26], small[:, 0:24])
            s_nm.inc(ag4)
            s_nm.wait(V)
            V.tensor_scalar(small[:, 26:27], small[:, 24:25], -1.0, None, op0=AOP.mult)
            nm = V.tensor_scalar(small[:, 27:28], small[:, 25:26], EPS, None, op0=AOP.add)
            s_nm.inc(nm)
            s_nm.wait(SC)
            SC.activation(small[:, 28:29], small[:, 27:28], AF.Ln)
            iv = SC.activation(small[:, 29:30], small[:, 28:29], AF.Exp, scale=-0.5)
            s_iv.inc(iv)
            s_iv.wait(V)
            s_nm.wait(V)
            if gc >= 2:
                s_tp.wait(V, v_tp[gc - 2])
            ap = V.tensor_scalar(xn_tm[bi][:, :], x_tm[bi][:, :], small[:, 26:27],
                                 small[:, 29:30], op0=AOP.add, op1=AOP.mult)
            v_ap[gc] = s_ap.inc(ap)
            s_ap.wait(T)
            if gc >= 1:
                s_ev.wait(T, 2 * gc)
            tgt = xn_own if bi == 0 else xn_oth
            j = gc // 2
            psCb = psC.ap().bitcast(BF16)
            for half in range(2):
                pp = psCb[:, 0:1024] if half == 0 else psCb[:, 1024:2048]
                for k in range(8):
                    dc = half * 8 + k
                    tp = T.transpose(pp[:, k * 128:(k + 1) * 128],
                                     xn_tm[bi][:, dc * 128:(dc + 1) * 128], eyebf[:, :])
                s_tp.inc(tp)
                v_tp[gc] = s_tp.val
                s_tp.wait(V)
                dview = tgt.ap().rearrange("p (c n) -> p c n", n=TOK)[:, half * 8:half * 8 + 8,
                                                                     j * 128:(j + 1) * 128]
                sview = pp.rearrange("p (c n) -> p c n", n=128)
                ev = V.tensor_copy(dview, sview)
                s_ev.inc(ev)
            v_ev[gc] = s_ev.val
            gb = gc // 2 if gc % 2 == 0 else 8 + gc // 2
            s_vp0 = sem("p1_vp")
            s_vev0 = sem("p1_vev")
            vsrc = xn_own if gb < 8 else xn_oth
            jj = gb % 8
            pv = psB if gc % 2 == 0 else psD
            sem("p1_wv").wait(T)
            s_ev.wait(T, v_ev[gc])
            if gc >= 2:
                s_vev0.wait(T, v_vev0[gc - 2])
            for dc in range(NDC):
                mm = T.matmul(pv[:, 0:HPP * 128],
                              vsrc[:, dc * TOK + jj * 128: dc * TOK + (jj + 1) * 128],
                              wvb[:, dc * HPP * 128:(dc + 1) * HPP * 128],
                              start=(dc == 0), stop=(dc == NDC - 1))
            v_vp0[gc] = s_vp0.inc(mm)
            s_vp0.wait(SC)
            ev2 = SC.copy(vtm[:, gb * HPP * 128:(gb + 1) * HPP * 128], pv[:, 0:HPP * 128])
            v_vev0[gc] = s_vev0.inc(ev2)
        s_ev.wait(V)
    if debug:
        sd = sem("dbg0")
        s_ev.wait(G)
        sd.inc(G.dma_start(dbg["xn_own"][:, :], xn_own[:, :]), 16)
        sd.wait(G)
        sd.wait(SY)

    if stop_after == "p0":
        p1s.close()
        p01.close()
        top.close()
        return nc
    if True:

        s_wv = sem("p1_wv")
        s_wk = sem("p1_wk")
        s_wq = sem("p1_wq")
        s_vp = sem("p1_vp")
        s_vev = sem("p1_vev")
        s_kp = sem("p1_kp")
        s_kro = sem("p1_kro")
        s_kswc = sem("p1_kswc")
        s_kfree = sem("p1_kfree")
        v_kswc = {}
        v_kfree = {}
        s_qp = sem("p1_qp")
        s_qro = sem("p1_qro")
        s_sm = sem("p1_sm")
        s_mx = sem("p1_mx")
        s_ex = sem("p1_ex")
        s_msk = sem("p1_msk")
        v_msk_head = {}
        s_pt = sem("p1_pt")
        s_ptev = sem("p1_ptev")
        s_av = sem("p1_av")
        s_rb = sem("p1_rb")
        s_cx = sem("p1_cx")

        v_vev = {}
        v_vp = {}
        v_kp = {}
        v_kro = {}
        v_pt_head = {}
        v_ex_head = {}
        v_cx_head = {}
        v_av_head = {}

        for p in range(NPASS):
            if p > 0:
                s_pt.wait(G, v_pt_head[p * HPP - 1])
                s_wv.inc(G.dma_start(wvb.rearrange("p (c v) -> p c v", c=NDC),
                                     wv_d[p, :, :, :].rearrange("c p v -> p c v")), 16)
            s_wv.wait(T)
            if p > 0:
                s_cx.wait(T, v_cx_head[p * HPP - 1])
            gb_order = [b for pair in zip(range(8), range(8, 16)) for b in pair]
            for oi, gb in enumerate(gb_order):
                if p == 0:
                    break
                src = xn_own if gb < 8 else xn_oth
                j = gb % 8
                pv = psB if oi % 2 == 0 else psD
                if oi >= 2:
                    s_vev.wait(T, v_vev[(p, gb_order[oi - 2])])
                for dc in range(NDC):
                    mm = T.matmul(pv[:, 0:HPP * 128],
                                  src[:, dc * TOK + j * 128: dc * TOK + (j + 1) * 128],
                                  wvb[:, dc * HPP * 128:(dc + 1) * HPP * 128],
                                  start=(dc == 0), stop=(dc == NDC - 1))
                v_vp[(p, gb)] = s_vp.inc(mm)
                s_vp.wait(SC)
                if oi == 0:
                    s_av.wait(SC, v_av_head[p * HPP - 1])
                ev = SC.copy(vtm[:, gb * HPP * 128:(gb + 1) * HPP * 128], pv[:, 0:HPP * 128])
                v_vev[(p, gb)] = s_vev.inc(ev)
            for ih in range(HPP):
                h = p * HPP + ih
                bi = ih % 2
                if ih >= 2:
                    s_kp.wait(G, v_kp[h - 2])
                s_wk.inc(G.dma_start(wkb[bi].rearrange("p (c k) -> p c k", c=NDC),
                                     wk_d[h, :, :, :].rearrange("c p k -> p c k")), 16)
                s_wk.wait(T)
                if p > 0 and ih == 0:
                    s_cx.wait(SC, v_cx_head[p * HPP - 1])
                for hf in range(2):
                    if ih == 0:
                        if hf == 0:
                            s_ex.wait(T)
                            if p == 0:
                                s_ev.wait(T, v_ev[NDC - 1])
                    else:
                        s_kswc.wait(T, v_kswc[(h - 1, hf)])
                        s_kfree.wait(T, v_kfree[(h - 1, hf)])
                    src = xn_own if hf == 0 else xn_oth
                    for dc in range(NDC):
                        for grp in range(2):
                            mm = T.matmul(psA[:, hf * 1024 + grp * 512: hf * 1024 + (grp + 1) * 512],
                                          wkb[bi][:, dc * 128:(dc + 1) * 128],
                                          src[:, dc * TOK + grp * 512: dc * TOK + (grp + 1) * 512],
                                          start=(dc == 0), stop=(dc == NDC - 1))
                    v_kp[(h, hf)] = s_kp.inc(mm)
                v_kp[h] = v_kp[(h, 1)]
                for hf in range(2):
                    s_kp.wait(SC, v_kp[(h, hf)])
                    if hf == 1:
                        s_kro.wait(SC)
                    SC.copy(rsw[0:64, :], psA[64:128, hf * 1024:(hf + 1) * 1024])
                    hc = SC.copy(rsw[64:128, :], psA[0:64, hf * 1024:(hf + 1) * 1024])
                    v_kswc[(h, hf)] = s_kswc.inc(hc)
                    s_kp.wait(V, v_kp[(h, hf)])
                    s_kswc.wait(V)
                    mf = V.tensor_tensor(psC[:, :], psA[:, hf * 1024:(hf + 1) * 1024],
                                         ckt[:, hf * 1024:(hf + 1) * 1024], op=AOP.mult)
                    v_kfree[(h, hf)] = s_kfree.inc(mf)
                    V.tensor_tensor(rsw[:, :], rsw[:, :],
                                    skt[:, hf * 1024:(hf + 1) * 1024], op=AOP.mult)
                    rp = V.tensor_tensor(kfm[:, ih * S + hf * 1024: ih * S + (hf + 1) * 1024],
                                         psC[:, :], rsw[:, :], op=AOP.add)
                    s_kro.inc(rp)
                v_kro[h] = s_kro.val
            if debug and p == 0:
                sdk = sem("dbgk")
                s_kro.wait(G, v_kro[HPP - 1])
                s_vev.wait(G)
                sdk.inc(G.dma_start(dbg["kfm"][:, :], kfm[:, 0:S]), 16)
                sdk.inc(G.dma_start(dbg["vtm"][:, 0:2048], vtm[:, 0:2048]), 16)
                sdk.wait(G)

            bst = {}

            def beta_begin(hb):
                bst[hb] = dict(av_base=s_av.val, prev_ptev=s_ptev.val,
                               evac={}, pt={}, avp={})
                if hb > 0 and (hb - 1) in v_cx_head:
                    s_cx.wait(T, v_cx_head[hb - 1])
                s_msk.wait(T, v_msk_head[hb])

            def _av_pair(hb, ihb, jp, is_first, is_last):
                st = bst[hb]
                q0 = jp * 128
                for ki, k in enumerate((jp, 8 + jp)):
                    pt_src = ptsb[jp % 2]
                    off = ki * 1024
                    for (c0, n) in _mm_chunks(q0, 1024):
                        mm = T.matmul(psC[:, c0:c0 + n],
                                      vtm[:, k * HPP * 128 + ihb * 128: k * HPP * 128 + (ihb + 1) * 128],
                                      pt_src[:, off + c0 - q0: off + c0 - q0 + n],
                                      start=(is_first and ki == 0), stop=(is_last and ki == 1))
                st["avp"][jp] = s_av.inc(mm)

            def _tp_block(hb, jp, ki):
                st = bst[hb]
                k = jp if ki == 0 else 8 + jp
                for sj in range(jp, NSLOT):
                    own_w = (sj + 1) * 128
                    col = (k * 128) if k < 8 else (own_w + (k - 8) * 128)
                    tp = T.transpose(psBb[:, (sj - jp) * 128:(sj - jp + 1) * 128],
                                     psb_sb[:, poff[sj] + col: poff[sj] + col + 128],
                                     eyebf[:, :])
                st["pt"][(jp, ki)] = s_pt.inc(tp)

            def _evac_block(hb, jp, ki):
                st = bst[hb]
                nsl = (NSLOT - jp) * 128
                s_pt.wait(V, st["pt"][(jp, ki)])
                if ki == 0:
                    s_av.wait(V, st["avp"][jp - 2] if jp >= 2 else st["av_base"])
                pe = V.tensor_copy(ptsb[jp % 2][:, ki * 1024: ki * 1024 + nsl],
                                   psBb[:, 0:nsl])
                st["evac"][(jp, ki)] = s_ptev.inc(pe)

            def beta_pair_transposes(hb, ihb, jp):
                st = bst[hb]
                if jp > 0:
                    s_ptev.wait(T, st["evac"][jp - 1])
                    _av_pair(hb, ihb, jp - 1, jp - 1 == 0, False)
                else:
                    s_ptev.wait(T, st["prev_ptev"])
                _tp_block(hb, jp, 0)

            def beta_pair_evacs(hb, jp):
                st = bst[hb]
                _evac_block(hb, jp, 0)
                s_ptev.wait(T, st["evac"][(jp, 0)])
                _tp_block(hb, jp, 1)
                st["pt"][jp] = st["pt"][(jp, 1)]
                _evac_block(hb, jp, 1)
                st["evac"][jp] = st["evac"][(jp, 1)]

            def beta_tail(hb, ihb):
                st = bst[hb]
                s_ptev.wait(T, st["evac"][7])
                _av_pair(hb, ihb, 7, False, True)
                s_ex.wait(V, v_ex_head[hb])
                rc = V.reciprocal(small[:, 48:56],
                                  small[:, 32 + 8 * (hb % 2):40 + 8 * (hb % 2)])
                s_rb.inc(rc)
                s_rb.wait(T)
                tpm = T.matmul(psD[0:8, 0:128], small[:, 48:56], eye32[:, :],
                               start=True, stop=True)
                s_rb.inc(tpm)
                s_rb.wait(SC)
                cp = SC.copy(rec8[:, :], psD[0:8, 0:128])
                s_rb.inc(cp)
                s_rb.wait(T)
                s_ptev.wait(T, st["evac"][7])
                for sj in range(NSLOT):
                    pd = psB if sj < 4 else psD
                    bc = T.matmul(pd[:, (sj % 4) * 128:(sj % 4 + 1) * 128],
                                  sel_sb[:, sj * 128:(sj + 1) * 128], rec8[:, :],
                                  start=True, stop=True)
                s_rb.inc(bc)
                s_rb.wait(SC)
                SC.copy(rsw[:, 0:512], psB[:, :])
                rc2 = SC.copy(rsw[:, 512:1024], psD[:, :])
                s_rb.inc(rc2)
                if debug and hb == 15:
                    sdt = sem("dbgt")
                    s_rb.wait(G)
                    sdt.inc(G.dma_start(dbg["tail"][:, 0:64], small[:, :]), 16)
                    sdt.inc(G.dma_start(dbg["tail"][0:8, 64:192], rec8[:, :]), 16)
                    sdt.inc(G.dma_start(dbg["tail"][:, 192:1216] if False else dbg["tail"][:, 128:1152], rsw[:, :]), 16)
                    sdt.wait(G)
                    sdt.wait(V)
                    sdt.wait(T)
                s_rb.wait(V)
                c1 = V.tensor_tensor(ctx[:, hb * TOK: hb * TOK + 512], psC[:, 0:512],
                                     rsw[:, 0:512], op=AOP.mult)
                c2 = V.tensor_tensor(ctx[:, hb * TOK + 512: hb * TOK + 1024], psC[:, 512:1024],
                                     rsw[:, 512:1024], op=AOP.mult)
                s_cx.inc(c1)
                v_cx_head[hb] = s_cx.inc(c2)

            pending = None
            for ih in range(HPP):
                h = p * HPP + ih
                if h > 1:
                    s_qp.wait(G, h - 1)
                s_wq.inc(G.dma_start(wqb[h % 2].rearrange("p (c k) -> p c k", c=NDC),
                                     wq_d[h, :, :, :].rearrange("c p k -> p c k")), 16)
                s_wq.wait(T)
                if ih == 0:
                    s_kro.wait(T, v_kro[p * HPP + HPP - 1])
                else:
                    s_ex.wait(T)
                for dc in range(NDC):
                    for grp in range(2):
                        mm = T.matmul(psA[:, grp * 512:(grp + 1) * 512],
                                      wqb[h % 2][:, dc * 128:(dc + 1) * 128],
                                      xn_own[:, dc * TOK + grp * 512: dc * TOK + (grp + 1) * 512],
                                      start=(dc == 0), stop=(dc == NDC - 1))
                s_qp.inc(mm)
                s_qp.wait(SC)
                gk = h - 1 if ih == 0 else h - 2
                if gk >= 0 and gk in v_cx_head:
                    s_cx.wait(SC, v_cx_head[gk])
                SC.copy(rsw[0:64, :], psA[64:128, 0:1024])
                hc = SC.copy(rsw[64:128, :], psA[0:64, 0:1024])
                s_kswc.inc(hc)
                s_qp.wait(V)
                s_kswc.wait(V)
                V.tensor_tensor(psA[:, 1024:2048], psA[:, 0:1024], cqt[:, :], op=AOP.mult)
                V.tensor_tensor(rsw[:, :], rsw[:, :], sqt[:, :], op=AOP.mult)
                rp = V.tensor_tensor(qfm[:, :], psA[:, 1024:2048], rsw[:, :], op=AOP.add)
                s_qro.inc(rp)

                if pending is not None:
                    beta_begin(pending)
                    beta_pair_transposes(pending, pending % HPP, 0)
                    beta_pair_evacs(pending, 0)
                s_qro.wait(T)
                for j in range(NSLOT):
                    own_w = (j + 1) * 128
                    ext = 2 * own_w
                    if j > 0:
                        s_ex.wait(T)
                    for w in range(2):
                        base_kv = w * 1024
                        col0 = w * own_w
                        for (c0, n) in _mm_chunks(col0, col0 + own_w):
                            off = c0 - col0
                            mm = T.matmul(psA[:, c0:c0 + n],
                                          qfm[:, j * 128:(j + 1) * 128],
                                          kfm[:, ih * S + base_kv + off: ih * S + base_kv + off + n],
                                          start=True, stop=True)
                    s_sm.inc(mm)
                    if pending is not None and j < NSLOT - 1:
                        beta_pair_transposes(pending, pending % HPP, j + 1)
                    s_sm.wait(V)
                    V.tensor_tensor(psA[:, j * 128:(j + 1) * 128], psA[:, j * 128:(j + 1) * 128],
                                    maskp[:, 0:128], op=AOP.add)
                    mk = V.tensor_tensor(psA[:, own_w + j * 128: own_w + (j + 1) * 128],
                                         psA[:, own_w + j * 128: own_w + (j + 1) * 128],
                                         maskp[:, 128:256], op=AOP.add)
                    s_mx.inc(mk)
                    if pending is not None and j < NSLOT - 1:
                        beta_pair_evacs(pending, j + 1)
                    s_mx.wait(SC)
                    if pending is not None:
                        s_pt.wait(SC, bst[pending]["pt"][(j, 1)])
                    elif h > 0 and j == 0:
                        s_pt.wait(SC, v_pt_head[h - 1])
                    ex = SC.activation(psb_sb[:, poff[j]:poff[j] + ext], psA[:, 0:ext], AF.Exp,
                                       accum_out=small[:, 32 + 8 * (h % 2) + j:33 + 8 * (h % 2) + j])
                    s_ex.inc(ex)
                v_msk_head[h] = s_msk.val
                v_ex_head[h] = s_ex.val
                if pending is not None:
                    beta_tail(pending, pending % HPP)
                    v_pt_head[pending] = s_pt.val
                    v_av_head[pending] = s_av.val

                if debug and h == 0:
                    sdq = sem("dbgq")
                    s_qro.wait(G)
                    sdq.inc(G.dma_start(dbg["qfm"][:, :], qfm[:, :]), 16)
                    s_msk.wait(G, v_msk_head[h])
                    sdq.inc(G.dma_start(dbg["psb"][:, :], psb_sb[:, :]), 16)
                    sdq.wait(G)
                    sdq.wait(SC)
                    sdq.wait(T)
                pending = h
            beta_begin(pending)
            for j in range(NSLOT):
                beta_pair_transposes(pending, pending % HPP, j)
                beta_pair_evacs(pending, j)
            beta_tail(pending, pending % HPP)
            v_pt_head[pending] = s_pt.val
            v_av_head[pending] = s_av.val
            pending = None
        s_cx.wait(V)
        s_cx.wait(T)
        s_cx.wait(SC)
    p1s.close()
    p01.close()
    barrier("b_p1")
    if debug:
        sd = sem("dbgc")
        sd.inc(G.dma_start(dbg["ctx"][:, :], ctx[:, :]), 16)
        sd.wait(G)
        sd.wait(SY)

    if stop_after == "p1":
        top.close()
        return nc
    p23 = ExitStack()
    acc = sb(p23, "acc", [128, NDC * TOK], F32)
    wos = sbN(p23, "wos", 2, NDC * 128, BF16)

    p2s = ExitStack()
    wup = sbN(p2s, "wup", 2, NDC * 128, BF16)
    wdn = sbN(p2s, "wdn", 2, ICPS * 128, BF16)
    wee = sbN(p2s, "wee", 2, NDC * 128, BF16)
    hsup = sb(p2s, "hsup", [128, ICPS * TOK], BF16)
    _ovlf = ovl.ap().bitcast(F32)
    ln2t = sbN(p2s, "ln2t", 2, TOK, F32) + [_ovlf[:, 2 * TOK:3 * TOK], _ovlf[:, 3 * TOK:4 * TOK]]
    ybf = [wos[0][:, 0:TOK], wos[0][:, TOK:2 * TOK]]
    y2bf = [ovl.ap()[:, 8 * TOK:9 * TOK], sb(p2s, "y2bf", [128, TOK], BF16)[:, :]]
    rtmp = [_ovlf[:, 0:TOK], _ovlf[:, TOK:2 * TOK]]
    if True:
        s_wup = sem("p2_wup")
        s_wdn = sem("p2_wdn")
        s_we = sem("p2_we")
        s_up = sem("p2_up")
        s_h = sem("p2_h")
        s_dn = sem("p2_dn")
        s_acc = sem("p2_acc")
        v_up = {}
        v_h = {}
        v_dn = {}
        v_acc = {}
        for sup in range(NSUP):
            for icw in range(ICPS):
                ic = sup * ICPS + icw
                bi = ic % 2
                if ic >= 2:
                    s_up.wait(G, v_up[ic - 2])
                s_wup.inc(G.dma_start(wup[bi].rearrange("p (c k) -> p c k", c=NDC),
                                      up_d[ic, :, :, :].rearrange("c p k -> p c k")), 16)
                s_wup.wait(T)
                if ic >= 2:
                    s_h.wait(T, v_h[ic - 2])
                if sup > 0 and icw < 2:
                    s_acc.wait(T, v_acc[(sup - 1, NDC - 1)])
                ph = psA[:, (icw % 2) * 1024:(icw % 2) * 1024 + 1024]
                for dc in range(NDC):
                    for grp in range(2):
                        mm = T.matmul(ph[:, grp * 512:(grp + 1) * 512],
                                      wup[bi][:, dc * 128:(dc + 1) * 128],
                                      xn_own[:, dc * TOK + grp * 512: dc * TOK + (grp + 1) * 512],
                                      start=(dc == 0), stop=(dc == NDC - 1))
                v_up[ic] = s_up.inc(mm)
                s_up.wait(SC)
                rl = SC.activation(rtmp[ic % 2][:, :], ph[:, :], AF.Relu)
                s_up.inc(rl)
                s_up.wait(V)
                if sup > 0 and icw == 0:
                    s_dn.wait(V, v_dn[(sup - 1, NDC - 1)])
                hv = V.tensor_tensor(hsup[:, icw * TOK:(icw + 1) * TOK], rtmp[ic % 2][:, :],
                                     rtmp[ic % 2][:, :], op=AOP.mult)
                v_h[ic] = s_h.inc(hv)
            s_h.wait(T)
            for dq in range(NDC):
                bi = dq % 2
                gidx = sup * NDC + dq
                if gidx >= 2:
                    s_dn.wait(G, v_dn[(sup, dq - 2)] if dq >= 2 else v_dn[(sup - 1, NDC + dq - 2)])
                s_wdn.inc(G.dma_start(wdn[bi].rearrange("p (c k) -> p c k", c=ICPS),
                                      dn_d[sup, dq, :, :, :].rearrange("c p k -> p c k")), 16)
                s_wdn.wait(T)
                pd = psC if dq % 2 == 0 else psA[:, 0:1024]
                if gidx >= 2:
                    prev = (sup, dq - 2) if dq >= 2 else (sup - 1, NDC + dq - 2)
                    s_acc.wait(T, v_acc[prev])
                for icw in range(ICPS):
                    for grp in range(2):
                        mm = T.matmul(pd[:, grp * 512:(grp + 1) * 512],
                                      wdn[bi][:, icw * 128:(icw + 1) * 128],
                                      hsup[:, icw * TOK + grp * 512: icw * TOK + (grp + 1) * 512],
                                      start=(icw == 0), stop=(icw == ICPS - 1))
                v_dn[(sup, dq)] = s_dn.inc(mm)
                s_dn.wait(V)
                if sup == 0:
                    av = V.tensor_copy(acc[:, dq * TOK:(dq + 1) * TOK], pd[:, :])
                else:
                    av = V.tensor_tensor(acc[:, dq * TOK:(dq + 1) * TOK],
                                         acc[:, dq * TOK:(dq + 1) * TOK], pd[:, :], op=AOP.add)
                v_acc[(sup, dq)] = s_acc.inc(av)
        s_g = sem("p2_gate")
        s_cbf.wait(T)
        s_acc.wait(T)
        v_g_m8 = {}
        for tc in range(NSLOT):
            if tc > 0:
                s_g.wait(T, v_g_m8[tc - 1])
            for dc in range(NDC):
                mm = T.matmul(psD[:, 0:8], xn_own[:, dc * TOK + tc * 128: dc * TOK + (tc + 1) * 128],
                              gate_sb[:, dc * 8: dc * 8 + 8],
                              start=(dc == 0), stop=(dc == NDC - 1))
            s_g.inc(mm)
            s_g.wait(V)
            m8 = V.max(small[:, 48:56], psD[:, 0:8])
            v_g_m8[tc] = s_g.inc(m8)
            s_g.wait(V)
            tt = V.tensor_tensor(small[:, 56 + tc:57 + tc], small[:, 48:49], small[:, 49:50],
                                 op=AOP.add)
            s_g.inc(tt)
        s_g.wait(T)
        tpm = T.matmul(psD[0:8, 0:128], small[:, 56:64], eye32[:, :], start=True, stop=True)
        s_g.inc(tpm)
        s_g.wait(SC)
        cp = SC.copy(rec8[:, :], psD[0:8, 0:128])
        s_g.inc(cp)
        s_g.wait(T)
        for tc in range(NSLOT):
            mm = T.matmul(psC[:, tc * 128:(tc + 1) * 128],
                          sel_sb[:, tc * 128:(tc + 1) * 128], rec8[:, :],
                          start=True, stop=True)
        s_g.inc(mm)
        if debug:
            s_g.wait(V)
            dv = V.tensor_copy(ln2t[0][:, :], psC[:, :])
            sdv = sem("dbgs")
            sdv.inc(dv)
            sdv.wait(SY)
            sdv.inc(SY.dma_start(dbg["sbc"][:, :], ln2t[0][:, :]), 16)
            sdv.wait(SY)
        s_mo = sem("p2_moemm")
        s_ms = sem("p2_moes")
        v_mo = {}
        v_ms = {}
        s_g.wait(V)
        sbc_cp = V.tensor_copy(ln2t[2][:, :], psC[:, :])
        s_g.inc(sbc_cp)
        s_g.wait(V)
        for dq in range(NDC):
            bi = dq % 2
            if dq >= 2:
                s_mo.wait(G, v_mo[dq - 2])
            s_we.inc(G.dma_start(wee[bi].rearrange("p (c k) -> p c k", c=NDC),
                                 e_d[dq, :, :, :].rearrange("c p k -> p c k")), 16)
            s_we.wait(T)
            pd = psA[:, (dq % 2) * 1024:(dq % 2) * 1024 + 1024]
            if dq >= 2:
                s_ms.wait(T, v_ms[dq - 2][0])
            for dc in range(NDC):
                for grp in range(2):
                    mm = T.matmul(pd[:, grp * 512:(grp + 1) * 512],
                                  wee[bi][:, dc * 128:(dc + 1) * 128],
                                  xn_own[:, dc * TOK + grp * 512: dc * TOK + (grp + 1) * 512],
                                  start=(dc == 0), stop=(dc == NDC - 1))
            v_mo[dq] = s_mo.inc(mm)
            s_mo.wait(V)
            tt = V.tensor_tensor(ln2t[3][:, :], pd[:, :], ln2t[2][:, :], op=AOP.mult)
            a1 = s_ms.inc(tt)
            ad = V.tensor_tensor(acc[:, dq * TOK:(dq + 1) * TOK],
                                 acc[:, dq * TOK:(dq + 1) * TOK], ln2t[3][:, :], op=AOP.add)
            v_ms[dq] = (a1, s_ms.inc(ad))
        s_st = sem("p2_cast")
        s_smm = sem("p2_statmm")
        s_ones.wait(T)
        s_ms.wait(T)
        v_cast = {}
        v_smm = {}
        for dq in range(NDC):
            bi = dq % 2
            s_ms.wait(V, v_ms[dq][1])
            if dq >= 2:
                s_smm.wait(V, v_smm[dq - 2])
            cv = V.tensor_copy(ybf[bi][:, :], acc[:, dq * TOK:(dq + 1) * TOK])
            s_st.inc(cv)
            s_st.wait(SC)
            if dq >= 2:
                s_smm.wait(SC, v_smm[dq - 2])
            sq = SC.activation(y2bf[bi][:, :], ybf[bi][:, :], AF.Square)
            v_cast[dq] = s_st.inc(sq)
            s_st.wait(T)
            for grp in range(2):
                T.matmul(psA[:, grp * 512:512 + grp * 512], onesbf[:, :],
                         ybf[bi][:, grp * 512:(grp + 1) * 512],
                         start=(dq == 0), stop=(dq == NDC - 1))
                mm = T.matmul(psA[:, 1024 + grp * 512:1536 + grp * 512], onesbf[:, :],
                              y2bf[bi][:, grp * 512:(grp + 1) * 512],
                              start=(dq == 0), stop=(dq == NDC - 1))
            v_smm[dq] = s_smm.inc(mm)
        s_smm.wait(SC)
        SC.mul(ln2t[0][:, :], psA[:, 0:1024], 1.0 / D)
        m2 = SC.mul(ln2t[1][:, :], psA[:, 1024:2048], 1.0 / D)
        s_stm = sem("p2_stm")
        s_stm.inc(m2)
        s_stm.wait(V)
        mu2 = V.tensor_tensor(ln2t[2][:, :], ln2t[0][:, :], ln2t[0][:, :], op=AOP.mult)
        s_stm.inc(mu2)
        s_stm.wait(V)
        vv = V.scalar_tensor_tensor(ln2t[1][:, :], ln2t[1][:, :], EPS, ln2t[2][:, :],
                                    op0=AOP.add, op1=AOP.subtract)
        s_stm.inc(vv)
        s_stm.wait(SC)
        SC.activation(ln2t[2][:, :], ln2t[1][:, :], AF.Ln)
        iv = SC.activation(ln2t[1][:, :], ln2t[2][:, :], AF.Exp, scale=-0.5)
        s_stm.inc(iv)
        s_stm.wait(V)
        ng = V.tensor_scalar(ln2t[0][:, :], ln2t[0][:, :], -1.0, None, op0=AOP.mult)
        s_stm.inc(ng)
        s_ap2 = sem("p2_apply")
        v_ap2 = {}
        for dq in range(NDC):
            V.tensor_tensor(ln2t[3][:, :], acc[:, dq * TOK:(dq + 1) * TOK], ln2t[0][:, :], op=AOP.add)
            V.tensor_tensor(ln2t[3][:, :], ln2t[3][:, :], ln2t[1][:, :], op=AOP.mult)
            ap = V.tensor_scalar(acc[:, dq * TOK:(dq + 1) * TOK], ln2t[3][:, :],
                                 gb_sb[:, dq:dq + 1], gb_sb[:, 16 + dq:17 + dq],
                                 op0=AOP.mult, op1=AOP.add)
            v_ap2[dq] = s_ap2.inc(ap)
        v_stats_done = (s_stm.val, s_g.val)
        s_ap2.wait(SC)
    p2s.close()
    if debug:
        sd = sem("dbgf")
        sems["p2_apply"].wait(SY)
        sd.inc(SY.dma_start(dbg["ffn"][:, :], acc[:, :]), 16)
        sd.wait(SY)
        sd.wait(V)
        sd.wait(T)

    if stop_after == "p2":
        p23.close()
        top.close()
        return nc
    p3s = ExitStack()
    x2 = sbN(p3s, "x2", 2, TOK, F32)
    if True:
        s_wo = sem("p3_wo")
        s_womm = sem("p3_womm")
        s_wadd = sem("p3_wadd")
        s_xl = sem("p3_xl")
        s_xt = sem("p3_xt")
        s_xadd = sem("p3_xadd")
        s_ot = sem("p3_ot")
        s_oev = sem("p3_oev")
        s_ost = sem("p3_ost")
        v_womm = {}
        v_wadd = {}
        v_xl = {}
        v_xadd = {}
        v_oev = {}
        v_ost = {}
        sems["p2_stm"].wait(T, v_stats_done[0])
        sems["p2_gate"].wait(T, v_stats_done[1])
        for dq in range(NDC):
            bi = dq % 2
            if dq < 2:
                s_smm.wait(G, v_smm[NDC - 1])
            else:
                s_womm.wait(G, v_womm[dq - 2])
            s_wo.inc(G.dma_start(wos[bi].rearrange("p (c k) -> p c k", c=NDC),
                                 wo_d[dq, :, :, :].rearrange("c p k -> p c k")), 16)
            s_wo.wait(T)
            pd = psC if dq % 2 == 0 else psA[:, 0:1024]
            if dq >= 2:
                s_wadd.wait(T, v_wadd[dq - 2])
            for fc in range(NDC):
                for grp in range(2):
                    mm = T.matmul(pd[:, grp * 512:(grp + 1) * 512],
                                  wos[bi][:, fc * 128:(fc + 1) * 128],
                                  ctx[:, fc * TOK + grp * 512: fc * TOK + (grp + 1) * 512],
                                  start=(fc == 0), stop=(fc == NDC - 1))
            v_womm[dq] = s_womm.inc(mm)
            s_womm.wait(V)
            sems["p2_apply"].wait(V, v_ap2[dq])
            ad = V.tensor_tensor(acc[:, dq * TOK:(dq + 1) * TOK],
                                 acc[:, dq * TOK:(dq + 1) * TOK], pd[:, :], op=AOP.add)
            v_wadd[dq] = s_wadd.inc(ad)
        for dc in range(NDC):
            bi = dc % 2
            if dc == 0:
                sems["p2_apply"].wait(SY)
            if dc >= 2:
                s_xadd.wait(SY, v_xadd[dc - 2])
            v_xl[dc] = s_xl.inc(SY.dma_start(x2[bi][:, 0:TOK],
                                             xt_in[dc * 128:(dc + 1) * 128, :]), 16)
            s_xl.wait(V, v_xl[dc])
            s_wadd.wait(V, v_wadd[dc])
            ad = V.tensor_tensor(acc[:, dc * TOK:(dc + 1) * TOK],
                                 acc[:, dc * TOK:(dc + 1) * TOK], x2[bi][:, 0:TOK], op=AOP.add)
            v_xadd[dc] = s_xadd.inc(ad)
            s_xadd.wait(SY, v_xadd[dc])
            v_ost[dc] = s_ost.inc(
                SY.dma_start(out_d[:, dc * TOK:(dc + 1) * TOK], acc[:, dc * TOK:(dc + 1) * TOK]), 16)
        s_ost.wait(SY)
        s_ost.wait(T)
        s_ost.wait(V)
        s_ost.wait(SC)
        s_ost.wait(G)
    p3s.close()
    p23.close()
    top.close()
    return nc



def _prep_shared(wq, wk, wv, wo, up_w, down_w, ew, gate_w, ln_g, ln_b):
    f32 = np.float32
    perm = np.concatenate([np.arange(0, 128, 2), np.arange(1, 128, 2)])
    wq_p = wq.reshape(H, DK, D)[:, perm, :]
    wk_p = wk.reshape(H, DK, D)[:, perm, :]
    wq_pre = np.ascontiguousarray(wq_p.transpose(0, 2, 1).reshape(H, NDC, 128, 128), dtype=f32)
    wk_pre = np.ascontiguousarray(wk_p.transpose(0, 2, 1).reshape(H, NDC, 128, 128), dtype=f32)
    wv_pre = np.ascontiguousarray(
        wv.T.reshape(NDC, 128, NPASS, HPP * 128).transpose(2, 0, 1, 3), dtype=f32)
    wo_pre = np.ascontiguousarray(
        wo.T.reshape(NDC, 128, NDC, 128).transpose(2, 0, 1, 3), dtype=f32)
    up_pre = np.ascontiguousarray(
        up_w.reshape(NDC, 128, NIC, 128).transpose(2, 0, 1, 3), dtype=f32)
    dn_pre = np.ascontiguousarray(
        down_w.reshape(NSUP, ICPS, 128, NDC, 128).transpose(0, 3, 1, 2, 4), dtype=f32)
    e_pre = np.ascontiguousarray(
        ew.reshape(NDC, 128, NDC, 128).transpose(2, 0, 1, 3), dtype=f32)
    gate_pre = np.ascontiguousarray(
        gate_w.reshape(NDC, 128, NEXP).transpose(1, 0, 2).reshape(128, 128), dtype=f32)
    gb_pre = np.ascontiguousarray(
        np.concatenate([ln_g.reshape(NDC, 128).T, ln_b.reshape(NDC, 128).T], axis=1), dtype=f32)
    return dict(wq=wq_pre, wk=wk_pre, wv=wv_pre, wo=wo_pre, up=up_pre, dn=dn_pre,
                ew=e_pre, gate=gate_pre, gb=gb_pre,
                eye=np.eye(128, dtype=f32), ones1=np.ones((1, 128), dtype=f32),
                sel=np.ascontiguousarray(np.repeat(np.eye(8, dtype=f32), 128, axis=1)))


def _prep_core(x_b, par):
    f32 = np.float32
    xb = np.asarray(x_b).reshape(16, 128, D)
    order = []
    for j in range(8):
        order += [2 * j + par, 2 * j + (1 - par)]
    x_sw = np.ascontiguousarray(xb[order].reshape(S, D), dtype=f32)
    x_t = np.ascontiguousarray(x_sw.reshape(16, 128, D)[0::2].reshape(TOK, D).T, dtype=f32)

    inv = THETA ** (-(np.arange(0, DK, 2, dtype=np.float64) / DK))
    inv2 = np.concatenate([inv, inv])[:, None]
    own_pos = np.concatenate([np.arange(128) + (2 * j + par) * 128 for j in range(8)])
    oth_pos = np.concatenate([np.arange(128) + (2 * j + (1 - par)) * 128 for j in range(8)])
    sig_pos = np.concatenate([own_pos, oth_pos]).astype(np.float64)
    ang_k = sig_pos[None, :] * inv2
    ckt = np.cos(ang_k).astype(f32)
    skt = np.sin(ang_k).astype(f32)
    skt[0:64] *= -1.0
    ang_q = own_pos.astype(np.float64)[None, :] * inv2
    scale = 1.0 / np.sqrt(DK)
    cqt = (np.cos(ang_q) * scale).astype(f32)
    sqt = (np.sin(ang_q) * scale).astype(f32)
    sqt[0:64] *= -1.0
    tri = np.where(np.arange(128)[None, :] > np.arange(128)[:, None], NEG, 0.0).astype(f32)
    ptile = np.full((128, 128), NEG if par == 0 else 0.0, dtype=f32)
    maskp = np.ascontiguousarray(np.concatenate([tri, ptile], axis=1))
    return dict(x=x_sw, xT=x_t, ckt=ckt, skt=skt, cqt=cqt, sqt=sqt, maskp=maskp)


_NC_CACHE = {}


def kernel(L0_attn_x, L0_attn_wq, L0_attn_wk, L0_attn_wv, L0_attn_wo,
           L0_ffn_gate_w, L0_ffn_up_w, L0_ffn_down_w, L0_ffn_expert_weight,
           L0_ffn_ln_gamma, L0_ffn_ln_beta):
    f32 = np.float32
    x = np.asarray(L0_attn_x, dtype=f32)
    shared = _prep_shared(np.asarray(L0_attn_wq, f32), np.asarray(L0_attn_wk, f32),
                          np.asarray(L0_attn_wv, f32), np.asarray(L0_attn_wo, f32),
                          np.asarray(L0_ffn_up_w, f32), np.asarray(L0_ffn_down_w, f32),
                          np.asarray(L0_ffn_expert_weight, f32),
                          np.asarray(L0_ffn_gate_w, f32),
                          np.asarray(L0_ffn_ln_gamma, f32), np.asarray(L0_ffn_ln_beta, f32))
    if "nc" not in _NC_CACHE:
        _NC_CACHE["nc"] = build_program(debug=False)
    nc = _NC_CACHE["nc"]
    in_maps = []
    for core in range(NCORE):
        b, par = core // 2, core % 2
        m = dict(shared)
        m.update(_prep_core(x[b], par))
        in_maps.append(m)
    res = run_bass_kernel_spmd(nc, in_maps, core_ids=list(range(NCORE)))
    out = np.empty((B, S, D), dtype=f32)
    for core in range(NCORE):
        b, par = core // 2, core % 2
        o = np.asarray(res.results[core]["out"]).reshape(128, NDC, TOK).transpose(2, 1, 0)
        o = o.reshape(NSLOT, 128, D)
        for j in range(NSLOT):
            out[b, (2 * j + par) * 128:(2 * j + par + 1) * 128, :] = o[j]
    return out


# version 12
# speedup vs baseline: 1.1481x; 1.0206x over previous
import numpy as np
import concourse.bass as bass
import concourse.mybir as mybir
from concourse.bass_utils import run_bass_kernel_spmd
from contextlib import ExitStack

F32 = mybir.dt.float32
BF16 = mybir.dt.bfloat16
AOP = mybir.AluOpType
AF = mybir.ActivationFunctionType
AX = mybir.AxisListType

B, S, D = 4, 2048, 2048
H, DK = 16, 128
INTER = 8192
NEXP = 8
THETA = 10000.0
EPS = 1e-5
NCORE = 8
NSLOT = 8
TOK = NSLOT * 128
NDC = D // 128
NIC = INTER // 128
NSUP = 8
ICPS = NIC // NSUP
NEG = -1.0e30
NPASS = 4
HPP = H // NPASS


class Sem:
    def __init__(self, nc, name):
        self._cm = nc.semaphore(name)
        self.h = self._cm.__enter__()
        self.val = 0

    def inc(self, instr, n=1):
        instr.then_inc(self.h, n)
        self.val += n
        return self.val

    def wait(self, eng, val=None):
        eng.wait_ge(self.h, self.val if val is None else val)


def _mm_chunks(lo, hi):
    out = []
    c = lo
    while c < hi:
        n = min(hi, (c // 512 + 1) * 512) - c
        out.append((c, n))
        c += n
    return out


def build_program(debug=False, stop_after=None):
    nc = bass.Bass("TRN2", target_bir_lowering=False)
    top = ExitStack()

    dt = lambda n, shp, dty=F32, kind="ExternalInput": nc.dram_tensor(n, shp, dty, kind=kind).ap()
    x_in = dt("x", [S, D])
    xt_in = dt("xT", [D, TOK])
    wq_d = dt("wq", [H, NDC, 128, 128])
    wk_d = dt("wk", [H, NDC, 128, 128])
    wv_d = dt("wv", [NPASS, NDC, 128, HPP * 128])
    wo_d = dt("wo", [NDC, NDC, 128, 128])
    up_d = dt("up", [NIC, NDC, 128, 128])
    dn_d = dt("dn", [NSUP, NDC, ICPS, 128, 128])
    e_d = dt("ew", [NDC, NDC, 128, 128])
    gate_d = dt("gate", [128, 128])
    gb_d = dt("gb", [128, 32])
    ckt_d = dt("ckt", [128, S])
    skt_d = dt("skt", [128, S])
    cqt_d = dt("cqt", [128, TOK])
    sqt_d = dt("sqt", [128, TOK])
    mask_d = dt("maskp", [128, 256])
    eye_d = dt("eye", [128, 128])
    ones1_d = dt("ones1", [1, 128])
    sel_d = dt("sel", [8, 1024])
    out_d = dt("out", [128, NDC * TOK], F32, kind="ExternalOutput")
    dbg = {}
    if debug:
        dbg["xn_own"] = dt("dbg_xn_own", [128, NDC * TOK], F32, kind="ExternalOutput")
        dbg["kfm"] = dt("dbg_kfm", [128, S], F32, kind="ExternalOutput")
        dbg["vtm"] = dt("dbg_vtm", [128, 16 * HPP * 128], F32, kind="ExternalOutput")
        dbg["ctx"] = dt("dbg_ctx", [128, H * TOK], F32, kind="ExternalOutput")
        dbg["ffn"] = dt("dbg_ffn", [128, NDC * TOK], F32, kind="ExternalOutput")
        dbg["sbc"] = dt("dbg_sbc", [128, TOK], F32, kind="ExternalOutput")
        dbg["qfm"] = dt("dbg_qfm", [128, TOK], F32, kind="ExternalOutput")
        dbg["psb"] = dt("dbg_psb", [128, 9216], F32, kind="ExternalOutput")
        dbg["tail"] = dt("dbg_tail", [128, 1152], F32, kind="ExternalOutput")

    sb = lambda st_, n, shp, dty: st_.enter_context(nc.sbuf_tensor(n, shp, dty))

    def sbN(st_, name, n, w, dty):
        t = st_.enter_context(nc.sbuf_tensor(name, [128, n * w], dty))
        return [t[:, i * w:(i + 1) * w] for i in range(n)]
    xn_own = sb(top, "xn_own", [128, NDC * TOK], BF16)
    ctx = sb(top, "ctx", [128, H * TOK], BF16)
    eye32 = sb(top, "eye32", [128, 128], F32)
    eyebf = sb(top, "eyebf", [128, 128], BF16)
    onesbf = sb(top, "onesbf", [128, 128], BF16)
    ones1 = sb(top, "ones1_s", [1, 128], F32)
    gate_sb = sb(top, "gate_sb", [128, 128], BF16)
    gb_sb = sb(top, "gb_sb", [128, 32], F32)
    small = sb(top, "small", [128, 64], F32)
    ovl = sb(top, "ovl", [128, 9216], BF16)
    sel_sb = sb(top, "sel_sb", [8, 1024], F32)
    rec8 = sb(top, "rec8", [8, 128], F32)

    psA = top.enter_context(nc.psum_tensor("psA", [128, 2048], F32))
    psC = top.enter_context(nc.psum_tensor("psC", [128, 1024], F32))
    psE = top.enter_context(nc.psum_tensor("psE", [128, 1024], F32))
    psB = psE.ap()[:, 0:512]
    psD = psE.ap()[:, 512:1024]
    psBb = psE.ap().bitcast(BF16)[:, 0:1024]
    psAb = psA.ap().bitcast(BF16)

    T, V, SC, G, SY = nc.tensor, nc.vector, nc.scalar, nc.gpsimd, nc.sync
    ALL_ENG = (T, V, SC, G, SY)

    sems = {}

    def sem(name):
        if name not in sems:
            sems[name] = Sem(nc, name)
        return sems[name]

    def barrier(name):
        s = sem(name)
        base = s.val
        for eng in ALL_ENG:
            eng.sem_inc(s.h, 1)
        s.val = base + 5
        for eng in ALL_ENG:
            s.wait(eng)

    s_const = sem("const")
    for dst, src in ((eye32, eye_d), (gb_sb, gb_d)):
        s_const.inc(SY.dma_start(dst[:, :], src[:, :]), 16)
    s_const.inc(SY.dma_start(ones1[:, :], ones1_d[:, :]), 16)
    s_const.inc(SY.dma_start(sel_sb[:, :], sel_d[:, :]), 16)
    s_cbf = sem("cbf")
    s_cbf.inc(G.dma_start(eyebf[:, :], eye_d[:, :]), 16)
    s_cbf.inc(G.dma_start(gate_sb[:, :], gate_d[:, :]), 16)
    s_ones = sem("onesw")
    s_ones.inc(V.memset(onesbf[:, :], 1.0))

    p01 = ExitStack()
    xn_oth = sb(p01, "xn_oth", [128, NDC * TOK], BF16)
    ckt = sb(p01, "ckt_s", [128, S], F32)
    skt = sb(p01, "skt_s", [128, S], F32)
    cqt = sb(p01, "cqt_s", [128, TOK], F32)
    sqt = sb(p01, "sqt_s", [128, TOK], F32)
    maskp = sb(p01, "maskp_s", [128, 256], F32)
    for dst, src in ((ckt, ckt_d), (skt, skt_d), (cqt, cqt_d), (sqt, sqt_d), (maskp, mask_d)):
        s_const.inc(SY.dma_start(dst[:, :], src[:, :]), 16)
    s_const.wait(V)
    s_const.wait(T)
    s_const.wait(SC)
    s_wv0 = sem("p1_wv")
    wvb_pre = ovl.ap()[:, 0:NDC * HPP * 128]
    s_wv0.inc(G.dma_start(wvb_pre.rearrange("p (c v) -> p c v", c=NDC),
                          wv_d[0, :, :, :].rearrange("c p v -> p c v")), 16)

    poff = [0]
    for j in range(NSLOT):
        poff.append(poff[-1] + (2 * j + 2) * 128)
    PSB = poff[-1]

    p1s = ExitStack()
    kfm = sb(p1s, "kfm", [128, HPP * S], BF16)
    vtm = sb(p1s, "vtm", [128, 16 * HPP * 128], BF16)
    wkb = sbN(p1s, "wkb", 2, NDC * 128, BF16)
    wqb = sbN(p1s, "wqb", 2, NDC * 128, BF16)
    qfm = sb(p1s, "qfm", [128, TOK], BF16)
    ptsb_t = sb(p1s, "ptsb", [128, 2 * 2048], BF16)
    ptsb = [ptsb_t[:, 0:2048], ptsb_t[:, 2048:4096]]
    rsw = ptsb_t.ap().bitcast(F32)[:, 0:1024]
    psb_sb = ovl
    wvb = ovl.ap()[:, 0:NDC * HPP * 128]

    _kfmf = kfm.ap().bitcast(F32)
    x_tm = [_kfmf[:, 0:D], _kfmf[:, D:2 * D]]
    xn_tm = [ptsb_t[:, 0:D], ptsb_t[:, D:2 * D]]
    if True:
        s_x = sem("p0_x")
        s_ap = sem("p0_ap")
        s_iv = sem("p0_iv")
        s_nm = sem("p0_nm")
        s_tp = sem("p0_tp")
        s_ev = sem("p0_ev")
        v_ap = {}
        v_tp = {}
        v_ev = {}
        v_vp0 = {}
        v_vev0 = {}
        for gc in range(NDC):
            bi = gc % 2
            if gc >= 2:
                s_ap.wait(SY, v_ap[gc - 2])
            s_x.inc(SY.dma_start(x_tm[bi][:, :], x_in[gc * 128:(gc + 1) * 128, :]), 16)
            s_x.wait(V)
            for q in range(4):
                st4 = V.bn_stats(small[:, 6 * q:6 * q + 6], x_tm[bi][:, 512 * q:512 * (q + 1)])
            s_nm.inc(st4)
            s_nm.wait(V)
            ag4 = V.bn_aggr(small[:, 24:26], small[:, 0:24])
            s_nm.inc(ag4)
            s_nm.wait(V)
            V.tensor_scalar(small[:, 26:27], small[:, 24:25], -1.0, None, op0=AOP.mult)
            nm = V.tensor_scalar(small[:, 27:28], small[:, 25:26], EPS, None, op0=AOP.add)
            s_nm.inc(nm)
            s_nm.wait(SC)
            SC.activation(small[:, 28:29], small[:, 27:28], AF.Ln)
            iv = SC.activation(small[:, 29:30], small[:, 28:29], AF.Exp, scale=-0.5)
            s_iv.inc(iv)
            s_iv.wait(V)
            s_nm.wait(V)
            if gc >= 2:
                s_tp.wait(V, v_tp[gc - 2])
            ap = V.tensor_scalar(xn_tm[bi][:, :], x_tm[bi][:, :], small[:, 26:27],
                                 small[:, 29:30], op0=AOP.add, op1=AOP.mult)
            v_ap[gc] = s_ap.inc(ap)
            s_ap.wait(T)
            if gc >= 1:
                s_ev.wait(T, 2 * gc)
            tgt = xn_own if bi == 0 else xn_oth
            j = gc // 2
            psCb = psC.ap().bitcast(BF16)
            for half in range(2):
                pp = psCb[:, 0:1024] if half == 0 else psCb[:, 1024:2048]
                for k in range(8):
                    dc = half * 8 + k
                    tp = T.transpose(pp[:, k * 128:(k + 1) * 128],
                                     xn_tm[bi][:, dc * 128:(dc + 1) * 128], eyebf[:, :])
                s_tp.inc(tp)
                v_tp[gc] = s_tp.val
                s_tp.wait(V)
                dview = tgt.ap().rearrange("p (c n) -> p c n", n=TOK)[:, half * 8:half * 8 + 8,
                                                                     j * 128:(j + 1) * 128]
                sview = pp.rearrange("p (c n) -> p c n", n=128)
                ev = V.tensor_copy(dview, sview)
                s_ev.inc(ev)
            v_ev[gc] = s_ev.val
            gb = gc // 2 if gc % 2 == 0 else 8 + gc // 2
            s_vp0 = sem("p1_vp")
            s_vev0 = sem("p1_vev")
            vsrc = xn_own if gb < 8 else xn_oth
            jj = gb % 8
            pv = psB if gc % 2 == 0 else psD
            sem("p1_wv").wait(T)
            s_ev.wait(T, v_ev[gc])
            if gc >= 2:
                s_vev0.wait(T, v_vev0[gc - 2])
            for dc in range(NDC):
                mm = T.matmul(pv[:, 0:HPP * 128],
                              vsrc[:, dc * TOK + jj * 128: dc * TOK + (jj + 1) * 128],
                              wvb[:, dc * HPP * 128:(dc + 1) * HPP * 128],
                              start=(dc == 0), stop=(dc == NDC - 1))
            v_vp0[gc] = s_vp0.inc(mm)
            s_vp0.wait(SC)
            ev2 = SC.copy(vtm[:, gb * HPP * 128:(gb + 1) * HPP * 128], pv[:, 0:HPP * 128])
            v_vev0[gc] = s_vev0.inc(ev2)
        s_ev.wait(V)
    if debug:
        sd = sem("dbg0")
        s_ev.wait(G)
        sd.inc(G.dma_start(dbg["xn_own"][:, :], xn_own[:, :]), 16)
        sd.wait(G)
        sd.wait(SY)

    if stop_after == "p0":
        p1s.close()
        p01.close()
        top.close()
        return nc
    if True:

        s_wv = sem("p1_wv")
        s_wk = sem("p1_wk")
        s_wq = sem("p1_wq")
        s_vp = sem("p1_vp")
        s_vev = sem("p1_vev")
        s_kp = sem("p1_kp")
        s_kro = sem("p1_kro")
        s_kswc = sem("p1_kswc")
        s_kfree = sem("p1_kfree")
        v_kswc = {}
        v_kfree = {}
        s_qp = sem("p1_qp")
        s_qro = sem("p1_qro")
        s_sm = sem("p1_sm")
        s_mx = sem("p1_mx")
        s_ex = sem("p1_ex")
        s_msk = sem("p1_msk")
        v_msk_head = {}
        s_pt = sem("p1_pt")
        s_ptev = sem("p1_ptev")
        s_av = sem("p1_av")
        s_rb = sem("p1_rb")
        s_cx = sem("p1_cx")

        v_vev = {}
        v_vp = {}
        v_kp = {}
        v_kro = {}
        v_pt_head = {}
        v_ex_head = {}
        v_cx_head = {}
        v_av_head = {}

        for p in range(NPASS):
            if p > 0:
                s_pt.wait(G, v_pt_head[p * HPP - 1])
                s_wv.inc(G.dma_start(wvb.rearrange("p (c v) -> p c v", c=NDC),
                                     wv_d[p, :, :, :].rearrange("c p v -> p c v")), 16)
            s_wv.wait(T)
            if p > 0:
                s_cx.wait(T, v_cx_head[p * HPP - 1])
            gb_order = [b for pair in zip(range(8), range(8, 16)) for b in pair]
            for oi, gb in enumerate(gb_order):
                if p == 0:
                    break
                src = xn_own if gb < 8 else xn_oth
                j = gb % 8
                pv = psB if oi % 2 == 0 else psD
                if oi >= 2:
                    s_vev.wait(T, v_vev[(p, gb_order[oi - 2])])
                for dc in range(NDC):
                    mm = T.matmul(pv[:, 0:HPP * 128],
                                  src[:, dc * TOK + j * 128: dc * TOK + (j + 1) * 128],
                                  wvb[:, dc * HPP * 128:(dc + 1) * HPP * 128],
                                  start=(dc == 0), stop=(dc == NDC - 1))
                v_vp[(p, gb)] = s_vp.inc(mm)
                s_vp.wait(SC)
                if oi == 0:
                    s_av.wait(SC, v_av_head[p * HPP - 1])
                ev = SC.copy(vtm[:, gb * HPP * 128:(gb + 1) * HPP * 128], pv[:, 0:HPP * 128])
                v_vev[(p, gb)] = s_vev.inc(ev)
            for ih in range(HPP):
                h = p * HPP + ih
                bi = ih % 2
                if ih >= 2:
                    s_kp.wait(G, v_kp[h - 2])
                s_wk.inc(G.dma_start(wkb[bi].rearrange("p (c k) -> p c k", c=NDC),
                                     wk_d[h, :, :, :].rearrange("c p k -> p c k")), 16)
                s_wk.wait(T)
                if p > 0 and ih == 0:
                    s_cx.wait(SC, v_cx_head[p * HPP - 1])
                for hf in range(2):
                    if ih == 0:
                        if hf == 0:
                            s_ex.wait(T)
                            if p == 0:
                                s_ev.wait(T, v_ev[NDC - 1])
                    else:
                        s_kswc.wait(T, v_kswc[(h - 1, hf)])
                        s_kfree.wait(T, v_kfree[(h - 1, hf)])
                    src = xn_own if hf == 0 else xn_oth
                    for dc in range(NDC):
                        for grp in range(2):
                            mm = T.matmul(psA[:, hf * 1024 + grp * 512: hf * 1024 + (grp + 1) * 512],
                                          wkb[bi][:, dc * 128:(dc + 1) * 128],
                                          src[:, dc * TOK + grp * 512: dc * TOK + (grp + 1) * 512],
                                          start=(dc == 0), stop=(dc == NDC - 1))
                    v_kp[(h, hf)] = s_kp.inc(mm)
                v_kp[h] = v_kp[(h, 1)]
                for hf in range(2):
                    s_kp.wait(SC, v_kp[(h, hf)])
                    if hf == 1:
                        s_kro.wait(SC)
                    SC.copy(rsw[0:64, :], psA[64:128, hf * 1024:(hf + 1) * 1024])
                    hc = SC.copy(rsw[64:128, :], psA[0:64, hf * 1024:(hf + 1) * 1024])
                    v_kswc[(h, hf)] = s_kswc.inc(hc)
                    s_kp.wait(V, v_kp[(h, hf)])
                    s_kswc.wait(V)
                    mf = V.tensor_tensor(psC[:, :], psA[:, hf * 1024:(hf + 1) * 1024],
                                         ckt[:, hf * 1024:(hf + 1) * 1024], op=AOP.mult)
                    v_kfree[(h, hf)] = s_kfree.inc(mf)
                    V.tensor_tensor(rsw[:, :], rsw[:, :],
                                    skt[:, hf * 1024:(hf + 1) * 1024], op=AOP.mult)
                    rp = V.tensor_tensor(kfm[:, ih * S + hf * 1024: ih * S + (hf + 1) * 1024],
                                         psC[:, :], rsw[:, :], op=AOP.add)
                    s_kro.inc(rp)
                v_kro[h] = s_kro.val
            if debug and p == 0:
                sdk = sem("dbgk")
                s_kro.wait(G, v_kro[HPP - 1])
                s_vev.wait(G)
                sdk.inc(G.dma_start(dbg["kfm"][:, :], kfm[:, 0:S]), 16)
                sdk.inc(G.dma_start(dbg["vtm"][:, 0:2048], vtm[:, 0:2048]), 16)
                sdk.wait(G)

            bst = {}

            def beta_begin(hb):
                bst[hb] = dict(av_base=s_av.val, prev_ptev=s_ptev.val,
                               evac={}, pt={}, avp={})
                if hb > 0 and (hb - 1) in v_cx_head:
                    s_cx.wait(T, v_cx_head[hb - 1])
                s_msk.wait(T, v_msk_head[hb])

            def _av_pair(hb, ihb, jp, is_first, is_last):
                st = bst[hb]
                q0 = jp * 128
                for ki, k in enumerate((jp, 8 + jp)):
                    pt_src = ptsb[jp % 2]
                    off = ki * 1024
                    for (c0, n) in _mm_chunks(q0, 1024):
                        mm = T.matmul(psC[:, c0:c0 + n],
                                      vtm[:, k * HPP * 128 + ihb * 128: k * HPP * 128 + (ihb + 1) * 128],
                                      pt_src[:, off + c0 - q0: off + c0 - q0 + n],
                                      start=(is_first and ki == 0), stop=(is_last and ki == 1))
                st["avp"][jp] = s_av.inc(mm)

            def _tp_block(hb, jp, ki):
                st = bst[hb]
                k = jp if ki == 0 else 8 + jp
                for sj in range(jp, NSLOT):
                    own_w = (sj + 1) * 128
                    col = (k * 128) if k < 8 else (own_w + (k - 8) * 128)
                    tp = T.transpose(psBb[:, (sj - jp) * 128:(sj - jp + 1) * 128],
                                     psb_sb[:, poff[sj] + col: poff[sj] + col + 128],
                                     eyebf[:, :])
                st["pt"][(jp, ki)] = s_pt.inc(tp)

            def _evac_block(hb, jp, ki):
                st = bst[hb]
                nsl = (NSLOT - jp) * 128
                s_pt.wait(V, st["pt"][(jp, ki)])
                if ki == 0:
                    s_av.wait(V, st["avp"][jp - 2] if jp >= 2 else st["av_base"])
                pe = V.tensor_copy(ptsb[jp % 2][:, ki * 1024: ki * 1024 + nsl],
                                   psBb[:, 0:nsl])
                st["evac"][(jp, ki)] = s_ptev.inc(pe)

            def beta_pair_transposes(hb, ihb, jp):
                st = bst[hb]
                if jp > 0:
                    s_ptev.wait(T, st["evac"][jp - 1])
                    _av_pair(hb, ihb, jp - 1, jp - 1 == 0, False)
                else:
                    s_ptev.wait(T, st["prev_ptev"])
                _tp_block(hb, jp, 0)

            def beta_pair_evacs(hb, jp):
                st = bst[hb]
                _evac_block(hb, jp, 0)
                s_ptev.wait(T, st["evac"][(jp, 0)])
                _tp_block(hb, jp, 1)
                st["pt"][jp] = st["pt"][(jp, 1)]
                _evac_block(hb, jp, 1)
                st["evac"][jp] = st["evac"][(jp, 1)]

            def beta_tail(hb, ihb):
                st = bst[hb]
                s_ptev.wait(T, st["evac"][7])
                _av_pair(hb, ihb, 7, False, True)
                s_ex.wait(V, v_ex_head[hb])
                rc = V.reciprocal(small[:, 48:56],
                                  small[:, 32 + 8 * (hb % 2):40 + 8 * (hb % 2)])
                s_rb.inc(rc)
                s_rb.wait(T)
                tpm = T.matmul(psD[0:8, 0:128], small[:, 48:56], eye32[:, :],
                               start=True, stop=True)
                s_rb.inc(tpm)
                s_rb.wait(SC)
                cp = SC.copy(rec8[:, :], psD[0:8, 0:128])
                s_rb.inc(cp)
                s_rb.wait(T)
                s_ptev.wait(T, st["evac"][7])
                for sj in range(NSLOT):
                    pd = psB if sj < 4 else psD
                    bc = T.matmul(pd[:, (sj % 4) * 128:(sj % 4 + 1) * 128],
                                  sel_sb[:, sj * 128:(sj + 1) * 128], rec8[:, :],
                                  start=True, stop=True)
                s_rb.inc(bc)
                s_rb.wait(SC)
                SC.copy(rsw[:, 0:512], psB[:, :])
                rc2 = SC.copy(rsw[:, 512:1024], psD[:, :])
                s_rb.inc(rc2)
                if debug and hb == 15:
                    sdt = sem("dbgt")
                    s_rb.wait(G)
                    sdt.inc(G.dma_start(dbg["tail"][:, 0:64], small[:, :]), 16)
                    sdt.inc(G.dma_start(dbg["tail"][0:8, 64:192], rec8[:, :]), 16)
                    sdt.inc(G.dma_start(dbg["tail"][:, 192:1216] if False else dbg["tail"][:, 128:1152], rsw[:, :]), 16)
                    sdt.wait(G)
                    sdt.wait(V)
                    sdt.wait(T)
                s_rb.wait(V)
                c1 = V.tensor_tensor(ctx[:, hb * TOK: hb * TOK + 512], psC[:, 0:512],
                                     rsw[:, 0:512], op=AOP.mult)
                c2 = V.tensor_tensor(ctx[:, hb * TOK + 512: hb * TOK + 1024], psC[:, 512:1024],
                                     rsw[:, 512:1024], op=AOP.mult)
                s_cx.inc(c1)
                v_cx_head[hb] = s_cx.inc(c2)

            pending = None
            for ih in range(HPP):
                h = p * HPP + ih
                if h > 1:
                    s_qp.wait(G, h - 1)
                s_wq.inc(G.dma_start(wqb[h % 2].rearrange("p (c k) -> p c k", c=NDC),
                                     wq_d[h, :, :, :].rearrange("c p k -> p c k")), 16)
                s_wq.wait(T)
                if ih == 0:
                    s_kro.wait(T, v_kro[p * HPP + HPP - 1])
                else:
                    s_ex.wait(T)
                for dc in range(NDC):
                    for grp in range(2):
                        mm = T.matmul(psA[:, grp * 512:(grp + 1) * 512],
                                      wqb[h % 2][:, dc * 128:(dc + 1) * 128],
                                      xn_own[:, dc * TOK + grp * 512: dc * TOK + (grp + 1) * 512],
                                      start=(dc == 0), stop=(dc == NDC - 1))
                s_qp.inc(mm)
                s_qp.wait(SC)
                gk = h - 1 if ih == 0 else h - 2
                if gk >= 0 and gk in v_cx_head:
                    s_cx.wait(SC, v_cx_head[gk])
                SC.copy(rsw[0:64, :], psA[64:128, 0:1024])
                hc = SC.copy(rsw[64:128, :], psA[0:64, 0:1024])
                s_kswc.inc(hc)
                s_qp.wait(V)
                s_kswc.wait(V)
                V.tensor_tensor(psA[:, 1024:2048], psA[:, 0:1024], cqt[:, :], op=AOP.mult)
                V.tensor_tensor(rsw[:, :], rsw[:, :], sqt[:, :], op=AOP.mult)
                rp = V.tensor_tensor(qfm[:, :], psA[:, 1024:2048], rsw[:, :], op=AOP.add)
                s_qro.inc(rp)

                if pending is not None:
                    beta_begin(pending)
                    beta_pair_transposes(pending, pending % HPP, 0)
                    beta_pair_evacs(pending, 0)
                s_qro.wait(T)
                for j in range(NSLOT):
                    own_w = (j + 1) * 128
                    ext = 2 * own_w
                    if j > 0:
                        s_ex.wait(T)
                    for w in range(2):
                        base_kv = w * 1024
                        col0 = w * own_w
                        for (c0, n) in _mm_chunks(col0, col0 + own_w):
                            off = c0 - col0
                            mm = T.matmul(psA[:, c0:c0 + n],
                                          qfm[:, j * 128:(j + 1) * 128],
                                          kfm[:, ih * S + base_kv + off: ih * S + base_kv + off + n],
                                          start=True, stop=True)
                    s_sm.inc(mm)
                    if pending is not None and j < NSLOT - 1:
                        beta_pair_transposes(pending, pending % HPP, j + 1)
                    s_sm.wait(V)
                    V.tensor_tensor(psA[:, j * 128:(j + 1) * 128], psA[:, j * 128:(j + 1) * 128],
                                    maskp[:, 0:128], op=AOP.add)
                    mk = V.tensor_tensor(psA[:, own_w + j * 128: own_w + (j + 1) * 128],
                                         psA[:, own_w + j * 128: own_w + (j + 1) * 128],
                                         maskp[:, 128:256], op=AOP.add)
                    s_mx.inc(mk)
                    if pending is not None and j < NSLOT - 1:
                        beta_pair_evacs(pending, j + 1)
                    s_mx.wait(SC)
                    if pending is not None:
                        s_pt.wait(SC, bst[pending]["pt"][(j, 1)])
                    elif h > 0 and j == 0:
                        s_pt.wait(SC, v_pt_head[h - 1])
                    ex = SC.activation(psb_sb[:, poff[j]:poff[j] + ext], psA[:, 0:ext], AF.Exp,
                                       accum_out=small[:, 32 + 8 * (h % 2) + j:33 + 8 * (h % 2) + j])
                    s_ex.inc(ex)
                v_msk_head[h] = s_msk.val
                v_ex_head[h] = s_ex.val
                if pending is not None:
                    beta_tail(pending, pending % HPP)
                    v_pt_head[pending] = s_pt.val
                    v_av_head[pending] = s_av.val

                if debug and h == 0:
                    sdq = sem("dbgq")
                    s_qro.wait(G)
                    sdq.inc(G.dma_start(dbg["qfm"][:, :], qfm[:, :]), 16)
                    s_msk.wait(G, v_msk_head[h])
                    sdq.inc(G.dma_start(dbg["psb"][:, :], psb_sb[:, :]), 16)
                    sdq.wait(G)
                    sdq.wait(SC)
                    sdq.wait(T)
                pending = h
            beta_begin(pending)
            for j in range(NSLOT):
                beta_pair_transposes(pending, pending % HPP, j)
                beta_pair_evacs(pending, j)
            beta_tail(pending, pending % HPP)
            v_pt_head[pending] = s_pt.val
            v_av_head[pending] = s_av.val
            pending = None
        s_cx.wait(V)
        s_cx.wait(T)
        s_cx.wait(SC)
    p1s.close()
    p01.close()
    barrier("b_p1")
    if debug:
        sd = sem("dbgc")
        sd.inc(G.dma_start(dbg["ctx"][:, :], ctx[:, :]), 16)
        sd.wait(G)
        sd.wait(SY)

    if stop_after == "p1":
        top.close()
        return nc
    p23 = ExitStack()
    acc = sb(p23, "acc", [128, NDC * TOK], F32)
    wos = sbN(p23, "wos", 2, NDC * 128, BF16)

    p2s = ExitStack()
    wup = sbN(p2s, "wup", 2, NDC * 128, BF16)
    wdn = sbN(p2s, "wdn", 2, ICPS * 128, BF16)
    wee = sbN(p2s, "wee", 2, NDC * 128, BF16)
    hsup = sb(p2s, "hsup", [128, ICPS * TOK], BF16)
    _ovlf = ovl.ap().bitcast(F32)
    ln2t = sbN(p2s, "ln2t", 2, TOK, F32) + [_ovlf[:, 2 * TOK:3 * TOK], _ovlf[:, 3 * TOK:4 * TOK]]
    ybf = [wos[0][:, 0:TOK], wos[0][:, TOK:2 * TOK]]
    y2bf = [ovl.ap()[:, 8 * TOK:9 * TOK], sb(p2s, "y2bf", [128, TOK], BF16)[:, :]]
    rtmp = [_ovlf[:, 0:TOK], _ovlf[:, TOK:2 * TOK]]
    if True:
        s_wup = sem("p2_wup")
        s_wdn = sem("p2_wdn")
        s_we = sem("p2_we")
        s_up = sem("p2_up")
        s_h = sem("p2_h")
        s_dn = sem("p2_dn")
        s_acc = sem("p2_acc")
        v_up = {}
        v_h = {}
        v_dn = {}
        v_acc = {}
        for sup in range(NSUP):
            for icw in range(ICPS):
                ic = sup * ICPS + icw
                bi = ic % 2
                if ic >= 2:
                    s_up.wait(G, v_up[ic - 2])
                s_wup.inc(G.dma_start(wup[bi].rearrange("p (c k) -> p c k", c=NDC),
                                      up_d[ic, :, :, :].rearrange("c p k -> p c k")), 16)
                s_wup.wait(T)
                if ic >= 2:
                    s_h.wait(T, v_h[ic - 2])
                ph = psA[:, (icw % 2) * 1024:(icw % 2) * 1024 + 1024]
                for dc in range(NDC):
                    for grp in range(2):
                        mm = T.matmul(ph[:, grp * 512:(grp + 1) * 512],
                                      wup[bi][:, dc * 128:(dc + 1) * 128],
                                      xn_own[:, dc * TOK + grp * 512: dc * TOK + (grp + 1) * 512],
                                      start=(dc == 0), stop=(dc == NDC - 1))
                v_up[ic] = s_up.inc(mm)
                s_up.wait(SC)
                if ic >= 2:
                    s_h.wait(SC, v_h[ic - 2])
                rl = SC.activation(rtmp[ic % 2][:, :], ph[:, :], AF.Relu)
                s_up.inc(rl)
                s_up.wait(V)
                if sup > 0 and icw == 0:
                    s_dn.wait(V, v_dn[(sup - 1, NDC - 1)])
                hv = V.tensor_tensor(hsup[:, icw * TOK:(icw + 1) * TOK], rtmp[ic % 2][:, :],
                                     rtmp[ic % 2][:, :], op=AOP.mult)
                v_h[ic] = s_h.inc(hv)
            s_h.wait(T)
            for dq in range(NDC):
                bi = dq % 2
                gidx = sup * NDC + dq
                if gidx >= 2:
                    s_dn.wait(G, v_dn[(sup, dq - 2)] if dq >= 2 else v_dn[(sup - 1, NDC + dq - 2)])
                s_wdn.inc(G.dma_start(wdn[bi].rearrange("p (c k) -> p c k", c=ICPS),
                                      dn_d[sup, dq, :, :, :].rearrange("c p k -> p c k")), 16)
                s_wdn.wait(T)
                pd = psC if dq % 2 == 0 else psE[:, :]
                if gidx >= 2:
                    prev = (sup, dq - 2) if dq >= 2 else (sup - 1, NDC + dq - 2)
                    s_acc.wait(T, v_acc[prev])
                for icw in range(ICPS):
                    for grp in range(2):
                        mm = T.matmul(pd[:, grp * 512:(grp + 1) * 512],
                                      wdn[bi][:, icw * 128:(icw + 1) * 128],
                                      hsup[:, icw * TOK + grp * 512: icw * TOK + (grp + 1) * 512],
                                      start=(icw == 0), stop=(icw == ICPS - 1))
                v_dn[(sup, dq)] = s_dn.inc(mm)
                s_dn.wait(V)
                if sup == 0:
                    av = V.tensor_copy(acc[:, dq * TOK:(dq + 1) * TOK], pd[:, :])
                else:
                    av = V.tensor_tensor(acc[:, dq * TOK:(dq + 1) * TOK],
                                         acc[:, dq * TOK:(dq + 1) * TOK], pd[:, :], op=AOP.add)
                v_acc[(sup, dq)] = s_acc.inc(av)
        s_g = sem("p2_gate")
        s_cbf.wait(T)
        s_acc.wait(T)
        v_g_m8 = {}
        for tc in range(NSLOT):
            if tc > 0:
                s_g.wait(T, v_g_m8[tc - 1])
            for dc in range(NDC):
                mm = T.matmul(psD[:, 0:8], xn_own[:, dc * TOK + tc * 128: dc * TOK + (tc + 1) * 128],
                              gate_sb[:, dc * 8: dc * 8 + 8],
                              start=(dc == 0), stop=(dc == NDC - 1))
            s_g.inc(mm)
            s_g.wait(V)
            m8 = V.max(small[:, 48:56], psD[:, 0:8])
            v_g_m8[tc] = s_g.inc(m8)
            s_g.wait(V)
            tt = V.tensor_tensor(small[:, 56 + tc:57 + tc], small[:, 48:49], small[:, 49:50],
                                 op=AOP.add)
            s_g.inc(tt)
        s_g.wait(T)
        tpm = T.matmul(psD[0:8, 0:128], small[:, 56:64], eye32[:, :], start=True, stop=True)
        s_g.inc(tpm)
        s_g.wait(SC)
        cp = SC.copy(rec8[:, :], psD[0:8, 0:128])
        s_g.inc(cp)
        s_g.wait(T)
        for tc in range(NSLOT):
            mm = T.matmul(psC[:, tc * 128:(tc + 1) * 128],
                          sel_sb[:, tc * 128:(tc + 1) * 128], rec8[:, :],
                          start=True, stop=True)
        s_g.inc(mm)
        if debug:
            s_g.wait(V)
            dv = V.tensor_copy(ln2t[0][:, :], psC[:, :])
            sdv = sem("dbgs")
            sdv.inc(dv)
            sdv.wait(SY)
            sdv.inc(SY.dma_start(dbg["sbc"][:, :], ln2t[0][:, :]), 16)
            sdv.wait(SY)
        s_mo = sem("p2_moemm")
        s_ms = sem("p2_moes")
        v_mo = {}
        v_ms = {}
        s_g.wait(V)
        sbc_cp = V.tensor_copy(ln2t[2][:, :], psC[:, :])
        s_g.inc(sbc_cp)
        s_g.wait(V)
        for dq in range(NDC):
            bi = dq % 2
            if dq >= 2:
                s_mo.wait(G, v_mo[dq - 2])
            s_we.inc(G.dma_start(wee[bi].rearrange("p (c k) -> p c k", c=NDC),
                                 e_d[dq, :, :, :].rearrange("c p k -> p c k")), 16)
            s_we.wait(T)
            pd = psA[:, (dq % 2) * 1024:(dq % 2) * 1024 + 1024]
            if dq >= 2:
                s_ms.wait(T, v_ms[dq - 2][0])
            for dc in range(NDC):
                for grp in range(2):
                    mm = T.matmul(pd[:, grp * 512:(grp + 1) * 512],
                                  wee[bi][:, dc * 128:(dc + 1) * 128],
                                  xn_own[:, dc * TOK + grp * 512: dc * TOK + (grp + 1) * 512],
                                  start=(dc == 0), stop=(dc == NDC - 1))
            v_mo[dq] = s_mo.inc(mm)
            s_mo.wait(V)
            tt = V.tensor_tensor(ln2t[3][:, :], pd[:, :], ln2t[2][:, :], op=AOP.mult)
            a1 = s_ms.inc(tt)
            ad = V.tensor_tensor(acc[:, dq * TOK:(dq + 1) * TOK],
                                 acc[:, dq * TOK:(dq + 1) * TOK], ln2t[3][:, :], op=AOP.add)
            v_ms[dq] = (a1, s_ms.inc(ad))
        s_st = sem("p2_cast")
        s_smm = sem("p2_statmm")
        s_ones.wait(T)
        s_ms.wait(T)
        v_cast = {}
        v_smm = {}
        for dq in range(NDC):
            bi = dq % 2
            s_ms.wait(V, v_ms[dq][1])
            if dq >= 2:
                s_smm.wait(V, v_smm[dq - 2])
            cv = V.tensor_copy(ybf[bi][:, :], acc[:, dq * TOK:(dq + 1) * TOK])
            s_st.inc(cv)
            s_st.wait(SC)
            if dq >= 2:
                s_smm.wait(SC, v_smm[dq - 2])
            sq = SC.activation(y2bf[bi][:, :], ybf[bi][:, :], AF.Square)
            v_cast[dq] = s_st.inc(sq)
            s_st.wait(T)
            for grp in range(2):
                T.matmul(psA[:, grp * 512:512 + grp * 512], onesbf[:, :],
                         ybf[bi][:, grp * 512:(grp + 1) * 512],
                         start=(dq == 0), stop=(dq == NDC - 1))
                mm = T.matmul(psA[:, 1024 + grp * 512:1536 + grp * 512], onesbf[:, :],
                              y2bf[bi][:, grp * 512:(grp + 1) * 512],
                              start=(dq == 0), stop=(dq == NDC - 1))
            v_smm[dq] = s_smm.inc(mm)
        s_smm.wait(SC)
        SC.mul(ln2t[0][:, :], psA[:, 0:1024], 1.0 / D)
        m2 = SC.mul(ln2t[1][:, :], psA[:, 1024:2048], 1.0 / D)
        s_stm = sem("p2_stm")
        s_stm.inc(m2)
        s_stm.wait(V)
        mu2 = V.tensor_tensor(ln2t[2][:, :], ln2t[0][:, :], ln2t[0][:, :], op=AOP.mult)
        s_stm.inc(mu2)
        s_stm.wait(V)
        vv = V.scalar_tensor_tensor(ln2t[1][:, :], ln2t[1][:, :], EPS, ln2t[2][:, :],
                                    op0=AOP.add, op1=AOP.subtract)
        s_stm.inc(vv)
        s_stm.wait(SC)
        SC.activation(ln2t[2][:, :], ln2t[1][:, :], AF.Ln)
        iv = SC.activation(ln2t[1][:, :], ln2t[2][:, :], AF.Exp, scale=-0.5)
        s_stm.inc(iv)
        s_stm.wait(V)
        ng = V.tensor_scalar(ln2t[0][:, :], ln2t[0][:, :], -1.0, None, op0=AOP.mult)
        s_stm.inc(ng)
        s_ap2 = sem("p2_apply")
        v_ap2 = {}
        for dq in range(NDC):
            V.tensor_tensor(ln2t[3][:, :], acc[:, dq * TOK:(dq + 1) * TOK], ln2t[0][:, :], op=AOP.add)
            V.tensor_tensor(ln2t[3][:, :], ln2t[3][:, :], ln2t[1][:, :], op=AOP.mult)
            ap = V.tensor_scalar(acc[:, dq * TOK:(dq + 1) * TOK], ln2t[3][:, :],
                                 gb_sb[:, dq:dq + 1], gb_sb[:, 16 + dq:17 + dq],
                                 op0=AOP.mult, op1=AOP.add)
            v_ap2[dq] = s_ap2.inc(ap)
        v_stats_done = (s_stm.val, s_g.val)
        s_ap2.wait(SC)
    p2s.close()
    if debug:
        sd = sem("dbgf")
        sems["p2_apply"].wait(SY)
        sd.inc(SY.dma_start(dbg["ffn"][:, :], acc[:, :]), 16)
        sd.wait(SY)
        sd.wait(V)
        sd.wait(T)

    if stop_after == "p2":
        p23.close()
        top.close()
        return nc
    p3s = ExitStack()
    x2 = sbN(p3s, "x2", 2, TOK, F32)
    if True:
        s_wo = sem("p3_wo")
        s_womm = sem("p3_womm")
        s_wadd = sem("p3_wadd")
        s_xl = sem("p3_xl")
        s_xt = sem("p3_xt")
        s_xadd = sem("p3_xadd")
        s_ot = sem("p3_ot")
        s_oev = sem("p3_oev")
        s_ost = sem("p3_ost")
        v_womm = {}
        v_wadd = {}
        v_xl = {}
        v_xadd = {}
        v_oev = {}
        v_ost = {}
        sems["p2_stm"].wait(T, v_stats_done[0])
        sems["p2_gate"].wait(T, v_stats_done[1])
        for dq in range(NDC):
            bi = dq % 2
            if dq < 2:
                s_smm.wait(G, v_smm[NDC - 1])
            else:
                s_womm.wait(G, v_womm[dq - 2])
            s_wo.inc(G.dma_start(wos[bi].rearrange("p (c k) -> p c k", c=NDC),
                                 wo_d[dq, :, :, :].rearrange("c p k -> p c k")), 16)
            s_wo.wait(T)
            pd = psC if dq % 2 == 0 else psA[:, 0:1024]
            if dq >= 2:
                s_wadd.wait(T, v_wadd[dq - 2])
            for fc in range(NDC):
                for grp in range(2):
                    mm = T.matmul(pd[:, grp * 512:(grp + 1) * 512],
                                  wos[bi][:, fc * 128:(fc + 1) * 128],
                                  ctx[:, fc * TOK + grp * 512: fc * TOK + (grp + 1) * 512],
                                  start=(fc == 0), stop=(fc == NDC - 1))
            v_womm[dq] = s_womm.inc(mm)
            s_womm.wait(V)
            sems["p2_apply"].wait(V, v_ap2[dq])
            ad = V.tensor_tensor(acc[:, dq * TOK:(dq + 1) * TOK],
                                 acc[:, dq * TOK:(dq + 1) * TOK], pd[:, :], op=AOP.add)
            v_wadd[dq] = s_wadd.inc(ad)
            bi2 = dq % 2
            if dq == 0:
                sems["p2_apply"].wait(SY)
            if dq >= 2:
                s_xadd.wait(SY, v_xadd[dq - 2])
            v_xl[dq] = s_xl.inc(SY.dma_start(x2[bi2][:, 0:TOK],
                                             xt_in[dq * 128:(dq + 1) * 128, :]), 16)
            s_xl.wait(V, v_xl[dq])
            ad2 = V.tensor_tensor(acc[:, dq * TOK:(dq + 1) * TOK],
                                  acc[:, dq * TOK:(dq + 1) * TOK], x2[bi2][:, 0:TOK], op=AOP.add)
            v_xadd[dq] = s_xadd.inc(ad2)
            s_xadd.wait(G, v_xadd[dq])
            v_ost[dq] = s_ost.inc(
                G.dma_start(out_d[:, dq * TOK:(dq + 1) * TOK], acc[:, dq * TOK:(dq + 1) * TOK]), 16)
        s_ost.wait(G)
        s_ost.wait(SY)
        s_ost.wait(T)
        s_ost.wait(V)
        s_ost.wait(SC)
    p3s.close()
    p23.close()
    top.close()
    return nc



def _prep_shared(wq, wk, wv, wo, up_w, down_w, ew, gate_w, ln_g, ln_b):
    f32 = np.float32
    perm = np.concatenate([np.arange(0, 128, 2), np.arange(1, 128, 2)])
    wq_p = wq.reshape(H, DK, D)[:, perm, :]
    wk_p = wk.reshape(H, DK, D)[:, perm, :]
    wq_pre = np.ascontiguousarray(wq_p.transpose(0, 2, 1).reshape(H, NDC, 128, 128), dtype=f32)
    wk_pre = np.ascontiguousarray(wk_p.transpose(0, 2, 1).reshape(H, NDC, 128, 128), dtype=f32)
    wv_pre = np.ascontiguousarray(
        wv.T.reshape(NDC, 128, NPASS, HPP * 128).transpose(2, 0, 1, 3), dtype=f32)
    wo_pre = np.ascontiguousarray(
        wo.T.reshape(NDC, 128, NDC, 128).transpose(2, 0, 1, 3), dtype=f32)
    up_pre = np.ascontiguousarray(
        up_w.reshape(NDC, 128, NIC, 128).transpose(2, 0, 1, 3), dtype=f32)
    dn_pre = np.ascontiguousarray(
        down_w.reshape(NSUP, ICPS, 128, NDC, 128).transpose(0, 3, 1, 2, 4), dtype=f32)
    e_pre = np.ascontiguousarray(
        ew.reshape(NDC, 128, NDC, 128).transpose(2, 0, 1, 3), dtype=f32)
    gate_pre = np.ascontiguousarray(
        gate_w.reshape(NDC, 128, NEXP).transpose(1, 0, 2).reshape(128, 128), dtype=f32)
    gb_pre = np.ascontiguousarray(
        np.concatenate([ln_g.reshape(NDC, 128).T, ln_b.reshape(NDC, 128).T], axis=1), dtype=f32)
    return dict(wq=wq_pre, wk=wk_pre, wv=wv_pre, wo=wo_pre, up=up_pre, dn=dn_pre,
                ew=e_pre, gate=gate_pre, gb=gb_pre,
                eye=np.eye(128, dtype=f32), ones1=np.ones((1, 128), dtype=f32),
                sel=np.ascontiguousarray(np.repeat(np.eye(8, dtype=f32), 128, axis=1)))


def _prep_core(x_b, par):
    f32 = np.float32
    xb = np.asarray(x_b).reshape(16, 128, D)
    order = []
    for j in range(8):
        order += [2 * j + par, 2 * j + (1 - par)]
    x_sw = np.ascontiguousarray(xb[order].reshape(S, D), dtype=f32)
    x_t = np.ascontiguousarray(x_sw.reshape(16, 128, D)[0::2].reshape(TOK, D).T, dtype=f32)

    inv = THETA ** (-(np.arange(0, DK, 2, dtype=np.float64) / DK))
    inv2 = np.concatenate([inv, inv])[:, None]
    own_pos = np.concatenate([np.arange(128) + (2 * j + par) * 128 for j in range(8)])
    oth_pos = np.concatenate([np.arange(128) + (2 * j + (1 - par)) * 128 for j in range(8)])
    sig_pos = np.concatenate([own_pos, oth_pos]).astype(np.float64)
    ang_k = sig_pos[None, :] * inv2
    ckt = np.cos(ang_k).astype(f32)
    skt = np.sin(ang_k).astype(f32)
    skt[0:64] *= -1.0
    ang_q = own_pos.astype(np.float64)[None, :] * inv2
    scale = 1.0 / np.sqrt(DK)
    cqt = (np.cos(ang_q) * scale).astype(f32)
    sqt = (np.sin(ang_q) * scale).astype(f32)
    sqt[0:64] *= -1.0
    tri = np.where(np.arange(128)[None, :] > np.arange(128)[:, None], NEG, 0.0).astype(f32)
    ptile = np.full((128, 128), NEG if par == 0 else 0.0, dtype=f32)
    maskp = np.ascontiguousarray(np.concatenate([tri, ptile], axis=1))
    return dict(x=x_sw, xT=x_t, ckt=ckt, skt=skt, cqt=cqt, sqt=sqt, maskp=maskp)


_NC_CACHE = {}


def kernel(L0_attn_x, L0_attn_wq, L0_attn_wk, L0_attn_wv, L0_attn_wo,
           L0_ffn_gate_w, L0_ffn_up_w, L0_ffn_down_w, L0_ffn_expert_weight,
           L0_ffn_ln_gamma, L0_ffn_ln_beta):
    f32 = np.float32
    x = np.asarray(L0_attn_x, dtype=f32)
    shared = _prep_shared(np.asarray(L0_attn_wq, f32), np.asarray(L0_attn_wk, f32),
                          np.asarray(L0_attn_wv, f32), np.asarray(L0_attn_wo, f32),
                          np.asarray(L0_ffn_up_w, f32), np.asarray(L0_ffn_down_w, f32),
                          np.asarray(L0_ffn_expert_weight, f32),
                          np.asarray(L0_ffn_gate_w, f32),
                          np.asarray(L0_ffn_ln_gamma, f32), np.asarray(L0_ffn_ln_beta, f32))
    if "nc" not in _NC_CACHE:
        _NC_CACHE["nc"] = build_program(debug=False)
    nc = _NC_CACHE["nc"]
    in_maps = []
    for core in range(NCORE):
        b, par = core // 2, core % 2
        m = dict(shared)
        m.update(_prep_core(x[b], par))
        in_maps.append(m)
    res = run_bass_kernel_spmd(nc, in_maps, core_ids=list(range(NCORE)))
    out = np.empty((B, S, D), dtype=f32)
    for core in range(NCORE):
        b, par = core // 2, core % 2
        o = np.asarray(res.results[core]["out"]).reshape(128, NDC, TOK).transpose(2, 1, 0)
        o = o.reshape(NSLOT, 128, D)
        for j in range(NSLOT):
            out[b, (2 * j + par) * 128:(2 * j + par + 1) * 128, :] = o[j]
    return out


# version 13
# speedup vs baseline: 1.2238x; 1.0659x over previous
import numpy as np
import concourse.bass as bass
import concourse.mybir as mybir
from concourse.bass_utils import run_bass_kernel_spmd
from contextlib import ExitStack

F32 = mybir.dt.float32
BF16 = mybir.dt.bfloat16
AOP = mybir.AluOpType
AF = mybir.ActivationFunctionType
AX = mybir.AxisListType

B, S, D = 4, 2048, 2048
H, DK = 16, 128
INTER = 8192
NEXP = 8
THETA = 10000.0
EPS = 1e-5
NCORE = 8
NSLOT = 8
TOK = NSLOT * 128
NDC = D // 128
NIC = INTER // 128
NSUP = 8
ICPS = NIC // NSUP
NEG = -1.0e30
NPASS = 4
HPP = H // NPASS


class Sem:
    def __init__(self, nc, name):
        self._cm = nc.semaphore(name)
        self.h = self._cm.__enter__()
        self.val = 0

    def inc(self, instr, n=1):
        instr.then_inc(self.h, n)
        self.val += n
        return self.val

    def wait(self, eng, val=None):
        eng.wait_ge(self.h, self.val if val is None else val)


def _mm_chunks(lo, hi):
    out = []
    c = lo
    while c < hi:
        n = min(hi, (c // 512 + 1) * 512) - c
        out.append((c, n))
        c += n
    return out


def build_program(debug=False, stop_after=None):
    nc = bass.Bass("TRN2", target_bir_lowering=False)
    top = ExitStack()

    dt = lambda n, shp, dty=F32, kind="ExternalInput": nc.dram_tensor(n, shp, dty, kind=kind).ap()
    x_in = dt("x", [S, D])
    xt_in = dt("xT", [D, TOK])
    wq_d = dt("wq", [H, NDC, 128, 128])
    wk_d = dt("wk", [H, NDC, 128, 128])
    wv_d = dt("wv", [NPASS, NDC, 128, HPP * 128])
    wo_d = dt("wo", [NDC, NDC, 128, 128])
    up_d = dt("up", [NIC, NDC, 128, 128])
    dn_d = dt("dn", [NSUP, NDC, ICPS, 128, 128])
    e_d = dt("ew", [NDC, NDC, 128, 128])
    gate_d = dt("gate", [128, 128])
    gb_d = dt("gb", [128, 32])
    ckt_d = dt("ckt", [128, S])
    skt_d = dt("skt", [128, S])
    cqt_d = dt("cqt", [128, TOK])
    sqt_d = dt("sqt", [128, TOK])
    mask_d = dt("maskp", [128, 256])
    eye_d = dt("eye", [128, 128])
    ones1_d = dt("ones1", [1, 128])
    sel_d = dt("sel", [8, 1024])
    out_d = dt("out", [128, NDC * TOK], F32, kind="ExternalOutput")
    dbg = {}
    if debug:
        dbg["xn_own"] = dt("dbg_xn_own", [128, NDC * TOK], F32, kind="ExternalOutput")
        dbg["kfm"] = dt("dbg_kfm", [128, S], F32, kind="ExternalOutput")
        dbg["vtm"] = dt("dbg_vtm", [128, 16 * HPP * 128], F32, kind="ExternalOutput")
        dbg["ctx"] = dt("dbg_ctx", [128, H * TOK], F32, kind="ExternalOutput")
        dbg["ffn"] = dt("dbg_ffn", [128, NDC * TOK], F32, kind="ExternalOutput")
        dbg["sbc"] = dt("dbg_sbc", [128, TOK], F32, kind="ExternalOutput")
        dbg["qfm"] = dt("dbg_qfm", [128, TOK], F32, kind="ExternalOutput")
        dbg["psb"] = dt("dbg_psb", [128, 9216], F32, kind="ExternalOutput")
        dbg["tail"] = dt("dbg_tail", [128, 1152], F32, kind="ExternalOutput")

    sb = lambda st_, n, shp, dty: st_.enter_context(nc.sbuf_tensor(n, shp, dty))

    def sbN(st_, name, n, w, dty):
        t = st_.enter_context(nc.sbuf_tensor(name, [128, n * w], dty))
        return [t[:, i * w:(i + 1) * w] for i in range(n)]
    xn_own = sb(top, "xn_own", [128, NDC * TOK], BF16)
    ctx = sb(top, "ctx", [128, H * TOK], BF16)
    eye32 = sb(top, "eye32", [128, 128], F32)
    eyebf = sb(top, "eyebf", [128, 128], BF16)
    onesbf = sb(top, "onesbf", [128, 128], BF16)
    ones1 = sb(top, "ones1_s", [1, 128], F32)
    gate_sb = sb(top, "gate_sb", [128, 128], BF16)
    gb_sb = sb(top, "gb_sb", [128, 32], F32)
    small = sb(top, "small", [128, 64], F32)
    ovl = sb(top, "ovl", [128, 9216], BF16)
    sel_sb = sb(top, "sel_sb", [8, 1024], F32)
    rec8 = sb(top, "rec8", [8, 128], F32)

    psA = top.enter_context(nc.psum_tensor("psA", [128, 2048], F32))
    psC = top.enter_context(nc.psum_tensor("psC", [128, 1024], F32))
    psE = top.enter_context(nc.psum_tensor("psE", [128, 1024], F32))
    psB = psE.ap()[:, 0:512]
    psD = psE.ap()[:, 512:1024]
    psBb = psE.ap().bitcast(BF16)[:, 0:1024]
    psAb = psA.ap().bitcast(BF16)

    T, V, SC, G, SY = nc.tensor, nc.vector, nc.scalar, nc.gpsimd, nc.sync
    ALL_ENG = (T, V, SC, G, SY)

    sems = {}

    def sem(name):
        if name not in sems:
            sems[name] = Sem(nc, name)
        return sems[name]

    def barrier(name):
        s = sem(name)
        base = s.val
        for eng in ALL_ENG:
            eng.sem_inc(s.h, 1)
        s.val = base + 5
        for eng in ALL_ENG:
            s.wait(eng)

    s_const = sem("const")
    for dst, src in ((eye32, eye_d), (gb_sb, gb_d)):
        s_const.inc(SY.dma_start(dst[:, :], src[:, :]), 16)
    s_const.inc(SY.dma_start(ones1[:, :], ones1_d[:, :]), 16)
    s_const.inc(SY.dma_start(sel_sb[:, :], sel_d[:, :]), 16)
    s_cbf = sem("cbf")
    s_cbf.inc(G.dma_start(eyebf[:, :], eye_d[:, :]), 16)
    s_cbf.inc(G.dma_start(gate_sb[:, :], gate_d[:, :]), 16)
    s_ones = sem("onesw")
    s_ones.inc(V.memset(onesbf[:, :], 1.0))

    p01 = ExitStack()
    xn_oth = sb(p01, "xn_oth", [128, NDC * TOK], BF16)
    ckt = sb(p01, "ckt_s", [128, S], F32)
    skt = sb(p01, "skt_s", [128, S], F32)
    cqt = sb(p01, "cqt_s", [128, TOK], F32)
    sqt = sb(p01, "sqt_s", [128, TOK], F32)
    maskp = sb(p01, "maskp_s", [128, 256], F32)
    for dst, src in ((ckt, ckt_d), (skt, skt_d), (cqt, cqt_d), (sqt, sqt_d), (maskp, mask_d)):
        s_const.inc(SY.dma_start(dst[:, :], src[:, :]), 16)
    s_const.wait(V)
    s_const.wait(T)
    s_const.wait(SC)
    s_wv0 = sem("p1_wv")
    wvb_pre = ovl.ap()[:, 0:NDC * HPP * 128]
    s_wv0.inc(G.dma_start(wvb_pre.rearrange("p (c v) -> p c v", c=NDC),
                          wv_d[0, :, :, :].rearrange("c p v -> p c v")), 16)

    poff = [0]
    for j in range(NSLOT):
        poff.append(poff[-1] + (2 * j + 2) * 128)
    PSB = poff[-1]

    p1s = ExitStack()
    kfm = sb(p1s, "kfm", [128, HPP * S], BF16)
    vtm = sb(p1s, "vtm", [128, 16 * HPP * 128], BF16)
    wkb = sbN(p1s, "wkb", 2, NDC * 128, BF16)
    wqb = sbN(p1s, "wqb", 2, NDC * 128, BF16)
    qfm = sb(p1s, "qfm", [128, TOK], BF16)
    ptsb_t = sb(p1s, "ptsb", [128, 2 * 2048], BF16)
    ptsb = [ptsb_t[:, 0:2048], ptsb_t[:, 2048:4096]]
    rsw = ptsb_t.ap().bitcast(F32)[:, 0:1024]
    psb_sb = ovl
    wvb = ovl.ap()[:, 0:NDC * HPP * 128]

    _kfmf = kfm.ap().bitcast(F32)
    x_tm = [_kfmf[:, 0:D], _kfmf[:, D:2 * D]]
    xn_tm = [ptsb_t[:, 0:D], ptsb_t[:, D:2 * D]]
    if True:
        s_x = sem("p0_x")
        s_ap = sem("p0_ap")
        s_iv = sem("p0_iv")
        s_nm = sem("p0_nm")
        s_tp = sem("p0_tp")
        s_ev = sem("p0_ev")
        v_ap = {}
        v_tp = {}
        v_ev = {}
        v_vp0 = {}
        v_vev0 = {}
        for gc in range(NDC):
            bi = gc % 2
            if gc >= 2:
                s_ap.wait(SY, v_ap[gc - 2])
            s_x.inc(SY.dma_start(x_tm[bi][:, :], x_in[gc * 128:(gc + 1) * 128, :]), 16)
            s_x.wait(V)
            for q in range(4):
                st4 = V.bn_stats(small[:, 6 * q:6 * q + 6], x_tm[bi][:, 512 * q:512 * (q + 1)])
            s_nm.inc(st4)
            s_nm.wait(V)
            ag4 = V.bn_aggr(small[:, 24:26], small[:, 0:24])
            s_nm.inc(ag4)
            s_nm.wait(V)
            V.tensor_scalar(small[:, 26:27], small[:, 24:25], -1.0, None, op0=AOP.mult)
            nm = V.tensor_scalar(small[:, 27:28], small[:, 25:26], EPS, None, op0=AOP.add)
            s_nm.inc(nm)
            s_nm.wait(SC)
            SC.activation(small[:, 28:29], small[:, 27:28], AF.Ln)
            iv = SC.activation(small[:, 29:30], small[:, 28:29], AF.Exp, scale=-0.5)
            s_iv.inc(iv)
            s_iv.wait(V)
            s_nm.wait(V)
            if gc >= 2:
                s_tp.wait(V, v_tp[gc - 2])
            ap = V.tensor_scalar(xn_tm[bi][:, :], x_tm[bi][:, :], small[:, 26:27],
                                 small[:, 29:30], op0=AOP.add, op1=AOP.mult)
            v_ap[gc] = s_ap.inc(ap)
            s_ap.wait(T)
            if gc >= 1:
                s_ev.wait(T, 2 * gc)
            tgt = xn_own if bi == 0 else xn_oth
            j = gc // 2
            psCb = psC.ap().bitcast(BF16)
            for half in range(2):
                pp = psCb[:, 0:1024] if half == 0 else psCb[:, 1024:2048]
                for k in range(8):
                    dc = half * 8 + k
                    tp = T.transpose(pp[:, k * 128:(k + 1) * 128],
                                     xn_tm[bi][:, dc * 128:(dc + 1) * 128], eyebf[:, :])
                s_tp.inc(tp)
                v_tp[gc] = s_tp.val
                s_tp.wait(V)
                dview = tgt.ap().rearrange("p (c n) -> p c n", n=TOK)[:, half * 8:half * 8 + 8,
                                                                     j * 128:(j + 1) * 128]
                sview = pp.rearrange("p (c n) -> p c n", n=128)
                ev = V.tensor_copy(dview, sview)
                s_ev.inc(ev)
            v_ev[gc] = s_ev.val
            gb = gc // 2 if gc % 2 == 0 else 8 + gc // 2
            s_vp0 = sem("p1_vp")
            s_vev0 = sem("p1_vev")
            vsrc = xn_own if gb < 8 else xn_oth
            jj = gb % 8
            pv = psB if gc % 2 == 0 else psD
            sem("p1_wv").wait(T)
            s_ev.wait(T, v_ev[gc])
            if gc >= 2:
                s_vev0.wait(T, v_vev0[gc - 2])
            for dc in range(NDC):
                mm = T.matmul(pv[:, 0:HPP * 128],
                              vsrc[:, dc * TOK + jj * 128: dc * TOK + (jj + 1) * 128],
                              wvb[:, dc * HPP * 128:(dc + 1) * HPP * 128],
                              start=(dc == 0), stop=(dc == NDC - 1))
            v_vp0[gc] = s_vp0.inc(mm)
            s_vp0.wait(SC)
            ev2 = SC.copy(vtm[:, gb * HPP * 128:(gb + 1) * HPP * 128], pv[:, 0:HPP * 128])
            v_vev0[gc] = s_vev0.inc(ev2)
        s_ev.wait(V)
    if debug:
        sd = sem("dbg0")
        s_ev.wait(G)
        sd.inc(G.dma_start(dbg["xn_own"][:, :], xn_own[:, :]), 16)
        sd.wait(G)
        sd.wait(SY)

    if stop_after == "p0":
        p1s.close()
        p01.close()
        top.close()
        return nc
    if True:

        s_wv = sem("p1_wv")
        s_wk = sem("p1_wk")
        s_wq = sem("p1_wq")
        s_vp = sem("p1_vp")
        s_vev = sem("p1_vev")
        s_kp = sem("p1_kp")
        s_kro = sem("p1_kro")
        s_kswc = sem("p1_kswc")
        s_kfree = sem("p1_kfree")
        v_kswc = {}
        v_kfree = {}
        s_qp = sem("p1_qp")
        s_qro = sem("p1_qro")
        s_sm = sem("p1_sm")
        s_mx = sem("p1_mx")
        s_ex = sem("p1_ex")
        s_msk = sem("p1_msk")
        v_msk_head = {}
        s_pt = sem("p1_pt")
        s_ptev = sem("p1_ptev")
        s_av = sem("p1_av")
        s_rb = sem("p1_rb")
        s_cx = sem("p1_cx")

        v_vev = {}
        v_vp = {}
        v_kp = {}
        v_kro = {}
        v_pt_head = {}
        v_ex_head = {}
        v_cx_head = {}
        v_av_head = {}

        for p in range(NPASS):
            if p > 0:
                s_pt.wait(G, v_pt_head[p * HPP - 1])
                s_wv.inc(G.dma_start(wvb.rearrange("p (c v) -> p c v", c=NDC),
                                     wv_d[p, :, :, :].rearrange("c p v -> p c v")), 16)
            s_wv.wait(T)
            if p > 0:
                s_cx.wait(T, v_cx_head[p * HPP - 1])
            gb_order = [b for pair in zip(range(8), range(8, 16)) for b in pair]
            for oi, gb in enumerate(gb_order):
                if p == 0:
                    break
                src = xn_own if gb < 8 else xn_oth
                j = gb % 8
                pv = psB if oi % 2 == 0 else psD
                if oi >= 2:
                    s_vev.wait(T, v_vev[(p, gb_order[oi - 2])])
                for dc in range(NDC):
                    mm = T.matmul(pv[:, 0:HPP * 128],
                                  src[:, dc * TOK + j * 128: dc * TOK + (j + 1) * 128],
                                  wvb[:, dc * HPP * 128:(dc + 1) * HPP * 128],
                                  start=(dc == 0), stop=(dc == NDC - 1))
                v_vp[(p, gb)] = s_vp.inc(mm)
                s_vp.wait(SC)
                if oi == 0:
                    s_av.wait(SC, v_av_head[p * HPP - 1])
                ev = SC.copy(vtm[:, gb * HPP * 128:(gb + 1) * HPP * 128], pv[:, 0:HPP * 128])
                v_vev[(p, gb)] = s_vev.inc(ev)
            for ih in range(HPP):
                h = p * HPP + ih
                bi = ih % 2
                if ih >= 2:
                    s_kp.wait(G, v_kp[h - 2])
                s_wk.inc(G.dma_start(wkb[bi].rearrange("p (c k) -> p c k", c=NDC),
                                     wk_d[h, :, :, :].rearrange("c p k -> p c k")), 16)
                s_wk.wait(T)
                if p > 0 and ih == 0:
                    s_cx.wait(SC, v_cx_head[p * HPP - 1])
                for hf in range(2):
                    if ih == 0:
                        if hf == 0:
                            s_ex.wait(T)
                            if p == 0:
                                s_ev.wait(T, v_ev[NDC - 1])
                    else:
                        s_kswc.wait(T, v_kswc[(h - 1, hf)])
                        s_kfree.wait(T, v_kfree[(h - 1, hf)])
                    src = xn_own if hf == 0 else xn_oth
                    for dc in range(NDC):
                        for grp in range(2):
                            mm = T.matmul(psA[:, hf * 1024 + grp * 512: hf * 1024 + (grp + 1) * 512],
                                          wkb[bi][:, dc * 128:(dc + 1) * 128],
                                          src[:, dc * TOK + grp * 512: dc * TOK + (grp + 1) * 512],
                                          start=(dc == 0), stop=(dc == NDC - 1))
                    v_kp[(h, hf)] = s_kp.inc(mm)
                v_kp[h] = v_kp[(h, 1)]
                for hf in range(2):
                    s_kp.wait(SC, v_kp[(h, hf)])
                    if hf == 1:
                        s_kro.wait(SC)
                    SC.copy(rsw[0:64, :], psA[64:128, hf * 1024:(hf + 1) * 1024])
                    hc = SC.copy(rsw[64:128, :], psA[0:64, hf * 1024:(hf + 1) * 1024])
                    v_kswc[(h, hf)] = s_kswc.inc(hc)
                    s_kp.wait(V, v_kp[(h, hf)])
                    s_kswc.wait(V)
                    mf = V.tensor_tensor(psC[:, :], psA[:, hf * 1024:(hf + 1) * 1024],
                                         ckt[:, hf * 1024:(hf + 1) * 1024], op=AOP.mult)
                    v_kfree[(h, hf)] = s_kfree.inc(mf)
                    V.tensor_tensor(rsw[:, :], rsw[:, :],
                                    skt[:, hf * 1024:(hf + 1) * 1024], op=AOP.mult)
                    rp = V.tensor_tensor(kfm[:, ih * S + hf * 1024: ih * S + (hf + 1) * 1024],
                                         psC[:, :], rsw[:, :], op=AOP.add)
                    s_kro.inc(rp)
                v_kro[h] = s_kro.val
            if debug and p == 0:
                sdk = sem("dbgk")
                s_kro.wait(G, v_kro[HPP - 1])
                s_vev.wait(G)
                sdk.inc(G.dma_start(dbg["kfm"][:, :], kfm[:, 0:S]), 16)
                sdk.inc(G.dma_start(dbg["vtm"][:, 0:2048], vtm[:, 0:2048]), 16)
                sdk.wait(G)

            bst = {}

            def beta_begin(hb):
                bst[hb] = dict(av_base=s_av.val, prev_ptev=s_ptev.val,
                               evac={}, pt={}, avp={})
                if hb > 0 and (hb - 1) in v_cx_head:
                    s_cx.wait(T, v_cx_head[hb - 1])
                s_msk.wait(T, v_msk_head[hb])

            def _av_pair(hb, ihb, jp, is_first, is_last):
                st = bst[hb]
                q0 = jp * 128
                for ki, k in enumerate((jp, 8 + jp)):
                    pt_src = ptsb[jp % 2]
                    off = ki * 1024
                    for (c0, n) in _mm_chunks(q0, 1024):
                        mm = T.matmul(psC[:, c0:c0 + n],
                                      vtm[:, k * HPP * 128 + ihb * 128: k * HPP * 128 + (ihb + 1) * 128],
                                      pt_src[:, off + c0 - q0: off + c0 - q0 + n],
                                      start=(is_first and ki == 0), stop=(is_last and ki == 1))
                st["avp"][jp] = s_av.inc(mm)

            def _tp_block(hb, jp, ki):
                st = bst[hb]
                k = jp if ki == 0 else 8 + jp
                for sj in range(jp, NSLOT):
                    own_w = (sj + 1) * 128
                    col = (k * 128) if k < 8 else (own_w + (k - 8) * 128)
                    tp = T.transpose(psBb[:, (sj - jp) * 128:(sj - jp + 1) * 128],
                                     psb_sb[:, poff[sj] + col: poff[sj] + col + 128],
                                     eyebf[:, :])
                st["pt"][(jp, ki)] = s_pt.inc(tp)

            def _evac_block(hb, jp, ki):
                st = bst[hb]
                nsl = (NSLOT - jp) * 128
                s_pt.wait(V, st["pt"][(jp, ki)])
                if ki == 0:
                    s_av.wait(V, st["avp"][jp - 2] if jp >= 2 else st["av_base"])
                pe = V.tensor_copy(ptsb[jp % 2][:, ki * 1024: ki * 1024 + nsl],
                                   psBb[:, 0:nsl])
                st["evac"][(jp, ki)] = s_ptev.inc(pe)

            def beta_pair_transposes(hb, ihb, jp):
                st = bst[hb]
                if jp > 0:
                    s_ptev.wait(T, st["evac"][jp - 1])
                    _av_pair(hb, ihb, jp - 1, jp - 1 == 0, False)
                else:
                    s_ptev.wait(T, st["prev_ptev"])
                _tp_block(hb, jp, 0)

            def beta_pair_evacs(hb, jp):
                st = bst[hb]
                _evac_block(hb, jp, 0)
                s_ptev.wait(T, st["evac"][(jp, 0)])
                _tp_block(hb, jp, 1)
                st["pt"][jp] = st["pt"][(jp, 1)]
                _evac_block(hb, jp, 1)
                st["evac"][jp] = st["evac"][(jp, 1)]

            def beta_tail(hb, ihb):
                st = bst[hb]
                s_ptev.wait(T, st["evac"][7])
                _av_pair(hb, ihb, 7, False, True)
                s_ex.wait(V, v_ex_head[hb])
                rc = V.reciprocal(small[:, 48:56],
                                  small[:, 32 + 8 * (hb % 2):40 + 8 * (hb % 2)])
                s_rb.inc(rc)
                s_rb.wait(T)
                tpm = T.matmul(psD[0:8, 0:128], small[:, 48:56], eye32[:, :],
                               start=True, stop=True)
                s_rb.inc(tpm)
                s_rb.wait(SC)
                cp = SC.copy(rec8[:, :], psD[0:8, 0:128])
                s_rb.inc(cp)
                s_rb.wait(T)
                s_ptev.wait(T, st["evac"][7])
                for sj in range(NSLOT):
                    pd = psB if sj < 4 else psD
                    bc = T.matmul(pd[:, (sj % 4) * 128:(sj % 4 + 1) * 128],
                                  sel_sb[:, sj * 128:(sj + 1) * 128], rec8[:, :],
                                  start=True, stop=True)
                s_rb.inc(bc)
                s_rb.wait(SC)
                SC.copy(rsw[:, 0:512], psB[:, :])
                rc2 = SC.copy(rsw[:, 512:1024], psD[:, :])
                s_rb.inc(rc2)
                if debug and hb == 15:
                    sdt = sem("dbgt")
                    s_rb.wait(G)
                    sdt.inc(G.dma_start(dbg["tail"][:, 0:64], small[:, :]), 16)
                    sdt.inc(G.dma_start(dbg["tail"][0:8, 64:192], rec8[:, :]), 16)
                    sdt.inc(G.dma_start(dbg["tail"][:, 192:1216] if False else dbg["tail"][:, 128:1152], rsw[:, :]), 16)
                    sdt.wait(G)
                    sdt.wait(V)
                    sdt.wait(T)
                s_rb.wait(V)
                c1 = V.tensor_tensor(ctx[:, hb * TOK: hb * TOK + 512], psC[:, 0:512],
                                     rsw[:, 0:512], op=AOP.mult)
                c2 = V.tensor_tensor(ctx[:, hb * TOK + 512: hb * TOK + 1024], psC[:, 512:1024],
                                     rsw[:, 512:1024], op=AOP.mult)
                s_cx.inc(c1)
                v_cx_head[hb] = s_cx.inc(c2)

            pending = None
            for ih in range(HPP):
                h = p * HPP + ih
                if h > 1:
                    s_qp.wait(G, h - 1)
                s_wq.inc(G.dma_start(wqb[h % 2].rearrange("p (c k) -> p c k", c=NDC),
                                     wq_d[h, :, :, :].rearrange("c p k -> p c k")), 16)
                s_wq.wait(T)
                if ih == 0:
                    s_kro.wait(T, v_kro[p * HPP + HPP - 1])
                else:
                    s_ex.wait(T)
                for dc in range(NDC):
                    for grp in range(2):
                        mm = T.matmul(psA[:, grp * 512:(grp + 1) * 512],
                                      wqb[h % 2][:, dc * 128:(dc + 1) * 128],
                                      xn_own[:, dc * TOK + grp * 512: dc * TOK + (grp + 1) * 512],
                                      start=(dc == 0), stop=(dc == NDC - 1))
                s_qp.inc(mm)
                s_qp.wait(SC)
                gk = h - 1 if ih == 0 else h - 2
                if gk >= 0 and gk in v_cx_head:
                    s_cx.wait(SC, v_cx_head[gk])
                SC.copy(rsw[0:64, :], psA[64:128, 0:1024])
                hc = SC.copy(rsw[64:128, :], psA[0:64, 0:1024])
                s_kswc.inc(hc)
                s_qp.wait(V)
                s_kswc.wait(V)
                V.tensor_tensor(psA[:, 1024:2048], psA[:, 0:1024], cqt[:, :], op=AOP.mult)
                V.tensor_tensor(rsw[:, :], rsw[:, :], sqt[:, :], op=AOP.mult)
                rp = V.tensor_tensor(qfm[:, :], psA[:, 1024:2048], rsw[:, :], op=AOP.add)
                s_qro.inc(rp)

                if pending is not None:
                    beta_begin(pending)
                    beta_pair_transposes(pending, pending % HPP, 0)
                    beta_pair_evacs(pending, 0)
                s_qro.wait(T)
                for j in range(NSLOT):
                    own_w = (j + 1) * 128
                    ext = 2 * own_w
                    if j > 0:
                        s_ex.wait(T)
                    for w in range(2):
                        base_kv = w * 1024
                        col0 = w * own_w
                        for (c0, n) in _mm_chunks(col0, col0 + own_w):
                            off = c0 - col0
                            mm = T.matmul(psA[:, c0:c0 + n],
                                          qfm[:, j * 128:(j + 1) * 128],
                                          kfm[:, ih * S + base_kv + off: ih * S + base_kv + off + n],
                                          start=True, stop=True)
                    s_sm.inc(mm)
                    if pending is not None and j < NSLOT - 1:
                        beta_pair_transposes(pending, pending % HPP, j + 1)
                    s_sm.wait(V)
                    V.tensor_tensor(psA[:, j * 128:(j + 1) * 128], psA[:, j * 128:(j + 1) * 128],
                                    maskp[:, 0:128], op=AOP.add)
                    mk = V.tensor_tensor(psA[:, own_w + j * 128: own_w + (j + 1) * 128],
                                         psA[:, own_w + j * 128: own_w + (j + 1) * 128],
                                         maskp[:, 128:256], op=AOP.add)
                    s_mx.inc(mk)
                    if pending is not None and j < NSLOT - 1:
                        beta_pair_evacs(pending, j + 1)
                    s_mx.wait(SC)
                    if pending is not None:
                        s_pt.wait(SC, bst[pending]["pt"][(j, 1)])
                    elif h > 0 and j == 0:
                        s_pt.wait(SC, v_pt_head[h - 1])
                    ex = SC.activation(psb_sb[:, poff[j]:poff[j] + ext], psA[:, 0:ext], AF.Exp,
                                       accum_out=small[:, 32 + 8 * (h % 2) + j:33 + 8 * (h % 2) + j])
                    s_ex.inc(ex)
                v_msk_head[h] = s_msk.val
                v_ex_head[h] = s_ex.val
                if pending is not None:
                    beta_tail(pending, pending % HPP)
                    v_pt_head[pending] = s_pt.val
                    v_av_head[pending] = s_av.val

                if debug and h == 0:
                    sdq = sem("dbgq")
                    s_qro.wait(G)
                    sdq.inc(G.dma_start(dbg["qfm"][:, :], qfm[:, :]), 16)
                    s_msk.wait(G, v_msk_head[h])
                    sdq.inc(G.dma_start(dbg["psb"][:, :], psb_sb[:, :]), 16)
                    sdq.wait(G)
                    sdq.wait(SC)
                    sdq.wait(T)
                pending = h
            beta_begin(pending)
            for j in range(NSLOT):
                beta_pair_transposes(pending, pending % HPP, j)
                beta_pair_evacs(pending, j)
            beta_tail(pending, pending % HPP)
            v_pt_head[pending] = s_pt.val
            v_av_head[pending] = s_av.val
            pending = None
        s_cx.wait(V)
        s_cx.wait(T)
        s_cx.wait(SC)
    p1s.close()
    p01.close()
    barrier("b_p1")
    if debug:
        sd = sem("dbgc")
        sd.inc(G.dma_start(dbg["ctx"][:, :], ctx[:, :]), 16)
        sd.wait(G)
        sd.wait(SY)

    if stop_after == "p1":
        top.close()
        return nc
    p23 = ExitStack()
    acc = sb(p23, "acc", [128, NDC * TOK], F32)
    wos = sbN(p23, "wos", 2, NDC * 128, BF16)

    p2s = ExitStack()
    wup = sbN(p2s, "wup", 2, NDC * 128, BF16)
    wdn = sbN(p2s, "wdn", 2, ICPS * 128, BF16)
    wee = sbN(p2s, "wee", 2, NDC * 128, BF16)
    hsup = sb(p2s, "hsup", [128, ICPS * TOK], BF16)
    _ovlf = ovl.ap().bitcast(F32)
    ln2t = sbN(p2s, "ln2t", 2, TOK, F32) + [_ovlf[:, 2 * TOK:3 * TOK], _ovlf[:, 3 * TOK:4 * TOK]]
    ybf = [wos[0][:, 0:TOK], wos[0][:, TOK:2 * TOK]]
    y2bf = [ovl.ap()[:, 8 * TOK:9 * TOK], sb(p2s, "y2bf", [128, TOK], BF16)[:, :]]
    rtmp = [_ovlf[:, 0:TOK], _ovlf[:, TOK:2 * TOK]]
    if True:
        s_wup = sem("p2_wup")
        s_wdn = sem("p2_wdn")
        s_we = sem("p2_we")
        s_up = sem("p2_up")
        s_h = sem("p2_h")
        s_dn = sem("p2_dn")
        s_acc = sem("p2_acc")
        v_up = {}
        v_h = {}
        v_dn = {}
        v_acc = {}
        for sup in range(NSUP):
            for icw in range(ICPS):
                ic = sup * ICPS + icw
                bi = ic % 2
                if ic >= 2:
                    s_up.wait(G, v_up[ic - 2])
                s_wup.inc(G.dma_start(wup[bi].rearrange("p (c k) -> p c k", c=NDC),
                                      up_d[ic, :, :, :].rearrange("c p k -> p c k")), 16)
                s_wup.wait(T)
                if ic >= 2:
                    s_h.wait(T, v_h[ic - 2])
                ph = psA[:, (icw % 2) * 1024:(icw % 2) * 1024 + 1024]
                for dc in range(NDC):
                    for grp in range(2):
                        mm = T.matmul(ph[:, grp * 512:(grp + 1) * 512],
                                      wup[bi][:, dc * 128:(dc + 1) * 128],
                                      xn_own[:, dc * TOK + grp * 512: dc * TOK + (grp + 1) * 512],
                                      start=(dc == 0), stop=(dc == NDC - 1))
                v_up[ic] = s_up.inc(mm)
                s_up.wait(SC)
                if ic >= 2:
                    s_h.wait(SC, v_h[ic - 2])
                rl = SC.activation(rtmp[ic % 2][:, :], ph[:, :], AF.Relu)
                s_up.inc(rl)
                s_up.wait(V)
                if sup > 0 and icw == 0:
                    s_dn.wait(V, v_dn[(sup - 1, NDC - 1)])
                hv = V.tensor_tensor(hsup[:, icw * TOK:(icw + 1) * TOK], rtmp[ic % 2][:, :],
                                     rtmp[ic % 2][:, :], op=AOP.mult)
                v_h[ic] = s_h.inc(hv)
            s_h.wait(T)
            for dq in range(NDC):
                bi = dq % 2
                gidx = sup * NDC + dq
                if gidx >= 2:
                    s_dn.wait(G, v_dn[(sup, dq - 2)] if dq >= 2 else v_dn[(sup - 1, NDC + dq - 2)])
                s_wdn.inc(G.dma_start(wdn[bi].rearrange("p (c k) -> p c k", c=ICPS),
                                      dn_d[sup, dq, :, :, :].rearrange("c p k -> p c k")), 16)
                s_wdn.wait(T)
                pd = psC if dq % 2 == 0 else psE[:, :]
                if gidx >= 2:
                    prev = (sup, dq - 2) if dq >= 2 else (sup - 1, NDC + dq - 2)
                    s_acc.wait(T, v_acc[prev])
                for icw in range(ICPS):
                    for grp in range(2):
                        mm = T.matmul(pd[:, grp * 512:(grp + 1) * 512],
                                      wdn[bi][:, icw * 128:(icw + 1) * 128],
                                      hsup[:, icw * TOK + grp * 512: icw * TOK + (grp + 1) * 512],
                                      start=(icw == 0), stop=(icw == ICPS - 1))
                v_dn[(sup, dq)] = s_dn.inc(mm)
                s_dn.wait(V)
                if sup == 0:
                    av = V.tensor_copy(acc[:, dq * TOK:(dq + 1) * TOK], pd[:, :])
                else:
                    av = V.tensor_tensor(acc[:, dq * TOK:(dq + 1) * TOK],
                                         acc[:, dq * TOK:(dq + 1) * TOK], pd[:, :], op=AOP.add)
                v_acc[(sup, dq)] = s_acc.inc(av)
        s_g = sem("p2_gate")
        s_cbf.wait(T)
        s_acc.wait(T)
        v_g_m8 = {}
        for tc in range(NSLOT):
            if tc > 0:
                s_g.wait(T, v_g_m8[tc - 1])
            for dc in range(NDC):
                mm = T.matmul(psD[:, 0:8], xn_own[:, dc * TOK + tc * 128: dc * TOK + (tc + 1) * 128],
                              gate_sb[:, dc * 8: dc * 8 + 8],
                              start=(dc == 0), stop=(dc == NDC - 1))
            s_g.inc(mm)
            s_g.wait(V)
            m8 = V.max(small[:, 48:56], psD[:, 0:8])
            v_g_m8[tc] = s_g.inc(m8)
            s_g.wait(V)
            tt = V.tensor_tensor(small[:, 56 + tc:57 + tc], small[:, 48:49], small[:, 49:50],
                                 op=AOP.add)
            s_g.inc(tt)
        s_g.wait(T)
        tpm = T.matmul(psD[0:8, 0:128], small[:, 56:64], eye32[:, :], start=True, stop=True)
        s_g.inc(tpm)
        s_g.wait(SC)
        cp = SC.copy(rec8[:, :], psD[0:8, 0:128])
        s_g.inc(cp)
        s_g.wait(T)
        for tc in range(NSLOT):
            mm = T.matmul(psC[:, tc * 128:(tc + 1) * 128],
                          sel_sb[:, tc * 128:(tc + 1) * 128], rec8[:, :],
                          start=True, stop=True)
        s_g.inc(mm)
        if debug:
            s_g.wait(V)
            dv = V.tensor_copy(ln2t[0][:, :], psC[:, :])
            sdv = sem("dbgs")
            sdv.inc(dv)
            sdv.wait(SY)
            sdv.inc(SY.dma_start(dbg["sbc"][:, :], ln2t[0][:, :]), 16)
            sdv.wait(SY)
        s_mo = sem("p2_moemm")
        s_ms = sem("p2_moes")
        v_mo = {}
        v_ms = {}
        s_g.wait(V)
        sbc_cp = V.tensor_copy(ln2t[2][:, :], psC[:, :])
        s_g.inc(sbc_cp)
        s_g.wait(V)
        for dq in range(NDC):
            bi = dq % 2
            if dq >= 2:
                s_mo.wait(G, v_mo[dq - 2])
            s_we.inc(G.dma_start(wee[bi].rearrange("p (c k) -> p c k", c=NDC),
                                 e_d[dq, :, :, :].rearrange("c p k -> p c k")), 16)
            s_we.wait(T)
            pd = psA[:, (dq % 2) * 1024:(dq % 2) * 1024 + 1024]
            if dq >= 2:
                s_ms.wait(T, v_ms[dq - 2][0])
            for dc in range(NDC):
                for grp in range(2):
                    mm = T.matmul(pd[:, grp * 512:(grp + 1) * 512],
                                  wee[bi][:, dc * 128:(dc + 1) * 128],
                                  xn_own[:, dc * TOK + grp * 512: dc * TOK + (grp + 1) * 512],
                                  start=(dc == 0), stop=(dc == NDC - 1))
            v_mo[dq] = s_mo.inc(mm)
            s_mo.wait(V)
            tt = V.tensor_tensor(ln2t[3][:, :], pd[:, :], ln2t[2][:, :], op=AOP.mult)
            a1 = s_ms.inc(tt)
            ad = V.tensor_tensor(acc[:, dq * TOK:(dq + 1) * TOK],
                                 acc[:, dq * TOK:(dq + 1) * TOK], ln2t[3][:, :], op=AOP.add)
            v_ms[dq] = (a1, s_ms.inc(ad))
        s_st = sem("p2_cast")
        s_smm = sem("p2_statmm")
        s_ones.wait(T)
        s_ms.wait(T)
        v_cast = {}
        v_smm = {}
        for dq in range(NDC):
            bi = dq % 2
            s_ms.wait(V, v_ms[dq][1])
            if dq >= 2:
                s_smm.wait(V, v_smm[dq - 2])
            cv = V.tensor_copy(ybf[bi][:, :], acc[:, dq * TOK:(dq + 1) * TOK])
            s_st.inc(cv)
            s_st.wait(SC)
            if dq >= 2:
                s_smm.wait(SC, v_smm[dq - 2])
            sq = SC.activation(y2bf[bi][:, :], ybf[bi][:, :], AF.Square)
            v_cast[dq] = s_st.inc(sq)
            s_st.wait(T)
            for grp in range(2):
                T.matmul(psA[:, grp * 512:512 + grp * 512], onesbf[:, :],
                         ybf[bi][:, grp * 512:(grp + 1) * 512],
                         start=(dq == 0), stop=(dq == NDC - 1))
                mm = T.matmul(psA[:, 1024 + grp * 512:1536 + grp * 512], onesbf[:, :],
                              y2bf[bi][:, grp * 512:(grp + 1) * 512],
                              start=(dq == 0), stop=(dq == NDC - 1))
            v_smm[dq] = s_smm.inc(mm)
        s_smm.wait(SC)
        SC.mul(ln2t[0][:, :], psA[:, 0:1024], 1.0 / D)
        m2 = SC.mul(ln2t[1][:, :], psA[:, 1024:2048], 1.0 / D)
        s_stm = sem("p2_stm")
        s_stm.inc(m2)
        s_stm.wait(V)
        mu2 = V.tensor_tensor(ln2t[2][:, :], ln2t[0][:, :], ln2t[0][:, :], op=AOP.mult)
        s_stm.inc(mu2)
        s_stm.wait(V)
        vv = V.scalar_tensor_tensor(ln2t[1][:, :], ln2t[1][:, :], EPS, ln2t[2][:, :],
                                    op0=AOP.add, op1=AOP.subtract)
        s_stm.inc(vv)
        s_stm.wait(SC)
        SC.activation(ln2t[2][:, :], ln2t[1][:, :], AF.Ln)
        iv = SC.activation(ln2t[1][:, :], ln2t[2][:, :], AF.Exp, scale=-0.5)
        s_stm.inc(iv)
        s_stm.wait(V)
        ng = V.tensor_scalar(ln2t[0][:, :], ln2t[0][:, :], -1.0, None, op0=AOP.mult)
        s_stm.inc(ng)
        s_ap2 = sem("p2_apply")
        v_ap2 = {}
        for dq in range(NDC):
            V.tensor_tensor(ln2t[3][:, :], acc[:, dq * TOK:(dq + 1) * TOK], ln2t[0][:, :], op=AOP.add)
            V.tensor_tensor(ln2t[3][:, :], ln2t[3][:, :], ln2t[1][:, :], op=AOP.mult)
            ap = V.tensor_scalar(acc[:, dq * TOK:(dq + 1) * TOK], ln2t[3][:, :],
                                 gb_sb[:, dq:dq + 1], gb_sb[:, 16 + dq:17 + dq],
                                 op0=AOP.mult, op1=AOP.add)
            v_ap2[dq] = s_ap2.inc(ap)
        v_stats_done = (s_stm.val, s_g.val)
        s_ap2.wait(SC)
    p2s.close()
    if debug:
        sd = sem("dbgf")
        sems["p2_apply"].wait(SY)
        sd.inc(SY.dma_start(dbg["ffn"][:, :], acc[:, :]), 16)
        sd.wait(SY)
        sd.wait(V)
        sd.wait(T)

    if stop_after == "p2":
        p23.close()
        top.close()
        return nc
    p3s = ExitStack()
    x2 = sbN(p3s, "x2", 2, TOK, F32)
    if True:
        s_wo = sem("p3_wo")
        s_womm = sem("p3_womm")
        s_wadd = sem("p3_wadd")
        s_xl = sem("p3_xl")
        s_xt = sem("p3_xt")
        s_xadd = sem("p3_xadd")
        s_ot = sem("p3_ot")
        s_oev = sem("p3_oev")
        s_ost = sem("p3_ost")
        v_womm = {}
        v_wadd = {}
        v_xl = {}
        v_xadd = {}
        v_oev = {}
        v_ost = {}
        sems["p2_stm"].wait(T, v_stats_done[0])
        sems["p2_gate"].wait(T, v_stats_done[1])
        for dq in range(NDC):
            bi = dq % 2
            if dq < 2:
                s_smm.wait(G, v_smm[NDC - 1])
            else:
                s_womm.wait(G, v_womm[dq - 2])
            s_wo.inc(G.dma_start(wos[bi].rearrange("p (c k) -> p c k", c=NDC),
                                 wo_d[dq, :, :, :].rearrange("c p k -> p c k")), 16)
            s_wo.wait(T)
            pd = psC if dq % 2 == 0 else psA[:, 0:1024]
            if dq >= 2:
                s_wadd.wait(T, v_wadd[dq - 2])
            for fc in range(NDC):
                for grp in range(2):
                    mm = T.matmul(pd[:, grp * 512:(grp + 1) * 512],
                                  wos[bi][:, fc * 128:(fc + 1) * 128],
                                  ctx[:, fc * TOK + grp * 512: fc * TOK + (grp + 1) * 512],
                                  start=(fc == 0), stop=(fc == NDC - 1))
            v_womm[dq] = s_womm.inc(mm)
            s_womm.wait(V)
            sems["p2_apply"].wait(V, v_ap2[dq])
            ad = V.tensor_tensor(acc[:, dq * TOK:(dq + 1) * TOK],
                                 acc[:, dq * TOK:(dq + 1) * TOK], pd[:, :], op=AOP.add)
            v_wadd[dq] = s_wadd.inc(ad)
            bi2 = dq % 2
            if dq == 0:
                sems["p2_apply"].wait(SY)
            if dq >= 2:
                s_xadd.wait(SY, v_xadd[dq - 2])
            v_xl[dq] = s_xl.inc(SY.dma_start(x2[bi2][:, 0:TOK],
                                             xt_in[dq * 128:(dq + 1) * 128, :]), 16)
            s_xl.wait(V, v_xl[dq])
            ad2 = V.tensor_tensor(acc[:, dq * TOK:(dq + 1) * TOK],
                                  acc[:, dq * TOK:(dq + 1) * TOK], x2[bi2][:, 0:TOK], op=AOP.add)
            v_xadd[dq] = s_xadd.inc(ad2)
            s_xadd.wait(SY, v_xadd[dq])
            v_ost[dq] = s_ost.inc(
                SY.dma_start(out_d[:, dq * TOK:(dq + 1) * TOK], acc[:, dq * TOK:(dq + 1) * TOK]), 16)
        s_ost.wait(SY)
        s_ost.wait(G)
        s_ost.wait(T)
        s_ost.wait(V)
        s_ost.wait(SC)
    p3s.close()
    p23.close()
    top.close()
    return nc



def _prep_shared(wq, wk, wv, wo, up_w, down_w, ew, gate_w, ln_g, ln_b):
    f32 = np.float32
    perm = np.concatenate([np.arange(0, 128, 2), np.arange(1, 128, 2)])
    wq_p = wq.reshape(H, DK, D)[:, perm, :]
    wk_p = wk.reshape(H, DK, D)[:, perm, :]
    wq_pre = np.ascontiguousarray(wq_p.transpose(0, 2, 1).reshape(H, NDC, 128, 128), dtype=f32)
    wk_pre = np.ascontiguousarray(wk_p.transpose(0, 2, 1).reshape(H, NDC, 128, 128), dtype=f32)
    wv_pre = np.ascontiguousarray(
        wv.T.reshape(NDC, 128, NPASS, HPP * 128).transpose(2, 0, 1, 3), dtype=f32)
    wo_pre = np.ascontiguousarray(
        wo.T.reshape(NDC, 128, NDC, 128).transpose(2, 0, 1, 3), dtype=f32)
    up_pre = np.ascontiguousarray(
        up_w.reshape(NDC, 128, NIC, 128).transpose(2, 0, 1, 3), dtype=f32)
    dn_pre = np.ascontiguousarray(
        down_w.reshape(NSUP, ICPS, 128, NDC, 128).transpose(0, 3, 1, 2, 4), dtype=f32)
    e_pre = np.ascontiguousarray(
        ew.reshape(NDC, 128, NDC, 128).transpose(2, 0, 1, 3), dtype=f32)
    gate_pre = np.ascontiguousarray(
        gate_w.reshape(NDC, 128, NEXP).transpose(1, 0, 2).reshape(128, 128), dtype=f32)
    gb_pre = np.ascontiguousarray(
        np.concatenate([ln_g.reshape(NDC, 128).T, ln_b.reshape(NDC, 128).T], axis=1), dtype=f32)
    return dict(wq=wq_pre, wk=wk_pre, wv=wv_pre, wo=wo_pre, up=up_pre, dn=dn_pre,
                ew=e_pre, gate=gate_pre, gb=gb_pre,
                eye=np.eye(128, dtype=f32), ones1=np.ones((1, 128), dtype=f32),
                sel=np.ascontiguousarray(np.repeat(np.eye(8, dtype=f32), 128, axis=1)))


def _prep_core(x_b, par):
    f32 = np.float32
    xb = np.asarray(x_b).reshape(16, 128, D)
    order = []
    for j in range(8):
        order += [2 * j + par, 2 * j + (1 - par)]
    x_sw = np.ascontiguousarray(xb[order].reshape(S, D), dtype=f32)
    x_t = np.ascontiguousarray(x_sw.reshape(16, 128, D)[0::2].reshape(TOK, D).T, dtype=f32)

    inv = THETA ** (-(np.arange(0, DK, 2, dtype=np.float64) / DK))
    inv2 = np.concatenate([inv, inv])[:, None]
    own_pos = np.concatenate([np.arange(128) + (2 * j + par) * 128 for j in range(8)])
    oth_pos = np.concatenate([np.arange(128) + (2 * j + (1 - par)) * 128 for j in range(8)])
    sig_pos = np.concatenate([own_pos, oth_pos]).astype(np.float64)
    ang_k = sig_pos[None, :] * inv2
    ckt = np.cos(ang_k).astype(f32)
    skt = np.sin(ang_k).astype(f32)
    skt[0:64] *= -1.0
    ang_q = own_pos.astype(np.float64)[None, :] * inv2
    scale = 1.0 / np.sqrt(DK)
    cqt = (np.cos(ang_q) * scale).astype(f32)
    sqt = (np.sin(ang_q) * scale).astype(f32)
    sqt[0:64] *= -1.0
    tri = np.where(np.arange(128)[None, :] > np.arange(128)[:, None], NEG, 0.0).astype(f32)
    ptile = np.full((128, 128), NEG if par == 0 else 0.0, dtype=f32)
    maskp = np.ascontiguousarray(np.concatenate([tri, ptile], axis=1))
    return dict(x=x_sw, xT=x_t, ckt=ckt, skt=skt, cqt=cqt, sqt=sqt, maskp=maskp)


_NC_CACHE = {}


def kernel(L0_attn_x, L0_attn_wq, L0_attn_wk, L0_attn_wv, L0_attn_wo,
           L0_ffn_gate_w, L0_ffn_up_w, L0_ffn_down_w, L0_ffn_expert_weight,
           L0_ffn_ln_gamma, L0_ffn_ln_beta):
    f32 = np.float32
    x = np.asarray(L0_attn_x, dtype=f32)
    shared = _prep_shared(np.asarray(L0_attn_wq, f32), np.asarray(L0_attn_wk, f32),
                          np.asarray(L0_attn_wv, f32), np.asarray(L0_attn_wo, f32),
                          np.asarray(L0_ffn_up_w, f32), np.asarray(L0_ffn_down_w, f32),
                          np.asarray(L0_ffn_expert_weight, f32),
                          np.asarray(L0_ffn_gate_w, f32),
                          np.asarray(L0_ffn_ln_gamma, f32), np.asarray(L0_ffn_ln_beta, f32))
    if "nc" not in _NC_CACHE:
        _NC_CACHE["nc"] = build_program(debug=False)
    nc = _NC_CACHE["nc"]
    in_maps = []
    for core in range(NCORE):
        b, par = core // 2, core % 2
        m = dict(shared)
        m.update(_prep_core(x[b], par))
        in_maps.append(m)
    res = run_bass_kernel_spmd(nc, in_maps, core_ids=list(range(NCORE)))
    out = np.empty((B, S, D), dtype=f32)
    for core in range(NCORE):
        b, par = core // 2, core % 2
        o = np.asarray(res.results[core]["out"]).reshape(128, NDC, TOK).transpose(2, 1, 0)
        o = o.reshape(NSLOT, 128, D)
        for j in range(NSLOT):
            out[b, (2 * j + par) * 128:(2 * j + par + 1) * 128, :] = o[j]
    return out
